# version 12
# baseline (speedup 1.0000x reference)
import numpy as np
import ml_dtypes
import concourse.bass as bass
import concourse.mybir as mybir
from concourse.alu_op_type import AluOpType as OP
from concourse.bass_utils import run_bass_kernel_spmd
from concourse.bass import IndirectOffsetOnAxis
from concourse.replica_groups import maybe_share_collective_output_space

F32 = mybir.dt.float32
BF16 = mybir.dt.bfloat16
U32 = mybir.dt.uint32
AF = mybir.ActivationFunctionType
AX = mybir.AxisListType.X
BF = ml_dtypes.bfloat16

NCORE = 8
T = 2048
S = 4096
E = 1024
ECH = 8
H = 16
DH = 64
QCW = 512
NQC = T // QCW
NKT = S // 128
CE = E ** -0.5
CL0 = CE / S
BIGC = 65536.0
CAP = 1024
NEXP = 64
NEXP_MY = 8
TOPK = 8
EPS = 1e-5

SEM_ENG = {"d": "sync", "d2": "scalar", "g": "gpsimd", "g2": "gpsimd",
           "v": "vector", "s": "scalar", "t": "tensor", "q": "gpsimd",
           "c": "gpsimd"}

FA_STREAM = 0
FA_SQ = 4096
FA_ROWS = 5120
FA_IQ = 6144
FA_IK2 = 8192
FA_PEV = 0
FA_REC = 512
FA_G = 0
FA_X = 4096
FA_SS = 8192
FA_EQ = 0
FA_CAND = 1024
FA_T1 = 2048
FA_CNT8 = 3072
FA_SC = 6144
FA_SFC = 7168
FA_MW = 8192
FA_SEL = 9216
FA_PRE = 10240
FA_SL = 11264
SL_EI = 0
SL_POS = 128
SL_W = 256
SL_GS = 384
SL_H4 = 512
SL_R1 = 512
SL_R2 = 576
SL_R3 = 640
SL_RB = 768
FA_ROWB = 4096
FA_ACC = 0
FA_GT = 1024
FA_XC = 2048
FA_SC2 = 3072
FA_XO = 4096
FA_W = 12288

KP_A = 0
KP_WF = 2048
KP_TOK = 2176
KP_GAM = 2192
KP_BET = 2200
KP_B1 = 2208
KP_B2 = 2240
KP_RANK = 2248
KP_SGN = 2249
KP_EPS = 2252
KP_CNTR = 2256
KP_OFFS = 2320
KP_I64B = 2384
KP_I8B = 3408
KP_BIAS = 3536
KP_ONEST = 4560
KP_UT = 4688
KP_IDF = 4816
KP_ONE = 4944
KP_ONE1 = 4945
KP_GWT = 5120
KP_MAPF = 5680
KP_MAPF = 5680 if 5680 + 128 <= 6144 else 0
KP_W = 6144

KPB_SIGNS = 0
KPB_ID = 2048
KPB_W = 3072

KPU_PG = 0
KPU_MAPIO = 8
KPU_SCAT = 16
KPU_YIDX = 144
KPU_MAPU = 272
KPU_W = 384

B0_KA = 0
B0_QA = 8192
B0_V = 12288
B0_EXP = 20608
B0_W = 23680
BF_U1 = 0
BF_V2 = 2048
BF_V1H = 4096
BF_U2H = 12288
BF_P = 20480
BF_HH = 21504
BG_SEL = 0
BG_TOK = 8192
B1_K = 0
B1_V = 8192
B1_Q = 16512
B1_EXP = 18560
B1_WKV = 21632
B1_WQ = 25728
B1_AST = 27776
B1_Q2 = 28288
BW_A = 0
BW_WP = 4096
BM_BUFT = 0
BM_GT = 8192
BM_M = 10240
BM_MT = 11264
BM_WG = 15360
BM_WU = 19456
BM_WD = 23552
BM_Y = 27648
BM_SIL = 28672
BY_Y = 0
BB_W = 32768

BA_W = ECH * S


class Ctx:
    def __init__(self, nc, sems, eng_name, eng):
        self.nc = nc
        self.sems = sems
        self.eng_name = eng_name
        self.eng = eng
        self.cnt = {k: 0 for k in SEM_ENG}
        self.barv = 0

    def _emit(self, sem, fn, n):
        if self.eng_name == SEM_ENG[sem]:
            inst = fn(self.eng)
            if n:
                inst.then_inc(self.sems[sem], n)
        if n:
            self.cnt[sem] += n
        return self.cnt[sem]

    def dma(self, q, fn):
        return self._emit(q, fn, 16)

    def cop(self, e, fn, inc=True):
        return self._emit(e, fn, 1 if inc else 0)

    def coll(self, fn):
        return self._emit("c", fn, 1)

    def W(self, waiter_eng, sem, val=None):
        if val is None:
            val = self.cnt[sem]
        if val <= 0:
            return
        if self.eng_name == waiter_eng:
            self.eng.wait_ge(self.sems[sem], val)

    def bar(self):
        self.barv += 5
        if self.eng is not None:
            self.eng.sem_inc(self.sems["b"], 1)
            self.eng.wait_ge(self.sems["b"], self.barv)

    def sync_all(self):
        for en in ("sync", "vector", "scalar", "tensor", "gpsimd"):
            for sname in ("d", "g", "g2", "d2", "v", "s", "t", "q", "c"):
                self.W(en, sname)
        self.bar()

    def mark(self, sem):
        return self.cnt[sem]


DBG_SPECS = {
    "xn0": ((E, S), "f32"),
    "s0": ((E, T), "f32"),
    "x1": ((E, T), "f32"),
    "gg": ((E, T), "f32"),
    "rows": ((2, 2048), "f32"),
    "logits": ((128, 1024), "f32"),
    "gate": ((128, 4096), "f32"),
    "slots": ((128, 640), "f32"),
    "map": ((NEXP * CAP, 1), "f32"),
    "cnt": ((8, 64), "f32"),
    "xnf1": ((E, T), "f32"),
    "a1": ((E, T), "f32"),
    "s1": ((E, T), "f32"),
    "y": ((NEXP * CAP, E), "f32"),
    "xn1": ((E, S), "f32"),
}




def ph_ln_stats_qc(c, tn, src_view, qc):
    FA = tn["FA"]; KP = tn["KP"]
    psD = tn["psD"]; psE = tn["psE"]
    ones = KP[:, KP_ONE:KP_ONE + 1]
    src = src_view[:, :, qc * QCW:(qc + 1) * QCW]
    c.W("sync", "v")
    c.W("sync", "s")
    c.W("sync", "t")
    dm = c.dma("d", lambda e: e.dma_start(
        out=FA[:, FA_STREAM:FA_STREAM + 8 * QCW].rearrange("p (a t) -> p a t", a=8), in_=src))
    c.W("scalar", "d", dm)
    c.W("tensor", "d", dm)
    smm = [0] * ECH
    for ec in range(ECH):
        xc = FA[:, FA_STREAM + ec * QCW: FA_STREAM + (ec + 1) * QCW]
        sq = FA[:, FA_SQ + (ec % 2) * QCW: FA_SQ + (ec % 2 + 1) * QCW]
        if ec >= 2:
            c.W("scalar", "t", smm[ec - 2])
        sqm = c.cop("s", lambda e, xc=xc, sq=sq: e.activation(sq, xc, AF.Square))
        c.cop("t", lambda e, xc=xc, ec=ec: e.matmul(
            psD[0:1, 0:QCW], ones, xc, start=(ec == 0), stop=(ec == ECH - 1)), inc=False)
        c.W("tensor", "s", sqm)
        smm[ec] = c.cop("t", lambda e, sq=sq, ec=ec: e.matmul(
            psE[0:1, 0:QCW], ones, sq, start=(ec == 0), stop=(ec == ECH - 1)))


def ph_ln_rows(c, tn, qc):
    KP = tn["KP"]; FA = tn["FA"]; psD = tn["psD"]; psE = tn["psE"]
    arow = KP[0:1, KP_A + qc * QCW: KP_A + (qc + 1) * QCW]
    brow = KP[32:33, KP_A + qc * QCW: KP_A + (qc + 1) * QCW]
    mrow = FA[0:1, FA_ROWS: FA_ROWS + QCW]
    m2row = FA[0:1, FA_ROWS + QCW: FA_ROWS + 2 * QCW]
    c.W("vector", "t")
    v = c.cop("v", lambda e: e.tensor_scalar(mrow, psD[0:1, 0:QCW], 1.0 / E, None, OP.mult))
    c.W("vector", "v", v)
    v = c.cop("v", lambda e: e.tensor_tensor(m2row, mrow, mrow, op=OP.mult))
    c.W("vector", "v", v)
    v = c.cop("v", lambda e: e.tensor_scalar(m2row, m2row, -1.0, None, OP.mult))
    c.W("vector", "v", v)
    v = c.cop("v", lambda e: e.scalar_tensor_tensor(m2row, psE[0:1, 0:QCW], 1.0 / E, m2row, OP.mult, OP.add))
    c.W("scalar", "v", v)
    s = c.cop("s", lambda e: e.activation(m2row, m2row, AF.Ln, bias=KP[0:1, KP_EPS:KP_EPS + 1]))
    c.W("scalar", "s", s)
    s = c.cop("s", lambda e: e.activation(arow, m2row, AF.Exp, scale=-0.5))
    c.W("vector", "s", s)
    v = c.cop("v", lambda e: e.scalar_tensor_tensor(brow, mrow, -1.0, arow, OP.mult, OP.mult))
    c.W("vector", "v", v)
    return arow, brow


def ph_bcast_rows(c, tn, arow, brow, psX, psY):
    KP = tn["KP"]
    ones1 = KP[0:1, KP_ONE1:KP_ONE1 + 128]
    ones1b = KP[32:33, KP_ONE1:KP_ONE1 + 128]
    c.W("tensor", "v")
    c.W("tensor", "s")
    c.cop("t", lambda e: e.matmul(psX, ones1, arow, start=True, stop=True), inc=False)
    return c.cop("t", lambda e: e.matmul(psY, ones1b, brow, start=True, stop=True))


def ph_exchange(c, tn):
    BA = tn["BA"]; KPU = tn["KPU"]
    d_ag_in = tn["d_ag_in"]; d_ag_out = tn["d_ag_out"]
    dm = c.dma("d", lambda e: e.dma_start(
        out=d_ag_in.ap().rearrange("(a p) t -> p a t", p=128),
        in_=BA[:, 0:BA_W].rearrange("p (a k) -> p a k", a=8)[:, :, 0:T]))
    c.W("gpsimd", "d", dm)
    cm = c.coll(lambda e: e.collective_compute(
        "AllGather", mybir.AluOpType.bypass, replica_groups=tn["pairs"],
        ins=[d_ag_in.ap()], outs=[d_ag_out.ap()]))
    c.W("gpsimd", "c", cm)
    for ec in range(ECH):
        c.dma("g", lambda e, ec=ec: e.indirect_dma_start(
            out=BA[:, ec * S + T: ec * S + 2 * T], out_offset=None,
            in_=d_ag_out.ap(),
            in_offset=IndirectOffsetOnAxis(ap=KPU[:, KPU_PG + ec:KPU_PG + ec + 1], axis=0)))
    c.sync_all()


def ph_attn_core(c, tn, qc, lhsT_fn, rhs_fn, vslot_fn, scale_fn, straddle_fn,
                 s_dst, stage_bf=None):
    BB = tn["BB"]; FA = tn["FA"]; KP = tn["KP"]; KPB = tn["KPB"]
    psA = tn["psA"]; psB = tn["psB"]; psD = tn["psD"]; psE = tn["psE"]
    NG = 16
    ROT = 2
    psmap = [psA, psB]
    expslot = tn["exp_base"]
    expm = [0] * NG
    avm = [0] * NG
    eslots = [BB[:, expslot + k * 1024: expslot + (k + 1) * 1024] for k in range(ROT)]
    rhs = rhs_fn(qc)

    def emit_av(g):
        eslot = eslots[g % ROT]
        c.W("tensor", "s", expm[g])
        for i in range(2):
            kt = g * 2 + i
            vsl = vslot_fn(kt)
            c.cop("t", lambda e, vsl=vsl, eslot=eslot, i=i, g=g: e.matmul(
                psD[0:65, 0:512], vsl, eslot[:, i * 512:(i + 1) * 512],
                start=(g == 0 and i == 0), stop=(g == NG - 1 and i == 1)), inc=(i == 1))
        avm[g] = c.mark("t")

    for g in range(NG):
        ps = psmap[g % ROT]
        if g >= ROT:
            c.W("tensor", "s", expm[g - ROT])
        for i in range(2):
            kt = g * 2 + i
            lt = lhsT_fn(kt)
            c.cop("t", lambda e, ps=ps, i=i, lt=lt, rhs=rhs: e.matmul(
                ps[:, i * 512:(i + 1) * 512], lt, rhs,
                start=True, stop=True), inc=(i == 1))
        scm = c.mark("t")
        scale = scale_fn(g)
        sub = straddle_fn(g) if straddle_fn else None
        if sub is not None:
            c.W("vector", "t", scm)
            vs = c.cop("v", lambda e, ps=ps, sub=sub: e.tensor_tensor(
                ps[:, 0:1024], ps[:, 0:1024],
                KPB[:, KPB_SIGNS + sub * 1024:KPB_SIGNS + (sub + 1) * 1024], op=OP.mult))
            c.W("scalar", "v", vs)
            scale = 1.0
        c.W("scalar", "t", scm)
        if g >= ROT:
            c.W("scalar", "t", avm[g - ROT])
        eslot = eslots[g % ROT]
        expm[g] = c.cop("s", lambda e, ps=ps, eslot=eslot, scale=scale: e.activation(
            eslot, ps[:, 0:1024], AF.Exp, scale=scale))
        if g >= 1:
            emit_av(g - 1)
    emit_av(NG - 1)
    c.W("vector", "t", avm[NG - 1])
    c.W("vector", "d")
    pv = c.cop("v", lambda e: e.tensor_copy(FA[0:64, FA_PEV:FA_PEV + 512], psD[0:64, 0:512]))
    rv = c.cop("v", lambda e: e.reciprocal(FA[0:1, FA_REC:FA_REC + 512], psD[64:65, 0:512]))
    c.W("tensor", "v", rv)
    bm = c.cop("t", lambda e: e.matmul(
        psE[0:64, 0:512], KP[0:1, KP_ONE1:KP_ONE1 + 64],
        FA[0:1, FA_REC:FA_REC + 512], start=True, stop=True))
    c.W("vector", "t", bm)
    src_out = stage_bf if stage_bf is not None else FA[0:64, FA_PEV:FA_PEV + 512]
    sv = c.cop("v", lambda e, src_out=src_out: e.tensor_tensor(
        src_out, FA[0:64, FA_PEV:FA_PEV + 512], psE[0:64, 0:512], op=OP.mult))
    c.W("sync", "v", sv)
    dm = c.dma("d", lambda e, src_out=src_out: e.dma_start(out=s_dst, in_=src_out))
    c.W("tensor", "v", sv)
    c.W("sync", "d", dm)




def ph_attn_core_packed(c, tn, qc, ga, gb, a_dst_a, a_dst_b):
    BB = tn["BB"]; FA = tn["FA"]; KP = tn["KP"]
    psA = tn["psA"]; psB = tn["psB"]; psCx = tn["psC"]; psD = tn["psD"]; psE = tn["psE"]
    NG = 32
    ROT = 3
    psmap = [psA, psB, psCx]
    expslot = tn["exp_base"]
    expm = [0] * NG
    avm = [0] * NG
    eslots = [BB[:, expslot + k * 1024: expslot + (k + 1) * 1024] for k in range(ROT)]
    rhs_a = BB[0:64, B1_Q + qc * 512:B1_Q + (qc + 1) * 512]
    rhs_b = BB[64:128, B1_Q2 + qc * 512:B1_Q2 + (qc + 1) * 512]

    def lhsT_k(kt, g4):
        base = B1_K + (g4 // 2) * 4096 + kt * 128
        return BB[(g4 % 2) * 64:(g4 % 2) * 64 + 64, base:base + 128]

    def vsl(kt, g4):
        return BB[:, B1_V + kt * 260 + g4 * 65: B1_V + kt * 260 + g4 * 65 + 65]

    def emit_av(g):
        eslot = eslots[g % ROT]
        c.W("tensor", "s", expm[g])
        c.cop("t", lambda e, g=g, eslot=eslot: e.matmul(
            psD[0:65, 0:512], vsl(g, ga), eslot[:, 0:512],
            start=(g == 0), stop=(g == NG - 1)), inc=False)
        c.cop("t", lambda e, g=g, eslot=eslot: e.matmul(
            psE[0:65, 0:512], vsl(g, gb), eslot[:, 512:1024],
            start=(g == 0), stop=(g == NG - 1)), inc=True)
        avm[g] = c.mark("t")

    for g in range(NG):
        ps = psmap[g % ROT]
        if g >= ROT:
            c.W("tensor", "s", expm[g - ROT])
        c.cop("t", lambda e, g=g, ps=ps: e.matmul(
            ps[:, 0:512], lhsT_k(g, ga), rhs_a, start=True, stop=True), inc=False)
        c.cop("t", lambda e, g=g, ps=ps: e.matmul(
            ps[:, 512:1024], lhsT_k(g, gb), rhs_b, start=True, stop=True), inc=True)
        scm = c.mark("t")
        c.W("scalar", "t", scm)
        if g >= ROT:
            c.W("scalar", "t", avm[g - ROT])
        eslot = eslots[g % ROT]
        expm[g] = c.cop("s", lambda e, ps=ps, eslot=eslot: e.activation(
            eslot, ps[:, 0:1024], AF.Exp, scale=0.125))
        if g >= 1:
            emit_av(g - 1)
    emit_av(NG - 1)
    c.W("vector", "t", avm[NG - 1])
    c.W("vector", "d")
    for (psP, dst, pev, rec, bcol, prow) in (
            (psD, a_dst_a, 0, 512, 0, 0),
            (psE, a_dst_b, 1024, 1536, 512, 64)):
        pv = c.cop("v", lambda e, psP=psP, pev=pev: e.tensor_copy(
            FA[0:64, pev:pev + 512], psP[0:64, 0:512]))
        rv = c.cop("v", lambda e, psP=psP, rec=rec: e.reciprocal(
            FA[0:1, rec:rec + 512], psP[64:65, 0:512]))
        c.W("tensor", "v", rv)
        bm = c.cop("t", lambda e, rec=rec, bcol=bcol: e.matmul(
            psA[0:64, bcol:bcol + 512], KP[0:1, KP_ONE1:KP_ONE1 + 64],
            FA[0:1, rec:rec + 512], start=True, stop=True))
        c.W("vector", "t", bm)
        stage = BB[prow:prow + 64, B1_AST:B1_AST + 512]
        sv = c.cop("v", lambda e, stage=stage, pev=pev, bcol=bcol: e.tensor_tensor(
            stage, FA[0:64, pev:pev + 512], psA[0:64, bcol:bcol + 512], op=OP.mult))
        c.W("sync", "v", sv)
        dm = c.dma("d", lambda e, dst=dst, stage=stage: e.dma_start(out=dst, in_=stage))
    c.W("tensor", "v")
    c.W("sync", "d")



def ph_l0(c, tn, dbg):
    I = tn["I"]; BA = tn["BA"]; BB = tn["BB"]; FA = tn["FA"]; KP = tn["KP"]
    psA = tn["psA"]; psB = tn["psB"]; psC = tn["psC"]; psD = tn["psD"]; psE = tn["psE"]
    x_view = I["x_T"].rearrange("(a p) t -> p a t", p=128)

    for qc in range(NQC):
        ph_ln_stats_qc(c, tn, x_view, qc)
        arow, brow = ph_ln_rows(c, tn, qc)
        bm = ph_bcast_rows(c, tn, arow, brow, psA[:, 0:QCW], psB[:, 0:QCW])
        c.W("vector", "t", bm)
        c.W("vector", "s")
        for ec in range(ECH):
            xc = FA[:, FA_STREAM + ec * QCW: FA_STREAM + (ec + 1) * QCW]
            tmp = FA[:, FA_SQ: FA_SQ + QCW]
            dst = BA[:, ec * S + qc * QCW: ec * S + (qc + 1) * QCW]
            v = c.cop("v", lambda e, xc=xc, tmp=tmp: e.tensor_tensor(tmp, xc, psA[:, 0:QCW], op=OP.mult))
            c.W("vector", "v", v)
            v = c.cop("v", lambda e, tmp=tmp, dst=dst: e.tensor_tensor(dst, tmp, psB[:, 0:QCW], op=OP.add))
            c.W("vector", "v", v)
    c.sync_all()

    ph_exchange(c, tn)
    if "xn0" in dbg:
        for ec in range(ECH):
            c.dma("g", lambda e, ec=ec: e.dma_start(
                out=tn["DBG"]["xn0"][ec * 128:(ec + 1) * 128, :], in_=BA[:, ec * S:(ec + 1) * S]))
        c.sync_all()

    xn_ch = [BA[:, ec * S:(ec + 1) * S] for ec in range(ECH)]
    iq = FA[:, FA_IQ:FA_IQ + T]
    ik2 = FA[:, FA_IK2:FA_IK2 + S]
    sgncol = KP[:, KP_SGN:KP_SGN + 1]
    tn["exp_base"] = B0_EXP
    WSL = B0_W
    wdm = {}
    pjm = {}
    augm = {}
    vprm = {}
    attnm = {}
    qmm = {}

    def emit_wdma(h):
        if h >= H:
            return
        par = h % 2
        gq = "g" if par == 0 else "g2"
        wqd = BB[:, WSL + par * 2048: WSL + par * 2048 + 1024]
        wkd = BB[:, WSL + par * 2048 + 1024: WSL + par * 2048 + 2048]
        c.W("gpsimd", "t", pjm.get(h - 2, 0))
        srcq = I["WqTd"].rearrange("(a p) o -> p a o", p=128)[:, :, h * 128:(h + 1) * 128]
        srck = I["WkTd"].rearrange("(a p) o -> p a o", p=128)[:, :, h * 128:(h + 1) * 128]
        c.dma(gq, lambda e, srcq=srcq, wqd=wqd: e.dma_start(
            out=wqd.rearrange("p (a o) -> p a o", a=8), in_=srcq))
        c.dma(gq, lambda e, srck=srck, wkd=wkd: e.dma_start(
            out=wkd.rearrange("p (a o) -> p a o", a=8), in_=srck))
        if par == 0:
            hp = h // 2
            wv = BB[:, WSL + 4096 + (hp % 2) * 1024: WSL + 4096 + (hp % 2) * 1024 + 1024]
            srcv = I["WvT0"].rearrange("(a p) o -> p a o", p=128)[:, :, hp * 128:(hp + 1) * 128]
            c.dma(gq, lambda e, srcv=srcv, wv=wv: e.dma_start(
                out=wv.rearrange("p (a o) -> p a o", a=8), in_=srcv))
        wdm[h] = c.mark(gq)

    def emit_proj(h):
        if h >= H:
            return
        par = h % 2
        hp = h // 2
        wqd = BB[:, WSL + par * 2048: WSL + par * 2048 + 1024]
        wkd = BB[:, WSL + par * 2048 + 1024: WSL + par * 2048 + 2048]
        ka = BB[:, B0_KA + par * 4096: B0_KA + par * 4096 + S]
        qa = BB[:, B0_QA + par * 2048: B0_QA + par * 2048 + T]
        c.W("tensor", "g" if h % 2 == 0 else "g2", wdm[h])
        c.W("tensor", "v", augm.get(h - 2, 0))
        c.W("scalar", "t", attnm.get(h - 2, 0))
        c.W("tensor", "s", qmm.get(h - 1, 0))
        if (h - 1) >= 0 and (h - 1) % 2 == 0:
            c.W("tensor", "v", vprm.get((h - 1) // 2, 0))
        evm = [0] * 12
        for kc in range(8):
            psk = psC[:, (kc % 2) * 512:(kc % 2 + 1) * 512]
            if kc >= 2:
                c.W("tensor", "s", evm[kc - 2])
            for ei in range(ECH):
                c.cop("t", lambda e, kc=kc, ei=ei, wkd=wkd, psk=psk: e.matmul(
                    psk, wkd[:, ei * 128:(ei + 1) * 128],
                    xn_ch[ei][:, kc * 512:(kc + 1) * 512],
                    start=(ei == 0), stop=(ei == ECH - 1)), inc=(ei == ECH - 1))
            tmk = c.mark("t")
            c.W("scalar", "t", tmk)
            evm[kc] = c.cop("s", lambda e, kc=kc, ka=ka, psk=psk: e.copy(
                ka[:, kc * 512:(kc + 1) * 512], psk))
        for qcc in range(NQC):
            psk = psC[:, (qcc % 2) * 512:(qcc % 2 + 1) * 512]
            c.W("tensor", "s", evm[8 + qcc - 2] if qcc >= 2 else evm[6 + qcc])
            for ei in range(ECH):
                c.cop("t", lambda e, qcc=qcc, ei=ei, wqd=wqd, psk=psk: e.matmul(
                    psk, wqd[:, ei * 128:(ei + 1) * 128],
                    xn_ch[ei][:, qcc * 512:(qcc + 1) * 512],
                    start=(ei == 0), stop=(ei == ECH - 1)), inc=(ei == ECH - 1))
            tmk = c.mark("t")
            c.W("scalar", "t", tmk)
            evm[8 + qcc] = c.cop("s", lambda e, qcc=qcc, qa=qa, psk=psk: e.copy(
                qa[:, qcc * 512:(qcc + 1) * 512], psk))
        if par == 0:
            wv = BB[:, WSL + 4096 + (hp % 2) * 1024: WSL + 4096 + (hp % 2) * 1024 + 1024]
            vbase = B0_V + (hp % 2) * 4160
            c.W("gpsimd", "t", attnm.get(2 * (hp - 2) + 1, 0))
            qm = c.cop("q", lambda e, vbase=vbase: e.memset(BB[:, vbase:vbase + 32 * 130], 1.0))
            c.W("tensor", "q", qm)
            evv = [0] * NKT
            for kt in range(NKT):
                psk = psC[:, (kt % 2) * 512:(kt % 2) * 512 + 128]
                if kt >= 2:
                    c.W("tensor", "v", evv[kt - 2])
                else:
                    c.W("tensor", "s", evm[10 + kt])
                for ei in range(ECH):
                    c.cop("t", lambda e, kt=kt, ei=ei, wv=wv, psk=psk: e.matmul(
                        psk, xn_ch[ei][:, kt * 128:(kt + 1) * 128],
                        wv[:, ei * 128:(ei + 1) * 128],
                        start=(ei == 0), stop=(ei == ECH - 1)), inc=(ei == ECH - 1))
                tmk = c.mark("t")
                c.W("vector", "t", tmk)
                c.W("vector", "q", qm)
                c.cop("v", lambda e, kt=kt, psk=psk, vbase=vbase: e.tensor_copy(
                    BB[:, vbase + kt * 130:vbase + kt * 130 + 64], psk[0:128, 0:64]))
                evv[kt] = c.cop("v", lambda e, kt=kt, psk=psk, vbase=vbase: e.tensor_copy(
                    BB[:, vbase + kt * 130 + 65:vbase + kt * 130 + 129], psk[0:128, 64:128]))
            vprm[hp] = c.mark("v")
        pjm[h] = c.mark("t")
        qmm[h] = c.mark("s")

    def emit_aug(h):
        if h >= H:
            return
        par = h % 2
        ka = BB[:, B0_KA + par * 4096: B0_KA + par * 4096 + S]
        qa = BB[:, B0_QA + par * 2048: B0_QA + par * 2048 + T]
        c.W("vector", "s", qmm[h])
        v = c.cop("v", lambda e, ka=ka: e.tensor_scalar(
            ka[0:64, T:S], ka[0:64, T:S], sgncol[0:64, :], None, OP.mult))
        c.W("vector", "v", v)
        v = c.cop("v", lambda e, ka=ka: e.tensor_tensor(
            ka[64:128, 0:S], ka[64:128, 0:S], ik2[64:128, :], op=OP.mult))
        c.W("vector", "v", v)
        v = c.cop("v", lambda e, qa=qa: e.tensor_tensor(
            qa[0:64, 0:T], qa[0:64, 0:T], iq[0:64, :], op=OP.mult))
        c.W("vector", "v", v)
        v = c.cop("v", lambda e, qa=qa: e.tensor_scalar(
            qa[64:128, 0:T], qa[64:128, 0:T], -1.0, None, OP.mult))
        augm[h] = c.mark("v")

    def emit_attn(h):
        par = h % 2
        hp = h // 2
        ka = BB[:, B0_KA + par * 4096: B0_KA + par * 4096 + S]
        qa = BB[:, B0_QA + par * 2048: B0_QA + par * 2048 + T]
        vbase = B0_V + (hp % 2) * 4160
        c.W("tensor", "v", augm[h])
        c.W("tensor", "v", vprm[hp])
        c.W("scalar", "v", augm[h])

        def vsl0(kt, h=h, vbase=vbase):
            return BB[:, vbase + kt * 130 + (h % 2) * 65: vbase + kt * 130 + (h % 2) * 65 + 65]

        def lhsT0(kt, ka=ka):
            return ka[:, kt * 128:(kt + 1) * 128]

        def rhs0(qcc, qa=qa):
            return qa[:, qcc * 512:(qcc + 1) * 512]

        for qcc in range(NQC):
            def scale0(g, qcc=qcc):
                if g >= 8:
                    return 1.0
                return 1.0 if g < qcc * 2 else -1.0

            def straddle0(g, qcc=qcc):
                if g == qcc * 2:
                    return 0
                if g == qcc * 2 + 1:
                    return 1
                return None

            s_dst = tn["d_s"].ap()[h * 64:(h + 1) * 64, qcc * 512:(qcc + 1) * 512]
            ph_attn_core(c, tn, qcc, lhsT0, rhs0, vsl0, scale0, straddle0, s_dst)
        attnm[h] = c.mark("t")

    emit_wdma(0)
    emit_wdma(1)
    emit_proj(0)
    emit_aug(0)
    for h in range(H):
        emit_wdma(h + 2)
        emit_proj(h + 1)
        emit_aug(h + 1)
        emit_attn(h)
    c.sync_all()
    if "s0" in dbg:
        c.dma("d", lambda e: e.dma_start(out=tn["DBG"]["s0"], in_=tn["d_s"].ap()))
        c.sync_all()

    s_view = tn["d_s"].ap().rearrange("(a p) t -> p a t", p=128)
    for qc in range(NQC):
        ph_ln_stats_qc(c, tn, s_view, qc)
        ph_ln_rows(c, tn, qc)
    c.sync_all()

    if "rows" in dbg:
        c.dma("d", lambda e: e.dma_start(out=tn["DBG"]["rows"][0:1, :], in_=KP[0:1, KP_A:KP_A + 2048]))
        c.dma("d", lambda e: e.dma_start(out=tn["DBG"]["rows"][1:2, :], in_=KP[32:33, KP_A:KP_A + 2048]))
        c.sync_all()

    src = I["U1"].rearrange("(a p) o -> p a o", p=128)
    c.dma("g", lambda e, src=src: e.dma_start(
        out=BB[:, BF_U1:BF_U1 + 2048].rearrange("p (a o) -> p a o", a=8), in_=src))
    src = I["V2"].rearrange("(a p) o -> p a o", p=128)
    c.dma("g", lambda e, src=src: e.dma_start(
        out=BB[:, BF_V2:BF_V2 + 2048].rearrange("p (a o) -> p a o", a=2), in_=src))
    for half in range(2):
        src = I["V1"].rearrange("(a p) o -> p a o", p=128)[:, :, half * 2048:(half + 1) * 2048]
        c.dma("g", lambda e, src=src, half=half: e.dma_start(
            out=BB[:, BF_V1H + half * 4096:BF_V1H + (half + 1) * 4096].rearrange(
                "p (a o) -> p a o", a=2), in_=src))
        src = I["U2"].rearrange("(a p) o -> p a o", p=128)[:, half * 16:(half + 1) * 16, :]
        c.dma("g", lambda e, src=src, half=half: e.dma_start(
            out=BB[:, BF_U2H + half * 4096:BF_U2H + (half + 1) * 4096].rearrange(
                "p (a o) -> p a o", a=16), in_=src))
    gm_w = c.mark("g")
    c.W("tensor", "g", gm_w)
    c.W("scalar", "g", gm_w)

    x1_view = tn["d_x1"].ap().rearrange("(a p) t -> p a t", p=128)
    for qc in range(NQC):
        dxm = c.dma("d", lambda e, qc=qc: e.dma_start(
            out=FA[:, FA_X:FA_X + 8 * QCW].rearrange("p (a t) -> p a t", a=8),
            in_=x_view[:, :, qc * QCW:(qc + 1) * QCW]))
        dsm = c.dma("d", lambda e, qc=qc: e.dma_start(
            out=FA[:, FA_SS:FA_SS + 8 * QCW].rearrange("p (a t) -> p a t", a=8),
            in_=s_view[:, :, qc * QCW:(qc + 1) * QCW]))
        c.W("tensor", "s")
        c.W("tensor", "v")
        for pc in range(2):
            for ei in range(ECH):
                c.cop("t", lambda e, pc=pc, ei=ei, qc=qc: e.matmul(
                    psA[:, pc * 512:(pc + 1) * 512],
                    BB[:, BF_U1 + ei * 256 + pc * 128:BF_U1 + ei * 256 + (pc + 1) * 128],
                    BA[:, ei * S + qc * QCW: ei * S + (qc + 1) * QCW],
                    start=(ei == 0), stop=(ei == ECH - 1)), inc=(ei == ECH - 1))
        tm = c.mark("t")
        c.W("vector", "t", tm)
        vm = c.cop("v", lambda e: e.tensor_copy(BB[:, BF_P:BF_P + 1024], psA[:, 0:1024]))
        c.W("tensor", "v", vm)
        gel = [0] * 32

        def emit_q2(fg):
            half, f = fg // 16, fg % 16
            c.W("tensor", "s", gel[fg])
            for pc in range(2):
                psQ = psC if pc == 0 else psD
                c.cop("t", lambda e, half=half, f=f, fg=fg, pc=pc, psQ=psQ: e.matmul(
                    psQ[:, 0:512],
                    BB[:, BF_U2H + half * 4096 + f * 256 + pc * 128:
                       BF_U2H + half * 4096 + f * 256 + (pc + 1) * 128],
                    BB[:, BF_HH + (fg % 16) * 512:BF_HH + ((fg % 16) + 1) * 512],
                    start=(fg == 0), stop=(fg == 31)), inc=(pc == 1))

        for half in range(2):
            for f in range(16):
                fg = half * 16 + f
                psH = psB[:, (fg % 2) * 512:(fg % 2 + 1) * 512]
                if fg >= 2:
                    c.W("tensor", "s", gel[fg - 2])
                for pc in range(2):
                    c.cop("t", lambda e, half=half, f=f, pc=pc, psH=psH: e.matmul(
                        psH,
                        BB[:, BF_V1H + half * 4096 + pc * 2048 + f * 128:
                           BF_V1H + half * 4096 + pc * 2048 + (f + 1) * 128],
                        BB[:, BF_P + pc * 512:BF_P + (pc + 1) * 512],
                        start=(pc == 0), stop=(pc == 1)), inc=(pc == 1))
                tmH = c.mark("t")
                c.W("scalar", "t", tmH)
                gel[fg] = c.cop("s", lambda e, f=f, fg=fg, psH=psH: e.activation(
                    BB[:, BF_HH + (fg % 16) * 512:BF_HH + ((fg % 16) + 1) * 512], psH,
                    AF.Gelu_apprx_tanh, bias=KP[:, KP_B1 + fg:KP_B1 + fg + 1]))
                if fg >= 1:
                    emit_q2(fg - 1)
        emit_q2(31)
        tm = c.mark("t")
        c.W("vector", "t", tm)
        vm = c.cop("v", lambda e: e.tensor_copy(BB[:, BF_P:BF_P + 512], psC[:, 0:512]))
        c.W("vector", "v", vm)
        vm = c.cop("v", lambda e: e.tensor_copy(BB[:, BF_P + 512:BF_P + 1024], psD[:, 0:512]))
        c.W("tensor", "v", vm)
        silm = [0] * ECH
        for oc in range(ECH):
            psO = psE[:, 0:512]
            if oc >= 1:
                c.W("tensor", "s", silm[oc - 1])
            for pc in range(2):
                c.cop("t", lambda e, oc=oc, pc=pc, psO=psO: e.matmul(
                    psO,
                    BB[:, BF_V2 + pc * 1024 + oc * 128:BF_V2 + pc * 1024 + (oc + 1) * 128],
                    BB[:, BF_P + pc * 512:BF_P + (pc + 1) * 512],
                    start=(pc == 0), stop=(pc == 1)), inc=(pc == 1))
            tmO = c.mark("t")
            c.W("scalar", "t", tmO)
            silm[oc] = c.cop("s", lambda e, oc=oc, psO=psO: e.activation(
                FA[:, FA_G + oc * 512:FA_G + (oc + 1) * 512], psO, AF.Silu,
                bias=KP[:, KP_B2 + oc:KP_B2 + oc + 1]))
        sm_g = c.mark("s")
        if "gg" in dbg:
            c.W("sync", "s", sm_g)
            c.dma("d", lambda e, qc=qc: e.dma_start(
                out=tn["DBG"]["gg"].rearrange("(a p) t -> p a t", p=128)[:, :, qc * QCW:(qc + 1) * QCW],
                in_=FA[:, FA_G:FA_G + 8 * QCW].rearrange("p (a t) -> p a t", a=8)))
        arow = KP[0:1, KP_A + qc * QCW: KP_A + (qc + 1) * QCW]
        brow = KP[32:33, KP_A + qc * QCW: KP_A + (qc + 1) * QCW]
        bm = ph_bcast_rows(c, tn, arow, brow, psA[:, 0:QCW], psB[:, 0:QCW])
        c.W("vector", "t", bm)
        c.W("vector", "s", sm_g)
        c.W("vector", "d")
        for ec in range(ECH):
            sc = FA[:, FA_SS + ec * QCW: FA_SS + (ec + 1) * QCW]
            xc = FA[:, FA_X + ec * QCW: FA_X + (ec + 1) * QCW]
            gch = FA[:, FA_G + ec * QCW: FA_G + (ec + 1) * QCW]
            v = c.cop("v", lambda e, sc=sc: e.tensor_tensor(sc, sc, psA[:, 0:QCW], op=OP.mult))
            c.W("vector", "v", v)
            v = c.cop("v", lambda e, sc=sc: e.tensor_tensor(sc, sc, psB[:, 0:QCW], op=OP.add))
            c.W("vector", "v", v)
            v = c.cop("v", lambda e, sc=sc, ec=ec: e.tensor_scalar(
                sc, sc, KP[:, KP_GAM + ec:KP_GAM + ec + 1], KP[:, KP_BET + ec:KP_BET + ec + 1],
                OP.mult, OP.add))
            c.W("vector", "v", v)
            v = c.cop("v", lambda e, sc=sc, gch=gch: e.tensor_tensor(sc, sc, gch, op=OP.mult))
            c.W("vector", "v", v)
            v = c.cop("v", lambda e, sc=sc, xc=xc: e.tensor_tensor(xc, xc, sc, op=OP.add))
            c.W("vector", "v", v)
        vm = c.mark("v")
        c.W("sync", "v", vm)
        c.dma("d", lambda e, qc=qc: e.dma_start(
            out=x1_view[:, :, qc * QCW:(qc + 1) * QCW],
            in_=FA[:, FA_X:FA_X + 8 * QCW].rearrange("p (a t) -> p a t", a=8)))
        c.W("sync", "d")
    c.sync_all()
    if "x1" in dbg:
        c.dma("d", lambda e: e.dma_start(out=tn["DBG"]["x1"], in_=tn["d_x1"].ap()))
        c.sync_all()



def ph_l1(c, tn, dbg):
    I = tn["I"]; BA = tn["BA"]; BB = tn["BB"]; FA = tn["FA"]; KP = tn["KP"]
    KPB = tn["KPB"]; KPU = tn["KPU"]
    psA = tn["psA"]; psB = tn["psB"]; psC = tn["psC"]; psD = tn["psD"]; psE = tn["psE"]
    psT = psC.bitcast(BF16)
    x1_view = tn["d_x1"].ap().rearrange("(a p) t -> p a t", p=128)

    for qc in range(NQC):
        ph_ln_stats_qc(c, tn, x1_view, qc)
        ph_ln_rows(c, tn, qc)
    c.sync_all()

    xnf_view = tn["d_xnf"].ap().rearrange("(a p) t -> p a t", p=128)
    for qc in range(NQC):
        dxm = c.dma("d", lambda e, qc=qc: e.dma_start(
            out=FA[:, FA_STREAM:FA_STREAM + 8 * QCW].rearrange("p (a t) -> p a t", a=8),
            in_=x1_view[:, :, qc * QCW:(qc + 1) * QCW]))
        arow = KP[0:1, KP_A + qc * QCW: KP_A + (qc + 1) * QCW]
        brow = KP[32:33, KP_A + qc * QCW: KP_A + (qc + 1) * QCW]
        c.W("tensor", "v")
        bm = ph_bcast_rows(c, tn, arow, brow, psC[:, 0:QCW], psE[:, 0:QCW])
        c.W("vector", "t", bm)
        c.W("vector", "d", dxm)
        for ec in range(ECH):
            xc = FA[:, FA_STREAM + ec * QCW: FA_STREAM + (ec + 1) * QCW]
            xnf = FA[:, FA_SS + ec * QCW: FA_SS + (ec + 1) * QCW]
            dstb = BA[:, ec * S + qc * QCW: ec * S + (qc + 1) * QCW]
            v = c.cop("v", lambda e, xc=xc, xnf=xnf: e.tensor_tensor(xnf, xc, psC[:, 0:QCW], op=OP.mult))
            c.W("vector", "v", v)
            v = c.cop("v", lambda e, xnf=xnf: e.tensor_tensor(xnf, xnf, psE[:, 0:QCW], op=OP.add))
            c.W("vector", "v", v)
            v = c.cop("v", lambda e, xnf=xnf, dstb=dstb: e.tensor_copy(dstb, xnf))
            c.W("tensor", "v", v)
        c.W("tensor", "v")
        for tl in range(4):
            tt = qc * 4 + tl
            for ec in range(ECH):
                xnf = FA[:, FA_SS + ec * QCW: FA_SS + (ec + 1) * QCW]
                c.cop("t", lambda e, ec=ec, tl=tl, tt=tt, xnf=xnf: e.matmul(
                    psA[:, tt * 64:(tt + 1) * 64],
                    xnf[:, tl * 128:(tl + 1) * 128],
                    KP[:, KP_GWT + ec * 64:KP_GWT + (ec + 1) * 64],
                    start=(ec == 0), stop=(ec == ECH - 1)), inc=(ec == ECH - 1 and tl == 3))
        vm = c.mark("v")
        c.W("sync", "v", vm)
        c.dma("d", lambda e, qc=qc: e.dma_start(
            out=xnf_view[:, :, qc * QCW:(qc + 1) * QCW],
            in_=FA[:, FA_SS:FA_SS + 8 * QCW].rearrange("p (a t) -> p a t", a=8)))
        c.W("sync", "d")
        c.W("tensor", "d")
    c.sync_all()
    dm_rowb = c.dma("d", lambda e: e.dma_start(
        out=FA[:, FA_ROWB:FA_ROWB + 1024], in_=I["rowb_bc"]))
    if "xnf1" in dbg:
        c.dma("d", lambda e: e.dma_start(out=tn["DBG"]["xnf1"], in_=tn["d_xnf"].ap()))
        c.sync_all()
    if "logits" in dbg:
        v = c.cop("v", lambda e: e.tensor_copy(FA[:, FA_T1:FA_T1 + 1024], psA[:, 0:1024]))
        c.W("sync", "v", v)
        c.dma("d", lambda e: e.dma_start(out=tn["DBG"]["logits"], in_=FA[:, FA_T1:FA_T1 + 1024]))
        c.sync_all()

    scoresT = FA[:, FA_SC:FA_SC + 1024]
    sfc = FA[:, FA_SFC:FA_SFC + 1024]
    Mw = FA[:, FA_MW:FA_MW + 1024]
    selmask = FA[:, FA_SEL:FA_SEL + 1024]
    eq = FA[:, FA_EQ:FA_EQ + 1024]
    cand = FA[:, FA_CAND:FA_CAND + 1024]
    t1 = FA[:, FA_T1:FA_T1 + 1024]
    sl = FA[:, FA_SL:FA_SL + 1024]

    sm = c.cop("s", lambda e: e.activation(t1, psA[:, 0:1024], AF.Exp, scale=-1.0))
    c.W("vector", "s", sm)
    v = c.cop("v", lambda e: e.tensor_scalar(t1, t1, 1.0, None, OP.add))
    c.W("vector", "v", v)
    v = c.cop("v", lambda e: e.reciprocal(scoresT, t1))
    c.W("vector", "v", v)
    v = c.cop("v", lambda e: e.tensor_tensor(sfc, scoresT, KP[:, KP_BIAS:KP_BIAS + 1024], op=OP.add))
    c.W("vector", "v", v)
    v4 = sfc.rearrange("p (t g f w) -> p t g f w", t=16, g=8, f=4)
    h4 = sl.rearrange("p (t g f) -> p t g f", t=16, g=8)[:, :, :, 0:4]
    h4 = FA[:, FA_SL:FA_SL + 512].rearrange("p (t g f) -> p t g f", t=16, g=8)
    l4 = FA[:, FA_T1:FA_T1 + 512].rearrange("p (t g f) -> p t g f", t=16, g=8)
    v = c.cop("v", lambda e: e.tensor_tensor(h4, v4[:, :, :, :, 0:1].rearrange("p t g f w -> p t g (f w)"),
                                             v4[:, :, :, :, 1:2].rearrange("p t g f w -> p t g (f w)"), op=OP.max))
    v = c.cop("v", lambda e: e.tensor_tensor(l4, v4[:, :, :, :, 0:1].rearrange("p t g f w -> p t g (f w)"),
                                             v4[:, :, :, :, 1:2].rearrange("p t g f w -> p t g (f w)"), op=OP.min))
    c.W("vector", "v", v)
    h4v = FA[:, FA_SL:FA_SL + 512].rearrange("p (t g f w) -> p t g f w", t=16, g=8, f=2)
    l4v = FA[:, FA_T1:FA_T1 + 512].rearrange("p (t g f w) -> p t g f w", t=16, g=8, f=2)
    h2 = FA[:, FA_EQ:FA_EQ + 256].rearrange("p (t g f) -> p t g f", t=16, g=8)
    l2a = FA[:, FA_EQ + 256:FA_EQ + 512].rearrange("p (t g f) -> p t g f", t=16, g=8)
    l2b = FA[:, FA_EQ + 512:FA_EQ + 768].rearrange("p (t g f) -> p t g f", t=16, g=8)
    v = c.cop("v", lambda e: e.tensor_tensor(h2, h4v[:, :, :, :, 0:1].rearrange("p t g f w -> p t g (f w)"),
                                             h4v[:, :, :, :, 1:2].rearrange("p t g f w -> p t g (f w)"), op=OP.max))
    v = c.cop("v", lambda e: e.tensor_tensor(l2a, h4v[:, :, :, :, 0:1].rearrange("p t g f w -> p t g (f w)"),
                                             h4v[:, :, :, :, 1:2].rearrange("p t g f w -> p t g (f w)"), op=OP.min))
    v = c.cop("v", lambda e: e.tensor_tensor(l2b, l4v[:, :, :, :, 0:1].rearrange("p t g f w -> p t g (f w)"),
                                             l4v[:, :, :, :, 1:2].rearrange("p t g f w -> p t g (f w)"), op=OP.max))
    c.W("vector", "v", v)
    v = c.cop("v", lambda e: e.tensor_tensor(l2a, l2a, l2b, op=OP.max))
    c.W("vector", "v", v)
    h2v = FA[:, FA_EQ:FA_EQ + 256].rearrange("p (t g w) -> p t g w", t=16, g=8)
    l2v = FA[:, FA_EQ + 256:FA_EQ + 512].rearrange("p (t g w) -> p t g w", t=16, g=8)
    gH = FA[:, FA_CAND:FA_CAND + 128].rearrange("p (t g) -> p t g", t=16)
    gL1 = FA[:, FA_CAND + 128:FA_CAND + 256].rearrange("p (t g) -> p t g", t=16)
    gL2 = FA[:, FA_CAND + 256:FA_CAND + 384].rearrange("p (t g) -> p t g", t=16)
    v = c.cop("v", lambda e: e.tensor_tensor(gH, h2v[:, :, :, 0:1].rearrange("p t g w -> p t (g w)"),
                                             h2v[:, :, :, 1:2].rearrange("p t g w -> p t (g w)"), op=OP.max))
    v = c.cop("v", lambda e: e.tensor_tensor(gL1, h2v[:, :, :, 0:1].rearrange("p t g w -> p t (g w)"),
                                             h2v[:, :, :, 1:2].rearrange("p t g w -> p t (g w)"), op=OP.min))
    v = c.cop("v", lambda e: e.tensor_tensor(gL2, l2v[:, :, :, 0:1].rearrange("p t g w -> p t (g w)"),
                                             l2v[:, :, :, 1:2].rearrange("p t g w -> p t (g w)"), op=OP.max))
    c.W("vector", "v", v)
    v = c.cop("v", lambda e: e.tensor_tensor(gL1, gL1, gL2, op=OP.max))
    c.W("vector", "v", v)
    gs = FA[:, FA_SL + SL_GS:FA_SL + SL_GS + 128]
    v = c.cop("v", lambda e: e.tensor_tensor(gs, FA[:, FA_CAND:FA_CAND + 128],
                                             FA[:, FA_CAND + 128:FA_CAND + 256], op=OP.add))
    c.W("vector", "v", v)
    gmask = FA[:, FA_T1:FA_T1 + 128]
    v = c.cop("v", lambda e: e.memset(gmask, 0.0))
    gsv = gs.rearrange("p (t g) -> p t g", t=16)
    mx = FA[:, FA_SL + SL_R1:FA_SL + SL_R1 + 16]
    mn = FA[:, FA_SL + SL_R1 + 16:FA_SL + SL_R1 + 32]
    eqg = FA[:, FA_EQ:FA_EQ + 128]
    candg = FA[:, FA_EQ + 128:FA_EQ + 256]
    selg = FA[:, FA_EQ + 256:FA_EQ + 384]
    for r in range(4):
        c.W("vector", "v")
        v = c.cop("v", lambda e: e.tensor_reduce(mx.rearrange("p (t o) -> p t o", t=16), gsv, axis=AX, op=OP.max))
        c.W("vector", "v", v)
        v = c.cop("v", lambda e: e.tensor_tensor(
            eqg.rearrange("p (t g) -> p t g", t=16), gsv,
            mx.rearrange("p (t o) -> p t o", t=16).broadcast_to([128, 16, 8]), op=OP.is_equal))
        c.W("vector", "v", v)
        v = c.cop("v", lambda e: e.tensor_tensor(candg, eqg, KP[:, KP_I8B:KP_I8B + 128], op=OP.mult))
        c.W("vector", "v", v)
        v = c.cop("v", lambda e: e.tensor_reduce(
            mn.rearrange("p (t o) -> p t o", t=16),
            candg.rearrange("p (t g) -> p t g", t=16), axis=AX, op=OP.min))
        c.W("vector", "v", v)
        v = c.cop("v", lambda e: e.tensor_tensor(
            selg.rearrange("p (t g) -> p t g", t=16),
            candg.rearrange("p (t g) -> p t g", t=16),
            mn.rearrange("p (t o) -> p t o", t=16).broadcast_to([128, 16, 8]), op=OP.is_equal))
        c.W("vector", "v", v)
        v = c.cop("v", lambda e: e.tensor_tensor(gmask, gmask, selg, op=OP.add))
        c.W("vector", "v", v)
        v = c.cop("v", lambda e: e.scalar_tensor_tensor(gs, selg, -BIGC, gs, OP.mult, OP.add))
    c.W("vector", "v")
    for j in range(8):
        v = c.cop("v", lambda e, j=j: e.tensor_copy(
            Mw.rearrange("p (t g e) -> p t g e", t=16, g=8)[:, :, :, j:j + 1],
            gmask.rearrange("p (t g) -> p t g", t=16).rearrange("p t (g o) -> p t g o", o=1)))
    c.W("vector", "v")
    v = c.cop("v", lambda e: e.tensor_scalar(eq, Mw, BIGC, -BIGC, OP.mult, OP.add))
    c.W("vector", "v", v)
    v = c.cop("v", lambda e: e.tensor_tensor(Mw, Mw, sfc, op=OP.mult))
    c.W("vector", "v", v)
    v = c.cop("v", lambda e: e.tensor_tensor(Mw, Mw, eq, op=OP.add))
    c.W("vector", "v", v)
    BBsel = tn["BB"]
    mx16 = FA[:, FA_SL + SL_R1:FA_SL + SL_R1 + 16]
    mn16 = FA[:, FA_SL + SL_R1 + 16:FA_SL + SL_R1 + 32]
    Mwv = Mw.rearrange("p (t e) -> p t e", t=16)
    c.W("vector", "d", dm_rowb)
    for r in range(TOPK):
        v = c.cop("v", lambda e: e.tensor_reduce(
            mx16.rearrange("p (t o) -> p t o", t=16), Mwv, axis=AX, op=OP.max))
        c.W("vector", "v", v)
        v = c.cop("v", lambda e: e.tensor_tensor(
            eq.rearrange("p (t e) -> p t e", t=16), Mwv,
            mx16.rearrange("p (t o) -> p t o", t=16).broadcast_to([128, 16, 64]), op=OP.is_equal))
        c.W("vector", "v", v)
        v = c.cop("v", lambda e: e.tensor_tensor(cand, eq, KP[:, KP_I64B:KP_I64B + 1024], op=OP.mult))
        c.W("vector", "v", v)
        v = c.cop("v", lambda e: e.tensor_reduce(
            mn16.rearrange("p (t o) -> p t o", t=16),
            cand.rearrange("p (t e) -> p t e", t=16), axis=AX, op=OP.min))
        c.W("vector", "v", v)
        eir = FA[:, FA_SL + SL_EI + r * 16:FA_SL + SL_EI + (r + 1) * 16]
        v = c.cop("v", lambda e, eir=eir: e.tensor_scalar(eir, mn16, BIGC, None, OP.add))
        selr = BBsel[:, BG_SEL + r * 1024:BG_SEL + (r + 1) * 1024]
        v = c.cop("v", lambda e, selr=selr: e.tensor_tensor(
            selr.rearrange("p (t e) -> p t e", t=16),
            cand.rearrange("p (t e) -> p t e", t=16),
            mn16.rearrange("p (t o) -> p t o", t=16).broadcast_to([128, 16, 64]), op=OP.is_equal))
        c.W("vector", "v", v)
        if r == 0:
            v = c.cop("v", lambda e, selr=selr: e.tensor_copy(selmask, selr))
        else:
            v = c.cop("v", lambda e, selr=selr: e.tensor_tensor(selmask, selmask, selr, op=OP.add))
        v = c.cop("v", lambda e, selr=selr: e.tensor_tensor(t1, selr, scoresT, op=OP.mult))
        c.W("vector", "v", v)
        wr = FA[:, FA_SL + SL_W + r * 16:FA_SL + SL_W + (r + 1) * 16]
        v = c.cop("v", lambda e, wr=wr: e.tensor_reduce(
            wr.rearrange("p (t o) -> p t o", t=16),
            t1.rearrange("p (t e) -> p t e", t=16), axis=AX, op=OP.add))
        c.W("vector", "v", v)
        v = c.cop("v", lambda e, selr=selr: e.tensor_tensor(
            t1, selr, FA[:, FA_ROWB:FA_ROWB + 1024], op=OP.mult))
        c.W("vector", "v", v)
        rbr = FA[:, FA_SL + SL_RB + r * 16:FA_SL + SL_RB + (r + 1) * 16]
        v = c.cop("v", lambda e, rbr=rbr: e.tensor_reduce(
            rbr.rearrange("p (t o) -> p t o", t=16),
            t1.rearrange("p (t e) -> p t e", t=16), axis=AX, op=OP.add))
        v = c.cop("v", lambda e, selr=selr: e.scalar_tensor_tensor(Mw, selr, -BIGC, Mw, OP.mult, OP.add))
        c.W("vector", "v", v)
    v = c.cop("v", lambda e: e.tensor_tensor(t1, selmask, scoresT, op=OP.mult))
    c.W("vector", "v", v)
    wsum = FA[:, FA_SL + SL_R2:FA_SL + SL_R2 + 16]
    v = c.cop("v", lambda e: e.tensor_reduce(
        wsum.rearrange("p (t o) -> p t o", t=16),
        t1.rearrange("p (t e) -> p t e", t=16), axis=AX, op=OP.add))
    c.W("vector", "v", v)
    v = c.cop("v", lambda e: e.tensor_scalar(wsum, wsum, 1e-20, None, OP.add))
    c.W("vector", "v", v)
    winv = FA[:, FA_SL + SL_R2 + 16:FA_SL + SL_R2 + 32]
    v = c.cop("v", lambda e: e.reciprocal(winv, wsum))
    c.W("vector", "v", v)
    v = c.cop("v", lambda e: e.tensor_scalar(winv, winv, 2.5, None, OP.mult))
    c.W("vector", "v", v)
    if "gate" in dbg:
        c.W("sync", "v")
        c.W("sync", "s")
        for i, tsr in enumerate([scoresT, sfc, selmask, Mw]):
            c.dma("d", lambda e, i=i, tsr=tsr: e.dma_start(
                out=tn["DBG"]["gate"][:, i * 1024:(i + 1) * 1024], in_=tsr))
        c.sync_all()

    c.W("tensor", "v")
    for tt in range(16):
        c.cop("t", lambda e, tt=tt: e.matmul(
            psD[0:1, 0:64], KP[:, KP_ONE:KP_ONE + 1],
            selmask[:, tt * 64:(tt + 1) * 64],
            start=(tt == 0), stop=(tt == 15)), inc=(tt == 15))
    c.W("vector", "t")
    v = c.cop("v", lambda e: e.tensor_copy(KP[0:1, KP_CNTR:KP_CNTR + 64], psD[0:1, 0:64]))
    c.W("sync", "v", v)
    dm = c.dma("d", lambda e: e.dma_start(out=tn["d_cnt_in"].ap(), in_=KP[0:1, KP_CNTR:KP_CNTR + 64]))
    c.W("gpsimd", "d", dm)
    cm = c.coll(lambda e: e.collective_compute(
        "AllGather", mybir.AluOpType.bypass, replica_groups=tn["allg"],
        ins=[tn["d_cnt_in"].ap()], outs=[tn["d_cnt_out"].ap()]))
    c.W("sync", "c", cm)
    dm = c.dma("d", lambda e: e.dma_start(out=FA[0:8, FA_CNT8:FA_CNT8 + 64], in_=tn["d_cnt_out"].ap()))
    c.W("tensor", "d", dm)
    tm = c.cop("t", lambda e: e.matmul(
        psD[0:1, 0:64], KP[0:8, KP_RANK:KP_RANK + 1], FA[0:8, FA_CNT8:FA_CNT8 + 64],
        start=True, stop=True))
    c.W("vector", "t", tm)
    v = c.cop("v", lambda e: e.tensor_copy(KP[0:1, KP_OFFS:KP_OFFS + 64], psD[0:1, 0:64]))
    c.W("tensor", "v", v)
    tm = c.cop("t", lambda e: e.matmul(
        psE[:, 0:64], KP[0:1, KP_ONE1:KP_ONE1 + 128], KP[0:1, KP_OFFS:KP_OFFS + 64],
        start=True, stop=True))
    c.W("vector", "t", tm)
    offsb = FA[:, FA_T1:FA_T1 + 64]
    vob = c.cop("v", lambda e: e.tensor_copy(offsb, psE[:, 0:64]))
    c.W("tensor", "v", vob)
    if "cnt" in dbg:
        c.W("sync", "d")
        c.dma("d", lambda e: e.dma_start(out=tn["DBG"]["cnt"], in_=tn["d_cnt_out"].ap()))
    for tt in range(16):
        c.W("tensor", "v")
        for tp in range(tt):
            c.cop("t", lambda e, tt=tt, tp=tp: e.matmul(
                psB[:, 0:64], KP[:, KP_ONEST:KP_ONEST + 128],
                selmask[:, tp * 64:(tp + 1) * 64],
                start=(tp == 0), stop=False), inc=False)
        c.cop("t", lambda e, tt=tt: e.matmul(
            psB[:, 0:64], KP[:, KP_UT:KP_UT + 128],
            selmask[:, tt * 64:(tt + 1) * 64],
            start=(tt == 0), stop=True), inc=True)
        tm = c.mark("t")
        c.W("vector", "t", tm)
        v = c.cop("v", lambda e, tt=tt: e.tensor_tensor(
            FA[:, FA_PRE + tt * 64:FA_PRE + (tt + 1) * 64], psB[:, 0:64], offsb, op=OP.add))
        c.W("tensor", "v", v)
    prefix = FA[:, FA_PRE:FA_PRE + 1024]
    c.W("vector", "v")
    for r in range(TOPK):
        selr = BBsel[:, BG_SEL + r * 1024:BG_SEL + (r + 1) * 1024]
        v = c.cop("v", lambda e, selr=selr: e.tensor_tensor(t1, selr, prefix, op=OP.mult))
        c.W("vector", "v", v)
        posr = FA[:, FA_SL + SL_POS + r * 16:FA_SL + SL_POS + (r + 1) * 16]
        v = c.cop("v", lambda e, posr=posr: e.tensor_reduce(
            posr.rearrange("p (t o) -> p t o", t=16),
            t1.rearrange("p (t e) -> p t e", t=16), axis=AX, op=OP.add))
        c.W("vector", "v", v)
        validr = FA[:, FA_SL + SL_R3:FA_SL + SL_R3 + 16]
        v = c.cop("v", lambda e, posr=posr, validr=validr: e.tensor_scalar(
            validr, posr, float(CAP), None, OP.is_lt))
        c.W("vector", "v", v)
        wf = KP[:, KP_WF + r * 16:KP_WF + (r + 1) * 16]
        wr = FA[:, FA_SL + SL_W + r * 16:FA_SL + SL_W + (r + 1) * 16]
        v = c.cop("v", lambda e, wf=wf, wr=wr: e.tensor_tensor(wf, wr, winv, op=OP.mult))
        c.W("vector", "v", v)
        v = c.cop("v", lambda e, wf=wf, validr=validr: e.tensor_tensor(wf, wf, validr, op=OP.mult))
        c.W("vector", "v", v)
        eir = FA[:, FA_SL + SL_EI + r * 16:FA_SL + SL_EI + (r + 1) * 16]
        idxf = FA[:, FA_EQ:FA_EQ + 16]
        v = c.cop("v", lambda e, eir=eir, posr=posr, idxf=idxf: e.scalar_tensor_tensor(
            idxf, eir, float(CAP), posr, OP.mult, OP.add))
        c.W("vector", "v", v)
        v = c.cop("v", lambda e, idxf=idxf, validr=validr: e.scalar_tensor_tensor(
            idxf, validr, -2.0e6, idxf, OP.mult, OP.add))
        c.W("vector", "v", v)
        v = c.cop("v", lambda e, idxf=idxf: e.tensor_scalar(idxf, idxf, 2.0e6, None, OP.add))
        c.W("vector", "v", v)
        v = c.cop("v", lambda e, idxf=idxf, r=r: e.tensor_copy(
            KPU[:, KPU_SCAT + r * 16:KPU_SCAT + (r + 1) * 16], idxf))
        c.W("vector", "v", v)
        v = c.cop("v", lambda e, posr=posr, idxf=idxf: e.tensor_scalar(
            idxf, posr, float(CAP - 1), None, OP.min))
        c.W("vector", "v", v)
        rbr = FA[:, FA_SL + SL_RB + r * 16:FA_SL + SL_RB + (r + 1) * 16]
        v = c.cop("v", lambda e, rbr=rbr, idxf=idxf: e.tensor_tensor(
            idxf, idxf, rbr, op=OP.add))
        c.W("vector", "v", v)
        v = c.cop("v", lambda e, idxf=idxf, r=r: e.tensor_copy(
            KPU[:, KPU_YIDX + r * 16:KPU_YIDX + (r + 1) * 16], idxf))
        c.W("vector", "v", v)
    if "slots" in dbg:
        c.W("sync", "v")
        c.dma("d", lambda e: e.dma_start(
            out=tn["DBG"]["slots"][:, 0:384], in_=FA[:, FA_SL:FA_SL + 384]))
        c.dma("d", lambda e: e.dma_start(
            out=tn["DBG"]["slots"][:, 384:512], in_=KP[:, KP_WF:KP_WF + 128]))
        c.sync_all()

    v = c.cop("v", lambda e: e.memset(FA[:, FA_EQ:FA_EQ + 512], 0.0))
    c.W("sync", "v", v)
    dm = c.dma("d", lambda e: e.dma_start(
        out=tn["d_map_in"].ap().rearrange("(p a) o -> p (a o)", p=128),
        in_=FA[:, FA_EQ:FA_EQ + 512]))
    c.W("gpsimd", "d", dm)
    c.W("gpsimd", "v")
    breg = c.eng.to_reg(NEXP * CAP - 1) if c.eng_name == "gpsimd" else None
    for r in range(TOPK):
        for tt in range(16):
            c.dma("g", lambda e, r=r, tt=tt: e.indirect_dma_start(
                out=tn["d_map_in"].ap(),
                out_offset=IndirectOffsetOnAxis(
                    ap=KPU[:, KPU_SCAT + r * 16 + tt:KPU_SCAT + r * 16 + tt + 1], axis=0),
                in_=KP[:, KP_TOK + tt:KP_TOK + tt + 1], in_offset=None,
                bounds_check=breg, oob_is_err=False))
    gm = c.mark("g")
    c.W("gpsimd", "g", gm)
    cm = c.coll(lambda e: e.collective_compute(
        "AllReduce", mybir.AluOpType.add, replica_groups=tn["allg"],
        ins=[tn["d_map_in"].ap()], outs=[tn["d_map_out"].ap()]))
    if "map" in dbg:
        c.W("sync", "c", cm)
        c.dma("d", lambda e: e.dma_start(out=tn["DBG"]["map"], in_=tn["d_map_out"].ap()))
        c.sync_all()

    for tt in range(16):
        c.W("tensor", "g")
        for ec in range(ECH):
            tmm = c.cop("t", lambda e, tt=tt, ec=ec: e.transpose(
                psT[:, (ec % 4) * 256:(ec % 4) * 256 + 128],
                BA[:, ec * S + tt * 128: ec * S + (tt + 1) * 128],
                KPB[:, KPB_ID:KPB_ID + 128]))
            c.W("vector", "t", tmm)
            v = c.cop("v", lambda e, tt=tt, ec=ec: e.tensor_copy(
                BBsel[:, BG_TOK + ec * 128:BG_TOK + (ec + 1) * 128],
                psT[:, (ec % 4) * 256:(ec % 4) * 256 + 128]))
            c.W("tensor", "v", v)
        vm = c.mark("v")
        c.W("gpsimd", "v", vm)
        c.dma("g", lambda e, tt=tt: e.dma_start(
            out=tn["d_tok_in"].ap()[tt * 128:(tt + 1) * 128, :],
            in_=BBsel[:, BG_TOK:BG_TOK + 1024]))
    gm = c.mark("g")
    c.W("gpsimd", "g", gm)
    cm_tok = c.coll(lambda e: e.collective_compute(
        "AllGather", mybir.AluOpType.bypass, replica_groups=tn["allg"],
        ins=[tn["d_tok_in"].ap()], outs=[tn["d_tok_out"].ap()]))
    c.sync_all()

    ph_exchange(c, tn)
    if "xn1" in dbg:
        for ec in range(ECH):
            c.dma("g", lambda e, ec=ec: e.dma_start(
                out=tn["DBG"]["xn1"][ec * 128:(ec + 1) * 128, :], in_=BA[:, ec * S:(ec + 1) * S]))
        c.sync_all()

    xn_ch = [BA[:, ec * S:(ec + 1) * S] for ec in range(ECH)]
    tn["exp_base"] = B1_EXP
    src = I["Wk1"].rearrange("(a p) o -> p a o", p=128)
    c.dma("g", lambda e, src=src: e.dma_start(
        out=BB[:, B1_WKV:B1_WKV + 2048].rearrange("p (a o) -> p a o", a=8), in_=src))
    src = I["Wv1"].rearrange("(a p) o -> p a o", p=128)
    c.dma("g", lambda e, src=src: e.dma_start(
        out=BB[:, B1_WKV + 2048:B1_WKV + 4096].rearrange("p (a o) -> p a o", a=8), in_=src))
    gm = c.mark("g")
    c.W("tensor", "g", gm)
    c.W("tensor", "v")
    c.W("tensor", "s")
    evk = [0] * 16
    for ch in range(2):
        for kc in range(8):
            ii = ch * 8 + kc
            psk = psA[:, (ii % 2) * 512:(ii % 2 + 1) * 512]
            if ii >= 2:
                c.W("tensor", "s", evk[ii - 2])
            for ei in range(ECH):
                c.cop("t", lambda e, ch=ch, kc=kc, ei=ei, psk=psk: e.matmul(
                    psk,
                    BB[:, B1_WKV + ei * 256 + ch * 128:B1_WKV + ei * 256 + (ch + 1) * 128],
                    xn_ch[ei][:, kc * 512:(kc + 1) * 512],
                    start=(ei == 0), stop=(ei == ECH - 1)), inc=(ei == ECH - 1))
            tmk = c.mark("t")
            c.W("scalar", "t", tmk)
            evk[ii] = c.cop("s", lambda e, ch=ch, kc=kc, psk=psk: e.copy(
                BB[:, B1_K + ch * 4096 + kc * 512:B1_K + ch * 4096 + (kc + 1) * 512], psk))
    c.W("tensor", "s")
    qm = c.cop("q", lambda e: e.memset(BB[:, B1_V:B1_V + 32 * 260], 1.0))
    c.W("vector", "q", qm)
    c.W("tensor", "q", qm)
    evv = [0] * NKT
    for kt in range(NKT):
        psk = psC[:, (kt % 2) * 512:(kt % 2) * 512 + 256]
        if kt >= 2:
            c.W("tensor", "v", evv[kt - 2])
        for ei in range(ECH):
            c.cop("t", lambda e, kt=kt, ei=ei, psk=psk: e.matmul(
                psk, xn_ch[ei][:, kt * 128:(kt + 1) * 128],
                BB[:, B1_WKV + 2048 + ei * 256:B1_WKV + 2048 + (ei + 1) * 256],
                start=(ei == 0), stop=(ei == ECH - 1)), inc=(ei == ECH - 1))
        tmk = c.mark("t")
        c.W("vector", "t", tmk)
        evv[kt] = c.cop("v", lambda e, kt=kt, psk=psk: e.tensor_copy(
            BB[:, B1_V + kt * 260:B1_V + kt * 260 + 260].rearrange(
                "p (g o) -> p g o", g=4)[:, :, 0:64],
            psk.rearrange("p (g o) -> p g o", g=4)))
    c.W("tensor", "v")
    for hp in range(8):
        gpair = hp // 4
        hi = hp % 4
        ha = gpair * 8 + hi
        hb = gpair * 8 + 4 + hi
        ga, gb = gpair * 2, gpair * 2 + 1
        for sub, h in ((0, ha), (1, hb)):
            wq = BB[:, B1_WQ + sub * 1024:B1_WQ + sub * 1024 + 1024]
            c.W("gpsimd", "t")
            src = I["Wq1d"].rearrange("(a p) o -> p a o", p=128)[:, :, h * 128:(h + 1) * 128]
            gm = c.dma("g", lambda e, src=src, wq=wq: e.dma_start(
                out=wq.rearrange("p (a o) -> p a o", a=8), in_=src))
            c.W("tensor", "g", gm)
            c.W("tensor", "v")
            c.W("tensor", "s")
            qbase = B1_Q if sub == 0 else B1_Q2
            evq = [0] * NQC
            for qcc in range(NQC):
                psk = psB[:, (qcc % 2) * 512:(qcc % 2 + 1) * 512]
                if qcc >= 2:
                    c.W("tensor", "s", evq[qcc - 2])
                for ei in range(ECH):
                    c.cop("t", lambda e, qcc=qcc, ei=ei, wq=wq, psk=psk: e.matmul(
                        psk, wq[:, ei * 128:(ei + 1) * 128],
                        xn_ch[ei][:, qcc * 512:(qcc + 1) * 512],
                        start=(ei == 0), stop=(ei == ECH - 1)), inc=(ei == ECH - 1))
                tmk = c.mark("t")
                c.W("scalar", "t", tmk)
                evq[qcc] = c.cop("s", lambda e, qcc=qcc, psk=psk, qbase=qbase: e.copy(
                    BB[:, qbase + qcc * 512:(qbase + (qcc + 1) * 512)], psk))
            c.W("tensor", "s")

        for qcc in range(NQC):
            a_dst_a = tn["d_a"].ap()[ha * 64:(ha + 1) * 64, qcc * 512:(qcc + 1) * 512]
            a_dst_b = tn["d_a"].ap()[hb * 64:(hb + 1) * 64, qcc * 512:(qcc + 1) * 512]
            ph_attn_core_packed(c, tn, qcc, ga, gb, a_dst_a, a_dst_b)
    c.sync_all()
    if "a1" in dbg:
        c.dma("g", lambda e: e.dma_start(out=tn["DBG"]["a1"], in_=tn["d_a"].ap()))
        c.sync_all()

    src = I["Wp1"].rearrange("(a p) o -> p a o", p=128)
    c.dma("g", lambda e, src=src: e.dma_start(
        out=BB[:, BW_WP:BW_WP + 8192].rearrange("p (a o) -> p a o", a=8), in_=src))
    gm = c.mark("g")
    c.W("tensor", "g", gm)
    a_view = tn["d_a"].ap().rearrange("(a p) t -> p a t", p=128)
    xnf_view = tn["d_xnf"].ap().rearrange("(a p) t -> p a t", p=128)
    s_view = tn["d_s"].ap().rearrange("(a p) t -> p a t", p=128)
    for qc in range(NQC):
        c.W("sync", "t")
        dam = c.dma("d", lambda e, qc=qc: e.dma_start(
            out=BB[:, BW_A:BW_A + 4096].rearrange("p (a t) -> p a t", a=8),
            in_=a_view[:, :, qc * QCW:(qc + 1) * QCW]))
        dxm = c.dma("d", lambda e, qc=qc: e.dma_start(
            out=FA[:, FA_X:FA_X + 4096].rearrange("p (a t) -> p a t", a=8),
            in_=xnf_view[:, :, qc * QCW:(qc + 1) * QCW]))
        c.W("tensor", "d", dxm)
        c.W("tensor", "v")
        for oc in range(ECH):
            psX = psA if (oc % 2 == 0) else psB
            for ac in range(ECH):
                c.cop("t", lambda e, oc=oc, ac=ac, psX=psX: e.matmul(
                    psX[:, 0:512],
                    BB[:, BW_WP + ac * 1024 + oc * 128:BW_WP + ac * 1024 + (oc + 1) * 128],
                    BB[:, BW_A + ac * 512:BW_A + (ac + 1) * 512],
                    start=(ac == 0), stop=(ac == ECH - 1)), inc=(ac == ECH - 1))
            tm = c.mark("t")
            c.W("vector", "t", tm)
            c.W("vector", "d", dxm)
            v = c.cop("v", lambda e, oc=oc, psX=psX: e.tensor_tensor(
                FA[:, FA_SS + oc * 512:FA_SS + (oc + 1) * 512],
                FA[:, FA_X + oc * 512:FA_X + (oc + 1) * 512],
                psX[:, 0:512], op=OP.add))
            c.W("tensor", "v", v)
        vm = c.mark("v")
        c.W("sync", "v", vm)
        c.dma("d", lambda e, qc=qc: e.dma_start(
            out=s_view[:, :, qc * QCW:(qc + 1) * QCW],
            in_=FA[:, FA_SS:FA_SS + 4096].rearrange("p (a t) -> p a t", a=8)))
        c.W("sync", "d")
    c.sync_all()
    if "s1" in dbg:
        c.dma("d", lambda e: e.dma_start(out=tn["DBG"]["s1"], in_=tn["d_s"].ap()))
        c.sync_all()

    c.W("gpsimd", "c")
    gm = c.dma("g", lambda e: e.indirect_dma_start(
        out=KP[0:64, KP_MAPF:KP_MAPF + 128], out_offset=None,
        in_=tn["d_map_out"].ap().rearrange("(b w) o -> b (w o)", w=128),
        in_offset=IndirectOffsetOnAxis(ap=KPU[0:64, KPU_MAPIO:KPU_MAPIO + 1], axis=0)))
    c.W("tensor", "g", gm)
    tm = c.cop("t", lambda e: e.transpose(
        psC[:, 0:64], KP[0:64, KP_MAPF:KP_MAPF + 128], KP[0:64, KP_IDF:KP_IDF + 64]))
    c.W("vector", "t", tm)
    v = c.cop("v", lambda e: e.tensor_copy(KPU[:, KPU_MAPU:KPU_MAPU + 64], psC[:, 0:64]))
    c.W("gpsimd", "v", v)

    for j in range(NEXP_MY):
        c.W("gpsimd", "t")
        c.W("gpsimd", "s")
        src = I["wg_my"][j * 8:(j + 1) * 8, :].rearrange("a (p o) -> p a o", p=128)
        c.dma("g", lambda e, src=src: e.dma_start(
            out=BB[:, BM_WG:BM_WG + 4096].rearrange("p (a o) -> p a o", a=8), in_=src))
        src = I["wu_my"][j * 8:(j + 1) * 8, :].rearrange("a (p o) -> p a o", p=128)
        c.dma("g", lambda e, src=src: e.dma_start(
            out=BB[:, BM_WU:BM_WU + 4096].rearrange("p (a o) -> p a o", a=8), in_=src))
        src = I["wd_my"][j * 4:(j + 1) * 4, :].rearrange("a (p o) -> p a o", p=128)
        c.dma("g", lambda e, src=src: e.dma_start(
            out=BB[:, BM_WD:BM_WD + 4096].rearrange("p (a o) -> p a o", a=4), in_=src))
        gpm = [0] * 8
        tpm = [0] * 8

        def emit_gather_pb(pb, j=j):
            gslot = BB[:, BM_GT + (pb % 2) * 1024:BM_GT + (pb % 2 + 1) * 1024]
            gq = "g" if pb % 2 == 0 else "g2"
            if pb >= 2:
                c.W("gpsimd", "t", tpm[pb - 2])
            c.dma(gq, lambda e, j=j, pb=pb, gslot=gslot: e.indirect_dma_start(
                out=gslot, out_offset=None,
                in_=tn["d_tok_out"].ap(),
                in_offset=IndirectOffsetOnAxis(
                    ap=KPU[:, KPU_MAPU + j * 8 + pb:KPU_MAPU + j * 8 + pb + 1], axis=0)))
            gpm[pb] = c.mark(gq)

        emit_gather_pb(0)
        for pb in range(8):
            gslot = BB[:, BM_GT + (pb % 2) * 1024:BM_GT + (pb % 2 + 1) * 1024]
            if pb + 1 < 8:
                emit_gather_pb(pb + 1)
            c.W("tensor", "g" if pb % 2 == 0 else "g2", gpm[pb])
            for ec in range(ECH):
                tmm = c.cop("t", lambda e, ec=ec, gslot=gslot: e.transpose(
                    psT[:, (ec % 4) * 256:(ec % 4) * 256 + 128],
                    gslot[:, ec * 128:(ec + 1) * 128], KPB[:, KPB_ID:KPB_ID + 128]))
                c.W("vector", "t", tmm)
                v = c.cop("v", lambda e, ec=ec, pb=pb: e.tensor_copy(
                    BB[:, BM_BUFT + ec * 1024 + pb * 128:BM_BUFT + ec * 1024 + (pb + 1) * 128],
                    psT[:, (ec % 4) * 256:(ec % 4) * 256 + 128]))
                c.W("tensor", "v", v)
            tpm[pb] = c.mark("t")
        gmw = c.mark("g")
        c.W("tensor", "g", gmw)
        c.W("tensor", "v")
        gum = [0] * 8
        silm = [0] * 8
        mvm = [0] * 8

        def emit_gu(rt):
            pa = psA[:, (rt % 2) * 512:(rt % 2 + 1) * 512]
            pb = psB[:, (rt % 2) * 512:(rt % 2 + 1) * 512]
            if rt >= 2:
                c.W("tensor", "s", silm[rt - 2])
                c.W("tensor", "v", mvm[rt - 2])
            for hc in range(ECH):
                c.cop("t", lambda e, rt=rt, hc=hc, pa=pa: e.matmul(
                    pa,
                    BB[:, BM_BUFT + hc * 1024 + rt * 128:BM_BUFT + hc * 1024 + (rt + 1) * 128],
                    BB[:, BM_WG + hc * 512:BM_WG + (hc + 1) * 512],
                    start=(hc == 0), stop=(hc == ECH - 1)), inc=(hc == ECH - 1))
            for hc in range(ECH):
                c.cop("t", lambda e, rt=rt, hc=hc, pb=pb: e.matmul(
                    pb,
                    BB[:, BM_BUFT + hc * 1024 + rt * 128:BM_BUFT + hc * 1024 + (rt + 1) * 128],
                    BB[:, BM_WU + hc * 512:BM_WU + (hc + 1) * 512],
                    start=(hc == 0), stop=(hc == ECH - 1)), inc=(hc == ECH - 1))
            gum[rt] = c.mark("t")

        emit_gu(0)
        for rt in range(8):
            if rt + 1 < 8:
                emit_gu(rt + 1)
            pa = psA[:, (rt % 2) * 512:(rt % 2 + 1) * 512]
            pb = psB[:, (rt % 2) * 512:(rt % 2 + 1) * 512]
            c.W("scalar", "t", gum[rt])
            c.W("scalar", "v", mvm[rt - 2] if rt >= 2 else 0)
            silm[rt] = c.cop("s", lambda e, pa=pa, rt=rt: e.activation(
                BB[:, BM_SIL + (rt % 2) * 512:BM_SIL + (rt % 2 + 1) * 512], pa, AF.Silu))
            c.W("vector", "s", silm[rt])
            mslot = BB[:, BM_M + (rt % 2) * 512:BM_M + (rt % 2 + 1) * 512]
            v = c.cop("v", lambda e, mslot=mslot, pb=pb, rt=rt: e.tensor_tensor(
                mslot, BB[:, BM_SIL + (rt % 2) * 512:BM_SIL + (rt % 2 + 1) * 512], pb, op=OP.mult))
            c.W("tensor", "v", v)
            for ic in range(4):
                tmm = c.cop("t", lambda e, ic=ic, mslot=mslot: e.transpose(
                    psT[:, (ic % 4) * 256:(ic % 4) * 256 + 128],
                    mslot[:, ic * 128:(ic + 1) * 128], KPB[:, KPB_ID:KPB_ID + 128]))
                c.W("vector", "t", tmm)
                v = c.cop("v", lambda e, ic=ic, rt=rt: e.tensor_copy(
                    BB[:, BM_MT + ic * 1024 + rt * 128:BM_MT + ic * 1024 + (rt + 1) * 128],
                    psT[:, (ic % 4) * 256:(ic % 4) * 256 + 128]))
                c.W("tensor", "v", v)
            mvm[rt] = c.mark("v")
            for half in range(2):
                psY = psD if half == 0 else psE
                for ic in range(4):
                    c.cop("t", lambda e, ic=ic, rt=rt, half=half, psY=psY: e.matmul(
                        psY[:, 0:512],
                        BB[:, BM_MT + ic * 1024 + rt * 128:BM_MT + ic * 1024 + (rt + 1) * 128],
                        BB[:, BM_WD + ic * 1024 + half * 512:BM_WD + ic * 1024 + (half + 1) * 512],
                        start=(ic == 0), stop=(ic == 3)), inc=(ic == 3))
            tm = c.mark("t")
            c.W("vector", "t", tm)
            c.W("vector", "d")
            v = c.cop("v", lambda e: e.tensor_copy(BB[:, BM_Y:BM_Y + 512], psD[:, 0:512]))
            v = c.cop("v", lambda e: e.tensor_copy(BB[:, BM_Y + 512:BM_Y + 1024], psE[:, 0:512]))
            c.W("sync", "v", v)
            c.W("tensor", "v", v)
            dm = c.dma("d", lambda e, j=j, rt=rt: e.dma_start(
                out=tn["d_y_in"].ap()[j * CAP + rt * 128:j * CAP + (rt + 1) * 128, :],
                in_=BB[:, BM_Y:BM_Y + 1024]))
        if j == 3:
            c.W("gpsimd", "d")
            c.coll(lambda e: e.collective_compute(
                "AllGather", mybir.AluOpType.bypass, replica_groups=tn["allg"],
                ins=[tn["d_y_in"].ap()[0:4 * CAP, :]],
                outs=[tn["d_y_out"].ap()[0:NCORE * 4 * CAP, :]]))
    c.W("gpsimd", "d")
    cm = c.coll(lambda e: e.collective_compute(
        "AllGather", mybir.AluOpType.bypass, replica_groups=tn["allg"],
        ins=[tn["d_y_in"].ap()[4 * CAP:8 * CAP, :]],
        outs=[tn["d_y_out"].ap()[NCORE * 4 * CAP:NCORE * 8 * CAP, :]]))
    c.sync_all()
    if "y" in dbg:
        c.dma("g", lambda e: e.dma_start(out=tn["DBG"]["y"], in_=tn["d_y_out"].ap()))
        c.sync_all()

    x1v = tn["d_x1"].ap()
    sv_ = tn["d_s"].ap()
    gmarks = [0] * 19
    d2marks = [0] * 19
    vmarks = [0] * 19

    def emit_gather(tt):
        if tt >= 4:
            c.W("gpsimd", "v", vmarks[tt - 4])
        base = BY_Y + (tt % 4) * 8192
        for r in range(TOPK):
            q = "g" if r % 2 == 0 else "g2"
            c.dma(q, lambda e, r=r, tt=tt, base=base: e.indirect_dma_start(
                out=BB[:, base + r * 1024:base + (r + 1) * 1024], out_offset=None,
                in_=tn["d_y_out"].ap(),
                in_offset=IndirectOffsetOnAxis(
                    ap=KPU[:, KPU_YIDX + r * 16 + tt:KPU_YIDX + r * 16 + tt + 1], axis=0)))
        gmarks[tt] = c.mark("g")
        d2marks[tt] = c.mark("g2")

    emit_gather(0)
    emit_gather(1)
    emit_gather(2)
    for tt in range(16):
        if tt + 3 < 16:
            emit_gather(tt + 3)
        c.W("vector", "g", gmarks[tt])
        c.W("vector", "g2", d2marks[tt])
        for r in range(TOPK):
            wf = KP[:, KP_WF + r * 16 + tt:KP_WF + r * 16 + tt + 1]
            yr = BB[:, BY_Y + (tt % 4) * 8192 + r * 1024:BY_Y + (tt % 4) * 8192 + (r + 1) * 1024]
            if r == 0:
                v = c.cop("v", lambda e, yr=yr, wf=wf: e.tensor_scalar(
                    FA[:, FA_ACC:FA_ACC + 1024], yr, wf, None, OP.mult))
            else:
                v = c.cop("v", lambda e, yr=yr, wf=wf: e.scalar_tensor_tensor(
                    FA[:, FA_ACC:FA_ACC + 1024], yr, wf, FA[:, FA_ACC:FA_ACC + 1024],
                    OP.mult, OP.add))
            c.W("vector", "v", v)
        vmarks[tt] = c.mark("v")
        c.W("scalar", "v")
        sm = c.cop("s", lambda e: e.activation(
            FA[:, FA_GT:FA_GT + 1024], FA[:, FA_ACC:FA_ACC + 1024], AF.Silu))
        dxm = c.dma("d", lambda e, tt=tt: e.dma_start(
            out=FA[:, FA_XC:FA_XC + 1024].rearrange("p (a t) -> p a t", a=8),
            in_=x1v.rearrange("(a p) t -> p a t", p=128)[:, :, tt * 128:(tt + 1) * 128]))
        dsm = c.dma("d", lambda e, tt=tt: e.dma_start(
            out=FA[:, FA_SC2:FA_SC2 + 1024].rearrange("p (a t) -> p a t", a=8),
            in_=sv_.rearrange("(a p) t -> p a t", p=128)[:, :, tt * 128:(tt + 1) * 128]))
        c.W("tensor", "s", sm)
        c.W("tensor", "v")
        for ec in range(ECH):
            tmm = c.cop("t", lambda e, ec=ec: e.transpose(
                psC[:, 0:128] if ec % 2 == 0 else psD[:, 0:128],
                FA[:, FA_GT + ec * 128:FA_GT + (ec + 1) * 128], KP[:, KP_IDF:KP_IDF + 128]))
            c.W("vector", "t", tmm)
            c.W("vector", "d", dsm)
            psg = psC if ec % 2 == 0 else psD
            v = c.cop("v", lambda e, ec=ec, psg=psg: e.tensor_tensor(
                FA[:, FA_XO + ec * 128:FA_XO + (ec + 1) * 128],
                FA[:, FA_SC2 + ec * 128:FA_SC2 + (ec + 1) * 128],
                psg[:, 0:128], op=OP.mult))
            c.W("vector", "v", v)
            v = c.cop("v", lambda e, ec=ec: e.tensor_tensor(
                FA[:, FA_XO + ec * 128:FA_XO + (ec + 1) * 128],
                FA[:, FA_XO + ec * 128:FA_XO + (ec + 1) * 128],
                FA[:, FA_XC + ec * 128:FA_XC + (ec + 1) * 128], op=OP.add))
            c.W("tensor", "v", v)
        vm = c.mark("v")
        c.W("sync", "v", vm)
        c.dma("d", lambda e, tt=tt: e.dma_start(
            out=tn["out_T"].rearrange("(a p) t -> p a t", p=128)[:, :, tt * 128:(tt + 1) * 128],
            in_=FA[:, FA_XO:FA_XO + 1024].rearrange("p (a t) -> p a t", a=8)))
        c.W("sync", "d")
    c.sync_all()


def emit_program(c, tn, dbg):
    I = tn["I"]; KP = tn["KP"]; KPB = tn["KPB"]; KPU = tn["KPU"]; FA = tn["FA"]
    small = [("gammaT", KP_GAM, 8), ("betaT", KP_BET, 8),
             ("b1T", KP_B1, 32), ("b2T", KP_B2, 8),
             ("iota64B", KP_I64B, 1024), ("i8B", KP_I8B, 128),
             ("bias_bc", KP_BIAS, 1024),
             ("onesT_f", KP_ONEST, 128), ("UT_f", KP_UT, 128),
             ("ident_f", KP_IDF, 128), ("ones_f", KP_ONE, 1),
             ("tokio", KP_TOK, 16), ("sgn_c", KP_SGN, 1),
             ("gate_wT_r", KP_GWT, 512)]
    for nm, off, w in small:
        c.dma("d", lambda e, nm=nm, off=off, w=w: e.dma_start(
            out=KP[:, off:off + w], in_=I[nm]))
    c.dma("d", lambda e: e.dma_start(out=KP[0:1, KP_ONE1:KP_ONE1 + 128], in_=I["ones1_f"]))
    c.dma("d", lambda e: e.dma_start(out=KP[32:33, KP_ONE1:KP_ONE1 + 128], in_=I["ones1_f"]))
    c.dma("d", lambda e: e.dma_start(out=KP[0:8, KP_RANK:KP_RANK + 1], in_=I["rankm"]))
    c.dma("d", lambda e: e.dma_start(out=KPB[:, KPB_SIGNS:KPB_SIGNS + 2048], in_=I["signs_b"]))
    c.dma("d", lambda e: e.dma_start(out=KPB[:, KPB_ID:KPB_ID + 128], in_=I["ident_b"]))
    c.dma("d", lambda e: e.dma_start(out=KPU[:, KPU_PG:KPU_PG + 8], in_=I["pg_idx"]))
    c.dma("d", lambda e: e.dma_start(out=KPU[0:64, KPU_MAPIO:KPU_MAPIO + 1], in_=I["mapio"]))
    c.dma("d", lambda e: e.dma_start(out=FA[:, FA_IQ:FA_IQ + T], in_=I["iq_c"]))
    c.dma("d", lambda e: e.dma_start(out=FA[:, FA_IK2:FA_IK2 + S], in_=I["ik2_c"]))
    c.cop("q", lambda e: e.memset(KP[:, KP_EPS:KP_EPS + 1], EPS))
    c.sync_all()
    if not globals().get("_SKIP_L0", False):
        ph_l0(c, tn, dbg)
    if not globals().get("_SKIP_L1", False):
        ph_l1(c, tn, dbg)
    c.sync_all()


def build_nc(dbg=()):
    nc = bass.Bass("TRN2", target_bir_lowering=False)

    def din(name, shape, dtype=F32):
        return nc.dram_tensor(name, list(shape), dtype, kind="ExternalInput").ap()

    I = {}
    I["x_T"] = din("x_T", (E, T))
    for nm, shp in [("WqTd", (E, H * 128)), ("WkTd", (E, H * 128)), ("WvT0", (E, E)),
                    ("U1", (E, 256)), ("V1", (256, 4096)), ("U2", (4096, 256)),
                    ("V2", (256, E)),
                    ("Wq1d", (E, 2048)), ("Wk1", (E, 256)), ("Wv1", (E, 256)),
                    ("Wp1", (E, E)),
                    ("wg_my", (NEXP_MY * ECH, 128 * 512)),
                    ("wu_my", (NEXP_MY * ECH, 128 * 512)),
                    ("wd_my", (NEXP_MY * 4, 128 * 1024))]:
        I[nm] = din(nm, shp)
    for nm, shp in [("gammaT", (128, 8)), ("betaT", (128, 8)),
                    ("b1T", (128, 32)), ("b2T", (128, 8)),
                    ("iq_c", (128, T)), ("ik2_c", (128, S)), ("sgn_c", (128, 1)),
                    ("ident_f", (128, 128)), ("ones_f", (128, 1)),
                    ("ones1_f", (1, 128)), ("onesT_f", (128, 128)), ("UT_f", (128, 128)),
                    ("iota64B", (128, 1024)), ("i8B", (128, 128)),
                    ("bias_bc", (128, 1024)), ("rowb_bc", (128, 1024)),
                    ("tokio", (128, 16)), ("rankm", (8, 1)),
                    ("gate_wT_r", (128, 512))]:
        I[nm] = din(nm, shp)
    I["signs_b"] = din("signs_b", (128, 2048), BF16)
    I["ident_b"] = din("ident_b", (128, 128), BF16)
    I["pg_idx"] = din("pg_idx", (128, 8), U32)
    I["mapio"] = din("mapio", (64, 1), U32)
    out_T = dout = nc.dram_tensor("out_T", [E, T], F32, kind="ExternalOutput").ap()

    pairs = [[0, 1], [2, 3], [4, 5], [6, 7]]
    allg = [list(range(8))]
    d_x1 = nc.dram_tensor("d_x1", [E, T], F32)
    d_s = nc.dram_tensor("d_s", [E, T], F32)
    d_xnf = nc.dram_tensor("d_xnf", [E, T], F32)
    d_a = nc.dram_tensor("d_a", [E, T], BF16)
    d_ag_in = nc.dram_tensor("d_ag_in", [1024, T], BF16)
    d_ag_out = nc.dram_tensor(
        "d_ag_out", [2048, T], BF16,
        addr_space=maybe_share_collective_output_space("AllGather", pairs))
    d_tok_in = nc.dram_tensor("d_tok_in", [T, E], BF16)
    d_tok_out = nc.dram_tensor(
        "d_tok_out", [NCORE * T, E], BF16,
        addr_space=maybe_share_collective_output_space("AllGather", allg))
    d_cnt_in = nc.dram_tensor("d_cnt_in", [1, 64], F32)
    d_cnt_out = nc.dram_tensor(
        "d_cnt_out", [8, 64], F32,
        addr_space=maybe_share_collective_output_space("AllGather", allg))
    d_map_in = nc.dram_tensor("d_map_in", [NEXP * CAP, 1], F32)
    d_map_out = nc.dram_tensor(
        "d_map_out", [NEXP * CAP, 1], F32,
        addr_space=maybe_share_collective_output_space("AllReduce", allg))
    d_y_in = nc.dram_tensor("d_y_in", [NEXP_MY * CAP, E], BF16)
    d_y_out = nc.dram_tensor(
        "d_y_out", [NEXP * CAP, E], BF16,
        addr_space=maybe_share_collective_output_space("AllGather", allg))

    DBG_OUTS = {}
    for nm in dbg:
        shp, _ = DBG_SPECS[nm]
        DBG_OUTS[nm] = nc.dram_tensor("dbg_" + nm, list(shp), F32, kind="ExternalOutput").ap()

    import contextlib
    with contextlib.ExitStack() as _st:
        def _al(cm):
            return _st.enter_context(cm)
        BA = _al(nc.sbuf_tensor("BA", [128, BA_W], BF16))
        BB = _al(nc.sbuf_tensor("BB", [128, BB_W], BF16))
        FA = _al(nc.sbuf_tensor("FA", [128, FA_W], F32))
        KP = _al(nc.sbuf_tensor("KP", [128, KP_W], F32))
        KPU = _al(nc.sbuf_tensor("KPU", [128, KPU_W], U32))
        KPB = _al(nc.sbuf_tensor("KPB", [128, KPB_W], BF16))
        psA = _al(nc.psum_tensor("psA", [128, 1024], F32))
        psB = _al(nc.psum_tensor("psB", [128, 1024], F32))
        psC = _al(nc.psum_tensor("psC", [128, 1024], F32))
        psD = _al(nc.psum_tensor("psD", [128, 512], F32))
        psE = _al(nc.psum_tensor("psE", [128, 512], F32))
        sem_b = _al(nc.semaphore("sem_b"))
        sem_d = _al(nc.semaphore("sem_d"))
        sem_g = _al(nc.semaphore("sem_g"))
        sem_d2 = _al(nc.semaphore("sem_d2"))
        sem_v = _al(nc.semaphore("sem_v"))
        sem_s = _al(nc.semaphore("sem_s"))
        sem_t = _al(nc.semaphore("sem_t"))
        sem_q = _al(nc.semaphore("sem_q"))
        sem_c = _al(nc.semaphore("sem_c"))
        sem_g2 = _al(nc.semaphore("sem_g2"))
        block = _al(nc.Block())
        sems = {"b": sem_b, "d": sem_d, "g": sem_g, "g2": sem_g2, "d2": sem_d2,
                "v": sem_v, "s": sem_s, "t": sem_t, "q": sem_q, "c": sem_c}
        tn = dict(I=I, out_T=out_T, d_x1=d_x1, d_s=d_s, d_xnf=d_xnf, d_a=d_a,
                  d_ag_in=d_ag_in, d_ag_out=d_ag_out, d_tok_in=d_tok_in,
                  d_tok_out=d_tok_out, d_cnt_in=d_cnt_in, d_cnt_out=d_cnt_out,
                  d_map_in=d_map_in, d_map_out=d_map_out, d_y_in=d_y_in,
                  d_y_out=d_y_out, BA=BA, BB=BB, FA=FA, KP=KP, KPU=KPU, KPB=KPB,
                  psA=psA, psB=psB, psC=psC, psD=psD, psE=psE,
                  pairs=pairs, allg=allg, DBG=DBG_OUTS)

        def emit_all(eng_name, eng):
            c = Ctx(nc, sems, eng_name, eng)
            emit_program(c, tn, dbg)

        @block.sync
        def _(sync):
            emit_all("sync", sync)

        @block.vector
        def _(vector):
            emit_all("vector", vector)

        @block.scalar
        def _(scalar):
            emit_all("scalar", scalar)

        @block.tensor
        def _(tensor):
            emit_all("tensor", tensor)

        @block.gpsimd
        def _(gpsimd):
            emit_all("gpsimd", gpsimd)

    return nc



_LAST_DBG = None


def _prep_inputs(inputs):
    x = np.ascontiguousarray(np.asarray(inputs["L0_attn_x"], np.float32))
    Wq = np.asarray(inputs["L0_attn_Wq"], np.float32)
    Wk = np.asarray(inputs["L0_attn_Wk"], np.float32)
    Wv = np.asarray(inputs["L0_attn_Wv"], np.float32)
    gam = np.asarray(inputs["L0_attn_ln_gamma"], np.float32)
    bet = np.asarray(inputs["L0_attn_ln_beta"], np.float32)
    U1 = np.asarray(inputs["L0_ffn_U1"], np.float32)
    V1 = np.asarray(inputs["L0_ffn_V1"], np.float32)
    b1 = np.asarray(inputs["L0_ffn_b1"], np.float32)
    U2 = np.asarray(inputs["L0_ffn_U2"], np.float32)
    V2 = np.asarray(inputs["L0_ffn_V2"], np.float32)
    b2 = np.asarray(inputs["L0_ffn_b2"], np.float32)
    Wq1 = np.asarray(inputs["L1_attn_Wq"], np.float32)
    Wk1 = np.asarray(inputs["L1_attn_Wk"], np.float32)
    Wv1 = np.asarray(inputs["L1_attn_Wv"], np.float32)
    Wp1 = np.asarray(inputs["L1_attn_Wpost"], np.float32)
    gw = np.asarray(inputs["L1_ffn_gate_w"], np.float32)
    gb = np.asarray(inputs["L1_ffn_gate_bias"], np.float32)
    wgt = np.asarray(inputs["L1_ffn_w_gate"], np.float32)
    wup = np.asarray(inputs["L1_ffn_w_up"], np.float32)
    wdn = np.asarray(inputs["L1_ffn_w_down"], np.float32)

    WqT = Wq.T.copy()
    WkT = Wk.T.copy()
    WqTd = np.concatenate([np.concatenate([WqT[:, h * 64:(h + 1) * 64]] * 2, 1) for h in range(H)], 1)
    WkTd = np.concatenate([np.concatenate([WkT[:, h * 64:(h + 1) * 64]] * 2, 1) for h in range(H)], 1)

    shared = {
        "WqTd": np.ascontiguousarray(WqTd), "WkTd": np.ascontiguousarray(WkTd),
        "WvT0": np.ascontiguousarray(Wv.T),
        "U1": U1, "V1": V1, "U2": U2, "V2": V2,
        "Wq1d": np.ascontiguousarray(np.concatenate(
            [np.concatenate([Wq1[:, h * 64:(h + 1) * 64]] * 2, 1) for h in range(H)], 1)),
        "Wk1": Wk1, "Wv1": Wv1, "Wp1": Wp1,
        "gammaT": np.ascontiguousarray(gam.reshape(8, 128).T),
        "betaT": np.ascontiguousarray(bet.reshape(8, 128).T),
        "b1T": np.ascontiguousarray(b1.reshape(32, 128).T),
        "b2T": np.ascontiguousarray(b2.reshape(8, 128).T),
        "ident_f": np.eye(128, dtype=np.float32),
        "ident_b": np.eye(128, dtype=np.float32).astype(BF),
        "ones_f": np.ones((128, 1), np.float32),
        "ones1_f": np.ones((1, 128), np.float32),
        "onesT_f": np.ones((128, 128), np.float32),
        "UT_f": np.triu(np.ones((128, 128), np.float32), 1),
        "gate_wT_r": np.ascontiguousarray(gw.T.reshape(8, 128, 64).transpose(1, 0, 2).reshape(128, 512)),
        "bias_bc": np.ascontiguousarray(np.broadcast_to(np.tile(gb, 16)[None, :], (128, 1024))),
    }
    i64 = np.tile(np.arange(64, dtype=np.float32) - BIGC, 16)
    shared["iota64B"] = np.ascontiguousarray(np.broadcast_to(i64[None, :], (128, 1024)))
    ee = np.arange(64)
    rowb = (np.where(ee % 8 >= 4, 32768.0, 0.0) + (ee // 8) * 4096.0
            + ((ee % 8) % 4) * 1024.0).astype(np.float32)
    shared["rowb_bc"] = np.ascontiguousarray(
        np.broadcast_to(np.tile(rowb, 16)[None, :], (128, 1024)))
    i8 = np.tile(np.arange(8, dtype=np.float32) - BIGC, 16)
    shared["i8B"] = np.ascontiguousarray(np.broadcast_to(i8[None, :], (128, 128)))
    signs = np.zeros((128, 2048), np.float32)
    for sub in range(2):
        for i in range(2):
            off = (sub * 2 + i) * 128
            qq = np.arange(512, dtype=np.float32)[None, :]
            pp = np.arange(128, dtype=np.float32)[:, None]
            signs[:, sub * 1024 + i * 512: sub * 1024 + (i + 1) * 512] = np.sign(qq - off - pp)
    shared["signs_b"] = signs.astype(BF)

    in_maps = []
    for cc in range(NCORE):
        b = cc // 2
        r0 = (cc % 2) * T
        sg = 1.0 if (cc % 2) == 1 else -1.0
        m = dict(shared)
        m["x_T"] = np.ascontiguousarray(x[b, r0:r0 + T, :].T)
        m["iq_c"] = np.ascontiguousarray(np.broadcast_to(
            ((np.arange(T, dtype=np.float64) + r0) * CL0).astype(np.float32)[None, :], (128, T)))
        kown = (np.arange(T, dtype=np.float64) + r0) * CL0
        kpart = (np.arange(T, dtype=np.float64) + (T - r0)) * CL0 * sg
        m["ik2_c"] = np.ascontiguousarray(np.broadcast_to(
            np.concatenate([kown, kpart]).astype(np.float32)[None, :], (128, S)))
        m["sgn_c"] = np.full((128, 1), sg, np.float32)
        m["tokio"] = np.ascontiguousarray(
            (cc * T + np.arange(16, dtype=np.float32)[None, :] * 128
             + np.arange(128, dtype=np.float32)[:, None]))
        m["rankm"] = (np.arange(8) < cc).astype(np.float32).reshape(8, 1)
        m["pg_idx"] = np.ascontiguousarray(
            ((1 - (cc % 2)) * 1024 + np.arange(8, dtype=np.uint32)[None, :] * 128
             + np.arange(128, dtype=np.uint32)[:, None]).astype(np.uint32))
        m["mapio"] = (cc * 64 + np.arange(64, dtype=np.uint32)).reshape(64, 1)
        eb = cc * NEXP_MY
        m["wg_my"] = np.ascontiguousarray(
            wgt[eb:eb + 8].reshape(8, 8, 128, 512).transpose(0, 1, 2, 3).reshape(64, 128 * 512))
        m["wu_my"] = np.ascontiguousarray(
            wup[eb:eb + 8].reshape(8, 8, 128, 512).reshape(64, 128 * 512))
        m["wd_my"] = np.ascontiguousarray(
            wdn[eb:eb + 8].reshape(8, 4, 128, 1024).reshape(32, 128 * 1024))
        in_maps.append(m)
    return in_maps


def kernel(**inputs):
    global _LAST_DBG
    dbg = tuple(globals().get("_BUILD_DBG", ()))
    in_maps = _prep_inputs(inputs)
    nc = build_nc(dbg)
    res = run_bass_kernel_spmd(nc, in_maps, core_ids=list(range(NCORE)))
    out = np.empty((4, S, E), np.float32)
    for cc in range(NCORE):
        b = cc // 2
        r0 = (cc % 2) * T
        out[b, r0:r0 + T, :] = res.results[cc]["out_T"].T
    if dbg:
        _LAST_DBG = [{k: res.results[cc].get("dbg_" + k) for k in dbg} for cc in range(NCORE)]
    return out


# version 13
# speedup vs baseline: 1.0271x; 1.0271x over previous
import numpy as np
import ml_dtypes
import concourse.bass as bass
import concourse.mybir as mybir
from concourse.alu_op_type import AluOpType as OP
from concourse.bass_utils import run_bass_kernel_spmd
from concourse.bass import IndirectOffsetOnAxis
from concourse.replica_groups import maybe_share_collective_output_space

F32 = mybir.dt.float32
BF16 = mybir.dt.bfloat16
U32 = mybir.dt.uint32
AF = mybir.ActivationFunctionType
AX = mybir.AxisListType.X
BF = ml_dtypes.bfloat16

NCORE = 8
T = 2048
S = 4096
E = 1024
ECH = 8
H = 16
DH = 64
QCW = 512
NQC = T // QCW
NKT = S // 128
CE = E ** -0.5
CL0 = CE / S
BIGC = 65536.0
CAP = 1024
NEXP = 64
NEXP_MY = 8
TOPK = 8
EPS = 1e-5

SEM_ENG = {"d": "sync", "d2": "scalar", "g": "gpsimd", "g2": "gpsimd",
           "v": "vector", "s": "scalar", "t": "tensor", "q": "gpsimd",
           "c": "gpsimd"}

FA_STREAM = 0
FA_SQ = 4096
FA_ROWS = 5120
FA_IQ = 6144
FA_IK2 = 8192
FA_PEV = 0
FA_REC = 512
FA_G = 0
FA_X = 4096
FA_SS = 8192
FA_EQ = 0
FA_CAND = 1024
FA_T1 = 2048
FA_CNT8 = 3072
FA_SC = 6144
FA_SFC = 7168
FA_MW = 8192
FA_SEL = 9216
FA_PRE = 10240
FA_SL = 11264
SL_EI = 0
SL_POS = 128
SL_W = 256
SL_GS = 384
SL_H4 = 512
SL_R1 = 512
SL_R2 = 576
SL_R3 = 640
SL_RB = 768
FA_ROWB = 4096
FA_ACC = 0
FA_GT = 1024
FA_XC = 2048
FA_SC2 = 3072
FA_XO = 4096
FA_W = 12288

KP_A = 0
KP_WF = 2048
KP_TOK = 2176
KP_GAM = 2192
KP_BET = 2200
KP_B1 = 2208
KP_B2 = 2240
KP_RANK = 2248
KP_SGN = 2249
KP_EPS = 2252
KP_CNTR = 2256
KP_OFFS = 2320
KP_I64B = 2384
KP_I8B = 3408
KP_BIAS = 3536
KP_ONEST = 4560
KP_UT = 4688
KP_IDF = 4816
KP_ONE = 4944
KP_ONE1 = 4945
KP_GWT = 5120
KP_MAPF = 5680
KP_MAPF = 5680 if 5680 + 128 <= 6144 else 0
KP_W = 6144

KPB_SIGNS = 0
KPB_ID = 2048
KPB_W = 3072

KPU_PG = 0
KPU_MAPIO = 8
KPU_SCAT = 16
KPU_YIDX = 144
KPU_MAPU = 272
KPU_W = 384

B0_KA = 0
B0_QA = 8192
B0_V = 12288
B0_EXP = 20608
B0_W = 23680
BF_U1 = 0
BF_V2 = 2048
BF_V1H = 4096
BF_U2H = 12288
BF_P = 20480
BF_HH = 21504
BG_SEL = 0
BG_TOK = 8192
B1_K = 0
B1_V = 8192
B1_Q = 16512
B1_EXP = 18560
B1_WKV = 21632
B1_WQ = 25728
B1_AST = 27776
B1_Q2 = 28288
BW_A = 0
BW_WP = 4096
BM_BUFT = 0
BM_GT = 8192
BM_M = 10240
BM_MT = 11264
BM_WG = 15360
BM_WU = 19456
BM_WD = 23552
BM_Y = 27648
BM_SIL = 28672
BY_Y = 0
BB_W = 32768

BA_W = ECH * S


class Ctx:
    def __init__(self, nc, sems, eng_name, eng):
        self.nc = nc
        self.sems = sems
        self.eng_name = eng_name
        self.eng = eng
        self.cnt = {k: 0 for k in SEM_ENG}
        self.barv = 0

    def _emit(self, sem, fn, n):
        if self.eng_name == SEM_ENG[sem]:
            inst = fn(self.eng)
            if n:
                inst.then_inc(self.sems[sem], n)
        if n:
            self.cnt[sem] += n
        return self.cnt[sem]

    def dma(self, q, fn):
        return self._emit(q, fn, 16)

    def cop(self, e, fn, inc=True):
        return self._emit(e, fn, 1 if inc else 0)

    def coll(self, fn):
        return self._emit("c", fn, 1)

    def W(self, waiter_eng, sem, val=None):
        if val is None:
            val = self.cnt[sem]
        if val <= 0:
            return
        if self.eng_name == waiter_eng:
            self.eng.wait_ge(self.sems[sem], val)

    def bar(self):
        self.barv += 5
        if self.eng is not None:
            self.eng.sem_inc(self.sems["b"], 1)
            self.eng.wait_ge(self.sems["b"], self.barv)

    def sync_all(self):
        for en in ("sync", "vector", "scalar", "tensor", "gpsimd"):
            for sname in ("d", "g", "g2", "d2", "v", "s", "t", "q", "c"):
                self.W(en, sname)
        self.bar()

    def mark(self, sem):
        return self.cnt[sem]


DBG_SPECS = {
    "xn0": ((E, S), "f32"),
    "s0": ((E, T), "f32"),
    "x1": ((E, T), "f32"),
    "gg": ((E, T), "f32"),
    "rows": ((2, 2048), "f32"),
    "logits": ((128, 1024), "f32"),
    "gate": ((128, 4096), "f32"),
    "slots": ((128, 640), "f32"),
    "map": ((NEXP * CAP, 1), "f32"),
    "cnt": ((8, 64), "f32"),
    "xnf1": ((E, T), "f32"),
    "a1": ((E, T), "f32"),
    "s1": ((E, T), "f32"),
    "y": ((NEXP * CAP, E), "f32"),
    "xn1": ((E, S), "f32"),
}




def ph_ln_stats_qc(c, tn, src_view, qc):
    FA = tn["FA"]; KP = tn["KP"]
    psD = tn["psD"]; psE = tn["psE"]
    ones = KP[:, KP_ONE:KP_ONE + 1]
    src = src_view[:, :, qc * QCW:(qc + 1) * QCW]
    c.W("sync", "v")
    c.W("sync", "s")
    c.W("sync", "t")
    dm = c.dma("d", lambda e: e.dma_start(
        out=FA[:, FA_STREAM:FA_STREAM + 8 * QCW].rearrange("p (a t) -> p a t", a=8), in_=src))
    c.W("scalar", "d", dm)
    c.W("tensor", "d", dm)
    smm = [0] * ECH
    for ec in range(ECH):
        xc = FA[:, FA_STREAM + ec * QCW: FA_STREAM + (ec + 1) * QCW]
        sq = FA[:, FA_SQ + (ec % 2) * QCW: FA_SQ + (ec % 2 + 1) * QCW]
        if ec >= 2:
            c.W("scalar", "t", smm[ec - 2])
        sqm = c.cop("s", lambda e, xc=xc, sq=sq: e.activation(sq, xc, AF.Square))
        c.cop("t", lambda e, xc=xc, ec=ec: e.matmul(
            psD[0:1, 0:QCW], ones, xc, start=(ec == 0), stop=(ec == ECH - 1)), inc=False)
        c.W("tensor", "s", sqm)
        smm[ec] = c.cop("t", lambda e, sq=sq, ec=ec: e.matmul(
            psE[0:1, 0:QCW], ones, sq, start=(ec == 0), stop=(ec == ECH - 1)))


def ph_ln_rows(c, tn, qc):
    KP = tn["KP"]; FA = tn["FA"]; psD = tn["psD"]; psE = tn["psE"]
    arow = KP[0:1, KP_A + qc * QCW: KP_A + (qc + 1) * QCW]
    brow = KP[32:33, KP_A + qc * QCW: KP_A + (qc + 1) * QCW]
    mrow = FA[0:1, FA_ROWS: FA_ROWS + QCW]
    m2row = FA[0:1, FA_ROWS + QCW: FA_ROWS + 2 * QCW]
    c.W("vector", "t")
    v = c.cop("v", lambda e: e.tensor_scalar(mrow, psD[0:1, 0:QCW], 1.0 / E, None, OP.mult))
    c.W("vector", "v", v)
    v = c.cop("v", lambda e: e.tensor_tensor(m2row, mrow, mrow, op=OP.mult))
    c.W("vector", "v", v)
    v = c.cop("v", lambda e: e.tensor_scalar(m2row, m2row, -1.0, None, OP.mult))
    c.W("vector", "v", v)
    v = c.cop("v", lambda e: e.scalar_tensor_tensor(m2row, psE[0:1, 0:QCW], 1.0 / E, m2row, OP.mult, OP.add))
    c.W("scalar", "v", v)
    s = c.cop("s", lambda e: e.activation(m2row, m2row, AF.Ln, bias=KP[0:1, KP_EPS:KP_EPS + 1]))
    c.W("scalar", "s", s)
    s = c.cop("s", lambda e: e.activation(arow, m2row, AF.Exp, scale=-0.5))
    c.W("vector", "s", s)
    v = c.cop("v", lambda e: e.scalar_tensor_tensor(brow, mrow, -1.0, arow, OP.mult, OP.mult))
    c.W("vector", "v", v)
    return arow, brow


def ph_bcast_rows(c, tn, arow, brow, psX, psY):
    KP = tn["KP"]
    ones1 = KP[0:1, KP_ONE1:KP_ONE1 + 128]
    ones1b = KP[32:33, KP_ONE1:KP_ONE1 + 128]
    c.W("tensor", "v")
    c.W("tensor", "s")
    c.cop("t", lambda e: e.matmul(psX, ones1, arow, start=True, stop=True), inc=False)
    return c.cop("t", lambda e: e.matmul(psY, ones1b, brow, start=True, stop=True))


def ph_exchange(c, tn):
    BA = tn["BA"]; KPU = tn["KPU"]
    d_ag_in = tn["d_ag_in"]; d_ag_out = tn["d_ag_out"]
    dm = c.dma("d", lambda e: e.dma_start(
        out=d_ag_in.ap().rearrange("(a p) t -> p a t", p=128),
        in_=BA[:, 0:BA_W].rearrange("p (a k) -> p a k", a=8)[:, :, 0:T]))
    c.W("gpsimd", "d", dm)
    cm = c.coll(lambda e: e.collective_compute(
        "AllGather", mybir.AluOpType.bypass, replica_groups=tn["pairs"],
        ins=[d_ag_in.ap()], outs=[d_ag_out.ap()]))
    c.W("gpsimd", "c", cm)
    for ec in range(ECH):
        c.dma("g", lambda e, ec=ec: e.indirect_dma_start(
            out=BA[:, ec * S + T: ec * S + 2 * T], out_offset=None,
            in_=d_ag_out.ap(),
            in_offset=IndirectOffsetOnAxis(ap=KPU[:, KPU_PG + ec:KPU_PG + ec + 1], axis=0)))
    c.sync_all()


def ph_attn_core(c, tn, qc, lhsT_fn, rhs_fn, vslot_fn, scale_fn, straddle_fn,
                 s_dst, stage_bf=None):
    BB = tn["BB"]; FA = tn["FA"]; KP = tn["KP"]; KPB = tn["KPB"]
    psA = tn["psA"]; psB = tn["psB"]; psD = tn["psD"]; psE = tn["psE"]
    NG = 16
    ROT = 2
    psmap = [psA, psB]
    expslot = tn["exp_base"]
    expm = [0] * NG
    avm = [0] * NG
    eslots = [BB[:, expslot + k * 1024: expslot + (k + 1) * 1024] for k in range(ROT)]
    rhs = rhs_fn(qc)

    def emit_av(g):
        eslot = eslots[g % ROT]
        c.W("tensor", "s", expm[g])
        for i in range(2):
            kt = g * 2 + i
            vsl = vslot_fn(kt)
            c.cop("t", lambda e, vsl=vsl, eslot=eslot, i=i, g=g: e.matmul(
                psD[0:65, 0:512], vsl, eslot[:, i * 512:(i + 1) * 512],
                start=(g == 0 and i == 0), stop=(g == NG - 1 and i == 1)), inc=(i == 1))
        avm[g] = c.mark("t")

    for g in range(NG):
        ps = psmap[g % ROT]
        if g >= ROT:
            c.W("tensor", "s", expm[g - ROT])
        for i in range(2):
            kt = g * 2 + i
            lt = lhsT_fn(kt)
            c.cop("t", lambda e, ps=ps, i=i, lt=lt, rhs=rhs: e.matmul(
                ps[:, i * 512:(i + 1) * 512], lt, rhs,
                start=True, stop=True), inc=(i == 1))
        scm = c.mark("t")
        scale = scale_fn(g)
        sub = straddle_fn(g) if straddle_fn else None
        if sub is not None:
            c.W("vector", "t", scm)
            vs = c.cop("v", lambda e, ps=ps, sub=sub: e.tensor_tensor(
                ps[:, 0:1024], ps[:, 0:1024],
                KPB[:, KPB_SIGNS + sub * 1024:KPB_SIGNS + (sub + 1) * 1024], op=OP.mult))
            c.W("scalar", "v", vs)
            scale = 1.0
        c.W("scalar", "t", scm)
        if g >= ROT:
            c.W("scalar", "t", avm[g - ROT])
        eslot = eslots[g % ROT]
        expm[g] = c.cop("s", lambda e, ps=ps, eslot=eslot, scale=scale: e.activation(
            eslot, ps[:, 0:1024], AF.Exp, scale=scale))
        if g >= 1:
            emit_av(g - 1)
    emit_av(NG - 1)
    c.W("vector", "t", avm[NG - 1])
    c.W("vector", "d")
    pv = c.cop("v", lambda e: e.tensor_copy(FA[0:64, FA_PEV:FA_PEV + 512], psD[0:64, 0:512]))
    rv = c.cop("v", lambda e: e.reciprocal(FA[0:1, FA_REC:FA_REC + 512], psD[64:65, 0:512]))
    c.W("tensor", "v", rv)
    bm = c.cop("t", lambda e: e.matmul(
        psE[0:64, 0:512], KP[0:1, KP_ONE1:KP_ONE1 + 64],
        FA[0:1, FA_REC:FA_REC + 512], start=True, stop=True))
    c.W("vector", "t", bm)
    src_out = stage_bf if stage_bf is not None else FA[0:64, FA_PEV:FA_PEV + 512]
    sv = c.cop("v", lambda e, src_out=src_out: e.tensor_tensor(
        src_out, FA[0:64, FA_PEV:FA_PEV + 512], psE[0:64, 0:512], op=OP.mult))
    c.W("sync", "v", sv)
    dm = c.dma("d", lambda e, src_out=src_out: e.dma_start(out=s_dst, in_=src_out))
    c.W("tensor", "v", sv)
    c.W("sync", "d", dm)




def ph_attn_core_packed(c, tn, qc, ga, gb, a_dst_a, a_dst_b):
    BB = tn["BB"]; FA = tn["FA"]; KP = tn["KP"]
    psA = tn["psA"]; psB = tn["psB"]; psCx = tn["psC"]; psD = tn["psD"]; psE = tn["psE"]
    NG = 32
    ROT = 3
    psmap = [psA, psB, psCx]
    expslot = tn["exp_base"]
    expm = [0] * NG
    avm = [0] * NG
    eslots = [BB[:, expslot + k * 1024: expslot + (k + 1) * 1024] for k in range(ROT)]
    rhs_a = BB[0:64, B1_Q + qc * 512:B1_Q + (qc + 1) * 512]
    rhs_b = BB[64:128, B1_Q2 + qc * 512:B1_Q2 + (qc + 1) * 512]

    def lhsT_k(kt, g4):
        base = B1_K + (g4 // 2) * 4096 + kt * 128
        return BB[(g4 % 2) * 64:(g4 % 2) * 64 + 64, base:base + 128]

    def vsl(kt, g4):
        return BB[:, B1_V + kt * 260 + g4 * 65: B1_V + kt * 260 + g4 * 65 + 65]

    def emit_av(g):
        eslot = eslots[g % ROT]
        c.W("tensor", "s", expm[g])
        c.cop("t", lambda e, g=g, eslot=eslot: e.matmul(
            psD[0:65, 0:512], vsl(g, ga), eslot[:, 0:512],
            start=(g == 0), stop=(g == NG - 1)), inc=False)
        c.cop("t", lambda e, g=g, eslot=eslot: e.matmul(
            psE[0:65, 0:512], vsl(g, gb), eslot[:, 512:1024],
            start=(g == 0), stop=(g == NG - 1)), inc=True)
        avm[g] = c.mark("t")

    for g in range(NG):
        ps = psmap[g % ROT]
        if g >= ROT:
            c.W("tensor", "s", expm[g - ROT])
        c.cop("t", lambda e, g=g, ps=ps: e.matmul(
            ps[:, 0:512], lhsT_k(g, ga), rhs_a, start=True, stop=True), inc=False)
        c.cop("t", lambda e, g=g, ps=ps: e.matmul(
            ps[:, 512:1024], lhsT_k(g, gb), rhs_b, start=True, stop=True), inc=True)
        scm = c.mark("t")
        c.W("scalar", "t", scm)
        if g >= ROT:
            c.W("scalar", "t", avm[g - ROT])
        eslot = eslots[g % ROT]
        expm[g] = c.cop("s", lambda e, ps=ps, eslot=eslot: e.activation(
            eslot, ps[:, 0:1024], AF.Exp, scale=0.125))
        if g >= 1:
            emit_av(g - 1)
    emit_av(NG - 1)
    c.W("vector", "t", avm[NG - 1])
    c.W("vector", "d")
    for (psP, dst, pev, rec, bcol, prow) in (
            (psD, a_dst_a, 0, 512, 0, 0),
            (psE, a_dst_b, 1024, 1536, 512, 64)):
        pv = c.cop("v", lambda e, psP=psP, pev=pev: e.tensor_copy(
            FA[0:64, pev:pev + 512], psP[0:64, 0:512]))
        rv = c.cop("v", lambda e, psP=psP, rec=rec: e.reciprocal(
            FA[0:1, rec:rec + 512], psP[64:65, 0:512]))
        c.W("tensor", "v", rv)
        bm = c.cop("t", lambda e, rec=rec, bcol=bcol: e.matmul(
            psA[0:64, bcol:bcol + 512], KP[0:1, KP_ONE1:KP_ONE1 + 64],
            FA[0:1, rec:rec + 512], start=True, stop=True))
        c.W("vector", "t", bm)
        stage = BB[prow:prow + 64, B1_AST:B1_AST + 512]
        sv = c.cop("v", lambda e, stage=stage, pev=pev, bcol=bcol: e.tensor_tensor(
            stage, FA[0:64, pev:pev + 512], psA[0:64, bcol:bcol + 512], op=OP.mult))
        c.W("sync", "v", sv)
        dm = c.dma("d", lambda e, dst=dst, stage=stage: e.dma_start(out=dst, in_=stage))
    c.W("tensor", "v")
    c.W("sync", "d")



def ph_l0(c, tn, dbg):
    I = tn["I"]; BA = tn["BA"]; BB = tn["BB"]; FA = tn["FA"]; KP = tn["KP"]
    psA = tn["psA"]; psB = tn["psB"]; psC = tn["psC"]; psD = tn["psD"]; psE = tn["psE"]
    x_view = I["x_T"].rearrange("(a p) t -> p a t", p=128)

    for qc in range(NQC):
        ph_ln_stats_qc(c, tn, x_view, qc)
        arow, brow = ph_ln_rows(c, tn, qc)
        bm = ph_bcast_rows(c, tn, arow, brow, psA[:, 0:QCW], psB[:, 0:QCW])
        c.W("vector", "t", bm)
        c.W("vector", "s")
        for ec in range(ECH):
            xc = FA[:, FA_STREAM + ec * QCW: FA_STREAM + (ec + 1) * QCW]
            tmp = FA[:, FA_SQ: FA_SQ + QCW]
            dst = BA[:, ec * S + qc * QCW: ec * S + (qc + 1) * QCW]
            v = c.cop("v", lambda e, xc=xc, tmp=tmp: e.tensor_tensor(tmp, xc, psA[:, 0:QCW], op=OP.mult))
            c.W("vector", "v", v)
            v = c.cop("v", lambda e, tmp=tmp, dst=dst: e.tensor_tensor(dst, tmp, psB[:, 0:QCW], op=OP.add))
            c.W("vector", "v", v)
    c.sync_all()

    ph_exchange(c, tn)
    if "xn0" in dbg:
        for ec in range(ECH):
            c.dma("g", lambda e, ec=ec: e.dma_start(
                out=tn["DBG"]["xn0"][ec * 128:(ec + 1) * 128, :], in_=BA[:, ec * S:(ec + 1) * S]))
        c.sync_all()

    xn_ch = [BA[:, ec * S:(ec + 1) * S] for ec in range(ECH)]
    iq = FA[:, FA_IQ:FA_IQ + T]
    ik2 = FA[:, FA_IK2:FA_IK2 + S]
    sgncol = KP[:, KP_SGN:KP_SGN + 1]
    tn["exp_base"] = B0_EXP
    WSL = B0_W
    wdm = {}
    pjm = {}
    augm = {}
    vprm = {}
    attnm = {}
    qmm = {}

    def emit_wdma(h):
        if h >= H:
            return
        par = h % 2
        gq = "g" if par == 0 else "g2"
        wqd = BB[:, WSL + par * 2048: WSL + par * 2048 + 1024]
        wkd = BB[:, WSL + par * 2048 + 1024: WSL + par * 2048 + 2048]
        c.W("gpsimd", "t", pjm.get(h - 2, 0))
        srcq = I["WqTd"].rearrange("(a p) o -> p a o", p=128)[:, :, h * 128:(h + 1) * 128]
        srck = I["WkTd"].rearrange("(a p) o -> p a o", p=128)[:, :, h * 128:(h + 1) * 128]
        c.dma(gq, lambda e, srcq=srcq, wqd=wqd: e.dma_start(
            out=wqd.rearrange("p (a o) -> p a o", a=8), in_=srcq))
        c.dma(gq, lambda e, srck=srck, wkd=wkd: e.dma_start(
            out=wkd.rearrange("p (a o) -> p a o", a=8), in_=srck))
        if par == 0:
            hp = h // 2
            wv = BB[:, WSL + 4096 + (hp % 2) * 1024: WSL + 4096 + (hp % 2) * 1024 + 1024]
            srcv = I["WvT0"].rearrange("(a p) o -> p a o", p=128)[:, :, hp * 128:(hp + 1) * 128]
            c.dma(gq, lambda e, srcv=srcv, wv=wv: e.dma_start(
                out=wv.rearrange("p (a o) -> p a o", a=8), in_=srcv))
        wdm[h] = c.mark(gq)

    def emit_proj(h):
        if h >= H:
            return
        par = h % 2
        hp = h // 2
        wqd = BB[:, WSL + par * 2048: WSL + par * 2048 + 1024]
        wkd = BB[:, WSL + par * 2048 + 1024: WSL + par * 2048 + 2048]
        ka = BB[:, B0_KA + par * 4096: B0_KA + par * 4096 + S]
        qa = BB[:, B0_QA + par * 2048: B0_QA + par * 2048 + T]
        c.W("tensor", "g" if h % 2 == 0 else "g2", wdm[h])
        c.W("tensor", "v", augm.get(h - 2, 0))
        c.W("vector", "t", attnm.get(h - 2, 0))
        c.W("tensor", "v", qmm.get(h - 1, 0))
        if (h - 1) >= 0 and (h - 1) % 2 == 0:
            c.W("tensor", "v", vprm.get((h - 1) // 2, 0))
        evm = [0] * 12
        for kc in range(8):
            psk = psC[:, (kc % 2) * 512:(kc % 2 + 1) * 512]
            if kc >= 2:
                c.W("tensor", "v", evm[kc - 2])
            for ei in range(ECH):
                c.cop("t", lambda e, kc=kc, ei=ei, wkd=wkd, psk=psk: e.matmul(
                    psk, wkd[:, ei * 128:(ei + 1) * 128],
                    xn_ch[ei][:, kc * 512:(kc + 1) * 512],
                    start=(ei == 0), stop=(ei == ECH - 1)), inc=(ei == ECH - 1))
            tmk = c.mark("t")
            c.W("vector", "t", tmk)
            evm[kc] = c.cop("v", lambda e, kc=kc, ka=ka, psk=psk: e.tensor_copy(
                ka[:, kc * 512:(kc + 1) * 512], psk))
        for qcc in range(NQC):
            psk = psC[:, (qcc % 2) * 512:(qcc % 2 + 1) * 512]
            c.W("tensor", "v", evm[8 + qcc - 2] if qcc >= 2 else evm[6 + qcc])
            for ei in range(ECH):
                c.cop("t", lambda e, qcc=qcc, ei=ei, wqd=wqd, psk=psk: e.matmul(
                    psk, wqd[:, ei * 128:(ei + 1) * 128],
                    xn_ch[ei][:, qcc * 512:(qcc + 1) * 512],
                    start=(ei == 0), stop=(ei == ECH - 1)), inc=(ei == ECH - 1))
            tmk = c.mark("t")
            c.W("vector", "t", tmk)
            evm[8 + qcc] = c.cop("v", lambda e, qcc=qcc, qa=qa, psk=psk: e.tensor_copy(
                qa[:, qcc * 512:(qcc + 1) * 512], psk))
        if par == 0:
            wv = BB[:, WSL + 4096 + (hp % 2) * 1024: WSL + 4096 + (hp % 2) * 1024 + 1024]
            vbase = B0_V + (hp % 2) * 4160
            c.W("gpsimd", "t", attnm.get(2 * (hp - 2) + 1, 0))
            qm = c.cop("q", lambda e, vbase=vbase: e.memset(BB[:, vbase:vbase + 32 * 130], 1.0))
            c.W("tensor", "q", qm)
            evv = [0] * NKT
            for kt in range(NKT):
                psk = psC[:, (kt % 2) * 512:(kt % 2) * 512 + 128]
                if kt >= 2:
                    c.W("tensor", "v", evv[kt - 2])
                else:
                    c.W("tensor", "v", evm[10 + kt])
                for ei in range(ECH):
                    c.cop("t", lambda e, kt=kt, ei=ei, wv=wv, psk=psk: e.matmul(
                        psk, xn_ch[ei][:, kt * 128:(kt + 1) * 128],
                        wv[:, ei * 128:(ei + 1) * 128],
                        start=(ei == 0), stop=(ei == ECH - 1)), inc=(ei == ECH - 1))
                tmk = c.mark("t")
                c.W("vector", "t", tmk)
                c.W("vector", "q", qm)
                c.cop("v", lambda e, kt=kt, psk=psk, vbase=vbase: e.tensor_copy(
                    BB[:, vbase + kt * 130:vbase + kt * 130 + 64], psk[0:128, 0:64]))
                evv[kt] = c.cop("v", lambda e, kt=kt, psk=psk, vbase=vbase: e.tensor_copy(
                    BB[:, vbase + kt * 130 + 65:vbase + kt * 130 + 129], psk[0:128, 64:128]))
            vprm[hp] = c.mark("v")
        pjm[h] = c.mark("t")
        qmm[h] = c.mark("v")

    def emit_aug(h):
        if h >= H:
            return
        par = h % 2
        ka = BB[:, B0_KA + par * 4096: B0_KA + par * 4096 + S]
        qa = BB[:, B0_QA + par * 2048: B0_QA + par * 2048 + T]
        v = c.cop("v", lambda e, ka=ka: e.tensor_scalar(
            ka[0:64, T:S], ka[0:64, T:S], sgncol[0:64, :], None, OP.mult))
        c.W("vector", "v", v)
        v = c.cop("v", lambda e, ka=ka: e.tensor_tensor(
            ka[64:128, 0:S], ka[64:128, 0:S], ik2[64:128, :], op=OP.mult))
        c.W("vector", "v", v)
        v = c.cop("v", lambda e, qa=qa: e.tensor_tensor(
            qa[0:64, 0:T], qa[0:64, 0:T], iq[0:64, :], op=OP.mult))
        c.W("vector", "v", v)
        v = c.cop("v", lambda e, qa=qa: e.tensor_scalar(
            qa[64:128, 0:T], qa[64:128, 0:T], -1.0, None, OP.mult))
        augm[h] = c.mark("v")

    def emit_attn(h):
        par = h % 2
        hp = h // 2
        ka = BB[:, B0_KA + par * 4096: B0_KA + par * 4096 + S]
        qa = BB[:, B0_QA + par * 2048: B0_QA + par * 2048 + T]
        vbase = B0_V + (hp % 2) * 4160
        c.W("tensor", "v", augm[h])
        c.W("tensor", "v", vprm[hp])
        c.W("scalar", "v", augm[h])

        def vsl0(kt, h=h, vbase=vbase):
            return BB[:, vbase + kt * 130 + (h % 2) * 65: vbase + kt * 130 + (h % 2) * 65 + 65]

        def lhsT0(kt, ka=ka):
            return ka[:, kt * 128:(kt + 1) * 128]

        def rhs0(qcc, qa=qa):
            return qa[:, qcc * 512:(qcc + 1) * 512]

        for qcc in range(NQC):
            def scale0(g, qcc=qcc):
                if g >= 8:
                    return 1.0
                return 1.0 if g < qcc * 2 else -1.0

            def straddle0(g, qcc=qcc):
                if g == qcc * 2:
                    return 0
                if g == qcc * 2 + 1:
                    return 1
                return None

            s_dst = tn["d_s"].ap()[h * 64:(h + 1) * 64, qcc * 512:(qcc + 1) * 512]
            ph_attn_core(c, tn, qcc, lhsT0, rhs0, vsl0, scale0, straddle0, s_dst)
        attnm[h] = c.mark("t")

    emit_wdma(0)
    emit_wdma(1)
    emit_proj(0)
    emit_aug(0)
    for h in range(H):
        emit_wdma(h + 2)
        emit_proj(h + 1)
        emit_aug(h + 1)
        emit_attn(h)
    c.sync_all()
    if "s0" in dbg:
        c.dma("d", lambda e: e.dma_start(out=tn["DBG"]["s0"], in_=tn["d_s"].ap()))
        c.sync_all()

    s_view = tn["d_s"].ap().rearrange("(a p) t -> p a t", p=128)
    for qc in range(NQC):
        ph_ln_stats_qc(c, tn, s_view, qc)
        ph_ln_rows(c, tn, qc)
    c.sync_all()

    if "rows" in dbg:
        c.dma("d", lambda e: e.dma_start(out=tn["DBG"]["rows"][0:1, :], in_=KP[0:1, KP_A:KP_A + 2048]))
        c.dma("d", lambda e: e.dma_start(out=tn["DBG"]["rows"][1:2, :], in_=KP[32:33, KP_A:KP_A + 2048]))
        c.sync_all()

    src = I["U1"].rearrange("(a p) o -> p a o", p=128)
    c.dma("g", lambda e, src=src: e.dma_start(
        out=BB[:, BF_U1:BF_U1 + 2048].rearrange("p (a o) -> p a o", a=8), in_=src))
    src = I["V2"].rearrange("(a p) o -> p a o", p=128)
    c.dma("g", lambda e, src=src: e.dma_start(
        out=BB[:, BF_V2:BF_V2 + 2048].rearrange("p (a o) -> p a o", a=2), in_=src))
    for half in range(2):
        src = I["V1"].rearrange("(a p) o -> p a o", p=128)[:, :, half * 2048:(half + 1) * 2048]
        c.dma("g", lambda e, src=src, half=half: e.dma_start(
            out=BB[:, BF_V1H + half * 4096:BF_V1H + (half + 1) * 4096].rearrange(
                "p (a o) -> p a o", a=2), in_=src))
        src = I["U2"].rearrange("(a p) o -> p a o", p=128)[:, half * 16:(half + 1) * 16, :]
        c.dma("g", lambda e, src=src, half=half: e.dma_start(
            out=BB[:, BF_U2H + half * 4096:BF_U2H + (half + 1) * 4096].rearrange(
                "p (a o) -> p a o", a=16), in_=src))
    gm_w = c.mark("g")
    c.W("tensor", "g", gm_w)
    c.W("scalar", "g", gm_w)

    x1_view = tn["d_x1"].ap().rearrange("(a p) t -> p a t", p=128)
    for qc in range(NQC):
        dxm = c.dma("d", lambda e, qc=qc: e.dma_start(
            out=FA[:, FA_X:FA_X + 8 * QCW].rearrange("p (a t) -> p a t", a=8),
            in_=x_view[:, :, qc * QCW:(qc + 1) * QCW]))
        dsm = c.dma("d", lambda e, qc=qc: e.dma_start(
            out=FA[:, FA_SS:FA_SS + 8 * QCW].rearrange("p (a t) -> p a t", a=8),
            in_=s_view[:, :, qc * QCW:(qc + 1) * QCW]))
        c.W("tensor", "s")
        c.W("tensor", "v")
        for pc in range(2):
            for ei in range(ECH):
                c.cop("t", lambda e, pc=pc, ei=ei, qc=qc: e.matmul(
                    psA[:, pc * 512:(pc + 1) * 512],
                    BB[:, BF_U1 + ei * 256 + pc * 128:BF_U1 + ei * 256 + (pc + 1) * 128],
                    BA[:, ei * S + qc * QCW: ei * S + (qc + 1) * QCW],
                    start=(ei == 0), stop=(ei == ECH - 1)), inc=(ei == ECH - 1))
        tm = c.mark("t")
        c.W("vector", "t", tm)
        vm = c.cop("v", lambda e: e.tensor_copy(BB[:, BF_P:BF_P + 1024], psA[:, 0:1024]))
        c.W("tensor", "v", vm)
        gel = [0] * 32

        def emit_q2(fg):
            half, f = fg // 16, fg % 16
            c.W("tensor", "s", gel[fg])
            for pc in range(2):
                psQ = psC if pc == 0 else psD
                c.cop("t", lambda e, half=half, f=f, fg=fg, pc=pc, psQ=psQ: e.matmul(
                    psQ[:, 0:512],
                    BB[:, BF_U2H + half * 4096 + f * 256 + pc * 128:
                       BF_U2H + half * 4096 + f * 256 + (pc + 1) * 128],
                    BB[:, BF_HH + (fg % 16) * 512:BF_HH + ((fg % 16) + 1) * 512],
                    start=(fg == 0), stop=(fg == 31)), inc=(pc == 1))

        for half in range(2):
            for f in range(16):
                fg = half * 16 + f
                psH = psB[:, (fg % 2) * 512:(fg % 2 + 1) * 512]
                if fg >= 2:
                    c.W("tensor", "s", gel[fg - 2])
                for pc in range(2):
                    c.cop("t", lambda e, half=half, f=f, pc=pc, psH=psH: e.matmul(
                        psH,
                        BB[:, BF_V1H + half * 4096 + pc * 2048 + f * 128:
                           BF_V1H + half * 4096 + pc * 2048 + (f + 1) * 128],
                        BB[:, BF_P + pc * 512:BF_P + (pc + 1) * 512],
                        start=(pc == 0), stop=(pc == 1)), inc=(pc == 1))
                tmH = c.mark("t")
                c.W("scalar", "t", tmH)
                gel[fg] = c.cop("s", lambda e, f=f, fg=fg, psH=psH: e.activation(
                    BB[:, BF_HH + (fg % 16) * 512:BF_HH + ((fg % 16) + 1) * 512], psH,
                    AF.Gelu_apprx_tanh, bias=KP[:, KP_B1 + fg:KP_B1 + fg + 1]))
                if fg >= 1:
                    emit_q2(fg - 1)
        emit_q2(31)
        tm = c.mark("t")
        c.W("vector", "t", tm)
        vm = c.cop("v", lambda e: e.tensor_copy(BB[:, BF_P:BF_P + 512], psC[:, 0:512]))
        c.W("vector", "v", vm)
        vm = c.cop("v", lambda e: e.tensor_copy(BB[:, BF_P + 512:BF_P + 1024], psD[:, 0:512]))
        c.W("tensor", "v", vm)
        silm = [0] * ECH
        for oc in range(ECH):
            psO = psE[:, 0:512]
            if oc >= 1:
                c.W("tensor", "s", silm[oc - 1])
            for pc in range(2):
                c.cop("t", lambda e, oc=oc, pc=pc, psO=psO: e.matmul(
                    psO,
                    BB[:, BF_V2 + pc * 1024 + oc * 128:BF_V2 + pc * 1024 + (oc + 1) * 128],
                    BB[:, BF_P + pc * 512:BF_P + (pc + 1) * 512],
                    start=(pc == 0), stop=(pc == 1)), inc=(pc == 1))
            tmO = c.mark("t")
            c.W("scalar", "t", tmO)
            silm[oc] = c.cop("s", lambda e, oc=oc, psO=psO: e.activation(
                FA[:, FA_G + oc * 512:FA_G + (oc + 1) * 512], psO, AF.Silu,
                bias=KP[:, KP_B2 + oc:KP_B2 + oc + 1]))
        sm_g = c.mark("s")
        if "gg" in dbg:
            c.W("sync", "s", sm_g)
            c.dma("d", lambda e, qc=qc: e.dma_start(
                out=tn["DBG"]["gg"].rearrange("(a p) t -> p a t", p=128)[:, :, qc * QCW:(qc + 1) * QCW],
                in_=FA[:, FA_G:FA_G + 8 * QCW].rearrange("p (a t) -> p a t", a=8)))
        arow = KP[0:1, KP_A + qc * QCW: KP_A + (qc + 1) * QCW]
        brow = KP[32:33, KP_A + qc * QCW: KP_A + (qc + 1) * QCW]
        bm = ph_bcast_rows(c, tn, arow, brow, psA[:, 0:QCW], psB[:, 0:QCW])
        c.W("vector", "t", bm)
        c.W("vector", "s", sm_g)
        c.W("vector", "d")
        for ec in range(ECH):
            sc = FA[:, FA_SS + ec * QCW: FA_SS + (ec + 1) * QCW]
            xc = FA[:, FA_X + ec * QCW: FA_X + (ec + 1) * QCW]
            gch = FA[:, FA_G + ec * QCW: FA_G + (ec + 1) * QCW]
            v = c.cop("v", lambda e, sc=sc: e.tensor_tensor(sc, sc, psA[:, 0:QCW], op=OP.mult))
            c.W("vector", "v", v)
            v = c.cop("v", lambda e, sc=sc: e.tensor_tensor(sc, sc, psB[:, 0:QCW], op=OP.add))
            c.W("vector", "v", v)
            v = c.cop("v", lambda e, sc=sc, ec=ec: e.tensor_scalar(
                sc, sc, KP[:, KP_GAM + ec:KP_GAM + ec + 1], KP[:, KP_BET + ec:KP_BET + ec + 1],
                OP.mult, OP.add))
            c.W("vector", "v", v)
            v = c.cop("v", lambda e, sc=sc, gch=gch: e.tensor_tensor(sc, sc, gch, op=OP.mult))
            c.W("vector", "v", v)
            v = c.cop("v", lambda e, sc=sc, xc=xc: e.tensor_tensor(xc, xc, sc, op=OP.add))
            c.W("vector", "v", v)
        vm = c.mark("v")
        c.W("sync", "v", vm)
        c.dma("d", lambda e, qc=qc: e.dma_start(
            out=x1_view[:, :, qc * QCW:(qc + 1) * QCW],
            in_=FA[:, FA_X:FA_X + 8 * QCW].rearrange("p (a t) -> p a t", a=8)))
        c.W("sync", "d")
    c.sync_all()
    if "x1" in dbg:
        c.dma("d", lambda e: e.dma_start(out=tn["DBG"]["x1"], in_=tn["d_x1"].ap()))
        c.sync_all()



def ph_l1(c, tn, dbg):
    I = tn["I"]; BA = tn["BA"]; BB = tn["BB"]; FA = tn["FA"]; KP = tn["KP"]
    KPB = tn["KPB"]; KPU = tn["KPU"]
    psA = tn["psA"]; psB = tn["psB"]; psC = tn["psC"]; psD = tn["psD"]; psE = tn["psE"]
    psT = psC.bitcast(BF16)
    x1_view = tn["d_x1"].ap().rearrange("(a p) t -> p a t", p=128)

    for qc in range(NQC):
        ph_ln_stats_qc(c, tn, x1_view, qc)
        ph_ln_rows(c, tn, qc)
    c.sync_all()

    xnf_view = tn["d_xnf"].ap().rearrange("(a p) t -> p a t", p=128)
    for qc in range(NQC):
        dxm = c.dma("d", lambda e, qc=qc: e.dma_start(
            out=FA[:, FA_STREAM:FA_STREAM + 8 * QCW].rearrange("p (a t) -> p a t", a=8),
            in_=x1_view[:, :, qc * QCW:(qc + 1) * QCW]))
        arow = KP[0:1, KP_A + qc * QCW: KP_A + (qc + 1) * QCW]
        brow = KP[32:33, KP_A + qc * QCW: KP_A + (qc + 1) * QCW]
        c.W("tensor", "v")
        bm = ph_bcast_rows(c, tn, arow, brow, psC[:, 0:QCW], psE[:, 0:QCW])
        c.W("vector", "t", bm)
        c.W("vector", "d", dxm)
        for ec in range(ECH):
            xc = FA[:, FA_STREAM + ec * QCW: FA_STREAM + (ec + 1) * QCW]
            xnf = FA[:, FA_SS + ec * QCW: FA_SS + (ec + 1) * QCW]
            dstb = BA[:, ec * S + qc * QCW: ec * S + (qc + 1) * QCW]
            v = c.cop("v", lambda e, xc=xc, xnf=xnf: e.tensor_tensor(xnf, xc, psC[:, 0:QCW], op=OP.mult))
            c.W("vector", "v", v)
            v = c.cop("v", lambda e, xnf=xnf: e.tensor_tensor(xnf, xnf, psE[:, 0:QCW], op=OP.add))
            c.W("vector", "v", v)
            v = c.cop("v", lambda e, xnf=xnf, dstb=dstb: e.tensor_copy(dstb, xnf))
            c.W("tensor", "v", v)
        c.W("tensor", "v")
        for tl in range(4):
            tt = qc * 4 + tl
            for ec in range(ECH):
                xnf = FA[:, FA_SS + ec * QCW: FA_SS + (ec + 1) * QCW]
                c.cop("t", lambda e, ec=ec, tl=tl, tt=tt, xnf=xnf: e.matmul(
                    psA[:, tt * 64:(tt + 1) * 64],
                    xnf[:, tl * 128:(tl + 1) * 128],
                    KP[:, KP_GWT + ec * 64:KP_GWT + (ec + 1) * 64],
                    start=(ec == 0), stop=(ec == ECH - 1)), inc=(ec == ECH - 1 and tl == 3))
        vm = c.mark("v")
        c.W("sync", "v", vm)
        c.dma("d", lambda e, qc=qc: e.dma_start(
            out=xnf_view[:, :, qc * QCW:(qc + 1) * QCW],
            in_=FA[:, FA_SS:FA_SS + 8 * QCW].rearrange("p (a t) -> p a t", a=8)))
        c.W("sync", "d")
        c.W("tensor", "d")
    c.sync_all()
    dm_rowb = c.dma("d", lambda e: e.dma_start(
        out=FA[:, FA_ROWB:FA_ROWB + 1024], in_=I["rowb_bc"]))
    dm = c.dma("d", lambda e: e.dma_start(
        out=tn["d_ag_in"].ap().rearrange("(a p) t -> p a t", p=128),
        in_=BA[:, 0:BA_W].rearrange("p (a k) -> p a k", a=8)[:, :, 0:T]))
    c.W("gpsimd", "d", dm)
    cm_pair = c.coll(lambda e: e.collective_compute(
        "AllGather", mybir.AluOpType.bypass, replica_groups=tn["pairs"],
        ins=[tn["d_ag_in"].ap()], outs=[tn["d_ag_out"].ap()]))
    if "xnf1" in dbg:
        c.dma("d", lambda e: e.dma_start(out=tn["DBG"]["xnf1"], in_=tn["d_xnf"].ap()))
        c.sync_all()
    if "logits" in dbg:
        v = c.cop("v", lambda e: e.tensor_copy(FA[:, FA_T1:FA_T1 + 1024], psA[:, 0:1024]))
        c.W("sync", "v", v)
        c.dma("d", lambda e: e.dma_start(out=tn["DBG"]["logits"], in_=FA[:, FA_T1:FA_T1 + 1024]))
        c.sync_all()

    scoresT = FA[:, FA_SC:FA_SC + 1024]
    sfc = FA[:, FA_SFC:FA_SFC + 1024]
    Mw = FA[:, FA_MW:FA_MW + 1024]
    selmask = FA[:, FA_SEL:FA_SEL + 1024]
    eq = FA[:, FA_EQ:FA_EQ + 1024]
    cand = FA[:, FA_CAND:FA_CAND + 1024]
    t1 = FA[:, FA_T1:FA_T1 + 1024]
    sl = FA[:, FA_SL:FA_SL + 1024]

    sm = c.cop("s", lambda e: e.activation(t1, psA[:, 0:1024], AF.Exp, scale=-1.0))
    c.W("vector", "s", sm)
    v = c.cop("v", lambda e: e.tensor_scalar(t1, t1, 1.0, None, OP.add))
    c.W("vector", "v", v)
    v = c.cop("v", lambda e: e.reciprocal(scoresT, t1))
    c.W("vector", "v", v)
    v = c.cop("v", lambda e: e.tensor_tensor(sfc, scoresT, KP[:, KP_BIAS:KP_BIAS + 1024], op=OP.add))
    c.W("vector", "v", v)
    v4 = sfc.rearrange("p (t g f w) -> p t g f w", t=16, g=8, f=4)
    h4 = sl.rearrange("p (t g f) -> p t g f", t=16, g=8)[:, :, :, 0:4]
    h4 = FA[:, FA_SL:FA_SL + 512].rearrange("p (t g f) -> p t g f", t=16, g=8)
    l4 = FA[:, FA_T1:FA_T1 + 512].rearrange("p (t g f) -> p t g f", t=16, g=8)
    v = c.cop("v", lambda e: e.tensor_tensor(h4, v4[:, :, :, :, 0:1].rearrange("p t g f w -> p t g (f w)"),
                                             v4[:, :, :, :, 1:2].rearrange("p t g f w -> p t g (f w)"), op=OP.max))
    v = c.cop("v", lambda e: e.tensor_tensor(l4, v4[:, :, :, :, 0:1].rearrange("p t g f w -> p t g (f w)"),
                                             v4[:, :, :, :, 1:2].rearrange("p t g f w -> p t g (f w)"), op=OP.min))
    c.W("vector", "v", v)
    h4v = FA[:, FA_SL:FA_SL + 512].rearrange("p (t g f w) -> p t g f w", t=16, g=8, f=2)
    l4v = FA[:, FA_T1:FA_T1 + 512].rearrange("p (t g f w) -> p t g f w", t=16, g=8, f=2)
    h2 = FA[:, FA_EQ:FA_EQ + 256].rearrange("p (t g f) -> p t g f", t=16, g=8)
    l2a = FA[:, FA_EQ + 256:FA_EQ + 512].rearrange("p (t g f) -> p t g f", t=16, g=8)
    l2b = FA[:, FA_EQ + 512:FA_EQ + 768].rearrange("p (t g f) -> p t g f", t=16, g=8)
    v = c.cop("v", lambda e: e.tensor_tensor(h2, h4v[:, :, :, :, 0:1].rearrange("p t g f w -> p t g (f w)"),
                                             h4v[:, :, :, :, 1:2].rearrange("p t g f w -> p t g (f w)"), op=OP.max))
    v = c.cop("v", lambda e: e.tensor_tensor(l2a, h4v[:, :, :, :, 0:1].rearrange("p t g f w -> p t g (f w)"),
                                             h4v[:, :, :, :, 1:2].rearrange("p t g f w -> p t g (f w)"), op=OP.min))
    v = c.cop("v", lambda e: e.tensor_tensor(l2b, l4v[:, :, :, :, 0:1].rearrange("p t g f w -> p t g (f w)"),
                                             l4v[:, :, :, :, 1:2].rearrange("p t g f w -> p t g (f w)"), op=OP.max))
    c.W("vector", "v", v)
    v = c.cop("v", lambda e: e.tensor_tensor(l2a, l2a, l2b, op=OP.max))
    c.W("vector", "v", v)
    h2v = FA[:, FA_EQ:FA_EQ + 256].rearrange("p (t g w) -> p t g w", t=16, g=8)
    l2v = FA[:, FA_EQ + 256:FA_EQ + 512].rearrange("p (t g w) -> p t g w", t=16, g=8)
    gH = FA[:, FA_CAND:FA_CAND + 128].rearrange("p (t g) -> p t g", t=16)
    gL1 = FA[:, FA_CAND + 128:FA_CAND + 256].rearrange("p (t g) -> p t g", t=16)
    gL2 = FA[:, FA_CAND + 256:FA_CAND + 384].rearrange("p (t g) -> p t g", t=16)
    v = c.cop("v", lambda e: e.tensor_tensor(gH, h2v[:, :, :, 0:1].rearrange("p t g w -> p t (g w)"),
                                             h2v[:, :, :, 1:2].rearrange("p t g w -> p t (g w)"), op=OP.max))
    v = c.cop("v", lambda e: e.tensor_tensor(gL1, h2v[:, :, :, 0:1].rearrange("p t g w -> p t (g w)"),
                                             h2v[:, :, :, 1:2].rearrange("p t g w -> p t (g w)"), op=OP.min))
    v = c.cop("v", lambda e: e.tensor_tensor(gL2, l2v[:, :, :, 0:1].rearrange("p t g w -> p t (g w)"),
                                             l2v[:, :, :, 1:2].rearrange("p t g w -> p t (g w)"), op=OP.max))
    c.W("vector", "v", v)
    v = c.cop("v", lambda e: e.tensor_tensor(gL1, gL1, gL2, op=OP.max))
    c.W("vector", "v", v)
    gs = FA[:, FA_SL + SL_GS:FA_SL + SL_GS + 128]
    v = c.cop("v", lambda e: e.tensor_tensor(gs, FA[:, FA_CAND:FA_CAND + 128],
                                             FA[:, FA_CAND + 128:FA_CAND + 256], op=OP.add))
    c.W("vector", "v", v)
    gmask = FA[:, FA_T1:FA_T1 + 128]
    v = c.cop("v", lambda e: e.memset(gmask, 0.0))
    gsv = gs.rearrange("p (t g) -> p t g", t=16)
    mx = FA[:, FA_SL + SL_R1:FA_SL + SL_R1 + 16]
    mn = FA[:, FA_SL + SL_R1 + 16:FA_SL + SL_R1 + 32]
    eqg = FA[:, FA_EQ:FA_EQ + 128]
    candg = FA[:, FA_EQ + 128:FA_EQ + 256]
    selg = FA[:, FA_EQ + 256:FA_EQ + 384]
    for r in range(4):
        c.W("vector", "v")
        v = c.cop("v", lambda e: e.tensor_reduce(mx.rearrange("p (t o) -> p t o", t=16), gsv, axis=AX, op=OP.max))
        c.W("vector", "v", v)
        v = c.cop("v", lambda e: e.tensor_tensor(
            eqg.rearrange("p (t g) -> p t g", t=16), gsv,
            mx.rearrange("p (t o) -> p t o", t=16).broadcast_to([128, 16, 8]), op=OP.is_equal))
        c.W("vector", "v", v)
        v = c.cop("v", lambda e: e.tensor_tensor(candg, eqg, KP[:, KP_I8B:KP_I8B + 128], op=OP.mult))
        c.W("vector", "v", v)
        v = c.cop("v", lambda e: e.tensor_reduce(
            mn.rearrange("p (t o) -> p t o", t=16),
            candg.rearrange("p (t g) -> p t g", t=16), axis=AX, op=OP.min))
        c.W("vector", "v", v)
        v = c.cop("v", lambda e: e.tensor_tensor(
            selg.rearrange("p (t g) -> p t g", t=16),
            candg.rearrange("p (t g) -> p t g", t=16),
            mn.rearrange("p (t o) -> p t o", t=16).broadcast_to([128, 16, 8]), op=OP.is_equal))
        c.W("vector", "v", v)
        v = c.cop("v", lambda e: e.tensor_tensor(gmask, gmask, selg, op=OP.add))
        c.W("vector", "v", v)
        v = c.cop("v", lambda e: e.scalar_tensor_tensor(gs, selg, -BIGC, gs, OP.mult, OP.add))
    c.W("vector", "v")
    for j in range(8):
        v = c.cop("v", lambda e, j=j: e.tensor_copy(
            Mw.rearrange("p (t g e) -> p t g e", t=16, g=8)[:, :, :, j:j + 1],
            gmask.rearrange("p (t g) -> p t g", t=16).rearrange("p t (g o) -> p t g o", o=1)))
    c.W("vector", "v")
    v = c.cop("v", lambda e: e.tensor_scalar(eq, Mw, BIGC, -BIGC, OP.mult, OP.add))
    c.W("vector", "v", v)
    v = c.cop("v", lambda e: e.tensor_tensor(Mw, Mw, sfc, op=OP.mult))
    c.W("vector", "v", v)
    v = c.cop("v", lambda e: e.tensor_tensor(Mw, Mw, eq, op=OP.add))
    c.W("vector", "v", v)
    BBsel = tn["BB"]
    mx16 = FA[:, FA_SL + SL_R1:FA_SL + SL_R1 + 16]
    mn16 = FA[:, FA_SL + SL_R1 + 16:FA_SL + SL_R1 + 32]
    Mwv = Mw.rearrange("p (t e) -> p t e", t=16)
    c.W("vector", "d", dm_rowb)
    for r in range(TOPK):
        v = c.cop("v", lambda e: e.tensor_reduce(
            mx16.rearrange("p (t o) -> p t o", t=16), Mwv, axis=AX, op=OP.max))
        c.W("vector", "v", v)
        v = c.cop("v", lambda e: e.tensor_tensor(
            eq.rearrange("p (t e) -> p t e", t=16), Mwv,
            mx16.rearrange("p (t o) -> p t o", t=16).broadcast_to([128, 16, 64]), op=OP.is_equal))
        c.W("vector", "v", v)
        v = c.cop("v", lambda e: e.tensor_tensor(cand, eq, KP[:, KP_I64B:KP_I64B + 1024], op=OP.mult))
        c.W("vector", "v", v)
        v = c.cop("v", lambda e: e.tensor_reduce(
            mn16.rearrange("p (t o) -> p t o", t=16),
            cand.rearrange("p (t e) -> p t e", t=16), axis=AX, op=OP.min))
        c.W("vector", "v", v)
        eir = FA[:, FA_SL + SL_EI + r * 16:FA_SL + SL_EI + (r + 1) * 16]
        v = c.cop("v", lambda e, eir=eir: e.tensor_scalar(eir, mn16, BIGC, None, OP.add))
        selr = BBsel[:, BG_SEL + r * 1024:BG_SEL + (r + 1) * 1024]
        v = c.cop("v", lambda e, selr=selr: e.tensor_tensor(
            selr.rearrange("p (t e) -> p t e", t=16),
            cand.rearrange("p (t e) -> p t e", t=16),
            mn16.rearrange("p (t o) -> p t o", t=16).broadcast_to([128, 16, 64]), op=OP.is_equal))
        c.W("vector", "v", v)
        if r == 0:
            v = c.cop("v", lambda e, selr=selr: e.tensor_copy(selmask, selr))
        else:
            v = c.cop("v", lambda e, selr=selr: e.tensor_tensor(selmask, selmask, selr, op=OP.add))
        v = c.cop("v", lambda e, selr=selr: e.tensor_tensor(t1, selr, scoresT, op=OP.mult))
        c.W("vector", "v", v)
        wr = FA[:, FA_SL + SL_W + r * 16:FA_SL + SL_W + (r + 1) * 16]
        v = c.cop("v", lambda e, wr=wr: e.tensor_reduce(
            wr.rearrange("p (t o) -> p t o", t=16),
            t1.rearrange("p (t e) -> p t e", t=16), axis=AX, op=OP.add))
        c.W("vector", "v", v)
        v = c.cop("v", lambda e, selr=selr: e.tensor_tensor(
            t1, selr, FA[:, FA_ROWB:FA_ROWB + 1024], op=OP.mult))
        c.W("vector", "v", v)
        rbr = FA[:, FA_SL + SL_RB + r * 16:FA_SL + SL_RB + (r + 1) * 16]
        v = c.cop("v", lambda e, rbr=rbr: e.tensor_reduce(
            rbr.rearrange("p (t o) -> p t o", t=16),
            t1.rearrange("p (t e) -> p t e", t=16), axis=AX, op=OP.add))
        v = c.cop("v", lambda e, selr=selr: e.scalar_tensor_tensor(Mw, selr, -BIGC, Mw, OP.mult, OP.add))
        c.W("vector", "v", v)
    v = c.cop("v", lambda e: e.tensor_tensor(t1, selmask, scoresT, op=OP.mult))
    c.W("vector", "v", v)
    wsum = FA[:, FA_SL + SL_R2:FA_SL + SL_R2 + 16]
    v = c.cop("v", lambda e: e.tensor_reduce(
        wsum.rearrange("p (t o) -> p t o", t=16),
        t1.rearrange("p (t e) -> p t e", t=16), axis=AX, op=OP.add))
    c.W("vector", "v", v)
    v = c.cop("v", lambda e: e.tensor_scalar(wsum, wsum, 1e-20, None, OP.add))
    c.W("vector", "v", v)
    winv = FA[:, FA_SL + SL_R2 + 16:FA_SL + SL_R2 + 32]
    v = c.cop("v", lambda e: e.reciprocal(winv, wsum))
    c.W("vector", "v", v)
    v = c.cop("v", lambda e: e.tensor_scalar(winv, winv, 2.5, None, OP.mult))
    c.W("vector", "v", v)
    if "gate" in dbg:
        c.W("sync", "v")
        c.W("sync", "s")
        for i, tsr in enumerate([scoresT, sfc, selmask, Mw]):
            c.dma("d", lambda e, i=i, tsr=tsr: e.dma_start(
                out=tn["DBG"]["gate"][:, i * 1024:(i + 1) * 1024], in_=tsr))
        c.sync_all()

    c.W("tensor", "v")
    for tt in range(16):
        c.cop("t", lambda e, tt=tt: e.matmul(
            psD[0:1, 0:64], KP[:, KP_ONE:KP_ONE + 1],
            selmask[:, tt * 64:(tt + 1) * 64],
            start=(tt == 0), stop=(tt == 15)), inc=(tt == 15))
    c.W("vector", "t")
    v = c.cop("v", lambda e: e.tensor_copy(KP[0:1, KP_CNTR:KP_CNTR + 64], psD[0:1, 0:64]))
    c.W("sync", "v", v)
    dm = c.dma("d", lambda e: e.dma_start(out=tn["d_cnt_in"].ap(), in_=KP[0:1, KP_CNTR:KP_CNTR + 64]))
    c.W("gpsimd", "d", dm)
    cm = c.coll(lambda e: e.collective_compute(
        "AllGather", mybir.AluOpType.bypass, replica_groups=tn["allg"],
        ins=[tn["d_cnt_in"].ap()], outs=[tn["d_cnt_out"].ap()]))
    c.W("sync", "c", cm)
    dm = c.dma("d", lambda e: e.dma_start(out=FA[0:8, FA_CNT8:FA_CNT8 + 64], in_=tn["d_cnt_out"].ap()))
    c.W("tensor", "d", dm)
    tm = c.cop("t", lambda e: e.matmul(
        psD[0:1, 0:64], KP[0:8, KP_RANK:KP_RANK + 1], FA[0:8, FA_CNT8:FA_CNT8 + 64],
        start=True, stop=True))
    c.W("vector", "t", tm)
    v = c.cop("v", lambda e: e.tensor_copy(KP[0:1, KP_OFFS:KP_OFFS + 64], psD[0:1, 0:64]))
    c.W("tensor", "v", v)
    tm = c.cop("t", lambda e: e.matmul(
        psE[:, 0:64], KP[0:1, KP_ONE1:KP_ONE1 + 128], KP[0:1, KP_OFFS:KP_OFFS + 64],
        start=True, stop=True))
    c.W("vector", "t", tm)
    offsb = FA[:, FA_T1:FA_T1 + 64]
    vob = c.cop("v", lambda e: e.tensor_copy(offsb, psE[:, 0:64]))
    c.W("tensor", "v", vob)
    if "cnt" in dbg:
        c.W("sync", "d")
        c.dma("d", lambda e: e.dma_start(out=tn["DBG"]["cnt"], in_=tn["d_cnt_out"].ap()))
    for tt in range(16):
        c.W("tensor", "v")
        for tp in range(tt):
            c.cop("t", lambda e, tt=tt, tp=tp: e.matmul(
                psB[:, 0:64], KP[:, KP_ONEST:KP_ONEST + 128],
                selmask[:, tp * 64:(tp + 1) * 64],
                start=(tp == 0), stop=False), inc=False)
        c.cop("t", lambda e, tt=tt: e.matmul(
            psB[:, 0:64], KP[:, KP_UT:KP_UT + 128],
            selmask[:, tt * 64:(tt + 1) * 64],
            start=(tt == 0), stop=True), inc=True)
        tm = c.mark("t")
        c.W("vector", "t", tm)
        v = c.cop("v", lambda e, tt=tt: e.tensor_tensor(
            FA[:, FA_PRE + tt * 64:FA_PRE + (tt + 1) * 64], psB[:, 0:64], offsb, op=OP.add))
        c.W("tensor", "v", v)
    prefix = FA[:, FA_PRE:FA_PRE + 1024]
    c.W("vector", "v")
    for r in range(TOPK):
        selr = BBsel[:, BG_SEL + r * 1024:BG_SEL + (r + 1) * 1024]
        v = c.cop("v", lambda e, selr=selr: e.tensor_tensor(t1, selr, prefix, op=OP.mult))
        c.W("vector", "v", v)
        posr = FA[:, FA_SL + SL_POS + r * 16:FA_SL + SL_POS + (r + 1) * 16]
        v = c.cop("v", lambda e, posr=posr: e.tensor_reduce(
            posr.rearrange("p (t o) -> p t o", t=16),
            t1.rearrange("p (t e) -> p t e", t=16), axis=AX, op=OP.add))
        c.W("vector", "v", v)
        validr = FA[:, FA_SL + SL_R3:FA_SL + SL_R3 + 16]
        v = c.cop("v", lambda e, posr=posr, validr=validr: e.tensor_scalar(
            validr, posr, float(CAP), None, OP.is_lt))
        c.W("vector", "v", v)
        wf = KP[:, KP_WF + r * 16:KP_WF + (r + 1) * 16]
        wr = FA[:, FA_SL + SL_W + r * 16:FA_SL + SL_W + (r + 1) * 16]
        v = c.cop("v", lambda e, wf=wf, wr=wr: e.tensor_tensor(wf, wr, winv, op=OP.mult))
        c.W("vector", "v", v)
        v = c.cop("v", lambda e, wf=wf, validr=validr: e.tensor_tensor(wf, wf, validr, op=OP.mult))
        c.W("vector", "v", v)
        eir = FA[:, FA_SL + SL_EI + r * 16:FA_SL + SL_EI + (r + 1) * 16]
        idxf = FA[:, FA_EQ:FA_EQ + 16]
        v = c.cop("v", lambda e, eir=eir, posr=posr, idxf=idxf: e.scalar_tensor_tensor(
            idxf, eir, float(CAP), posr, OP.mult, OP.add))
        c.W("vector", "v", v)
        v = c.cop("v", lambda e, idxf=idxf, validr=validr: e.scalar_tensor_tensor(
            idxf, validr, -2.0e6, idxf, OP.mult, OP.add))
        c.W("vector", "v", v)
        v = c.cop("v", lambda e, idxf=idxf: e.tensor_scalar(idxf, idxf, 2.0e6, None, OP.add))
        c.W("vector", "v", v)
        v = c.cop("v", lambda e, idxf=idxf, r=r: e.tensor_copy(
            KPU[:, KPU_SCAT + r * 16:KPU_SCAT + (r + 1) * 16], idxf))
        c.W("vector", "v", v)
        v = c.cop("v", lambda e, posr=posr, idxf=idxf: e.tensor_scalar(
            idxf, posr, float(CAP - 1), None, OP.min))
        c.W("vector", "v", v)
        rbr = FA[:, FA_SL + SL_RB + r * 16:FA_SL + SL_RB + (r + 1) * 16]
        v = c.cop("v", lambda e, rbr=rbr, idxf=idxf: e.tensor_tensor(
            idxf, idxf, rbr, op=OP.add))
        c.W("vector", "v", v)
        v = c.cop("v", lambda e, idxf=idxf, r=r: e.tensor_copy(
            KPU[:, KPU_YIDX + r * 16:KPU_YIDX + (r + 1) * 16], idxf))
        c.W("vector", "v", v)
    if "slots" in dbg:
        c.W("sync", "v")
        c.dma("d", lambda e: e.dma_start(
            out=tn["DBG"]["slots"][:, 0:384], in_=FA[:, FA_SL:FA_SL + 384]))
        c.dma("d", lambda e: e.dma_start(
            out=tn["DBG"]["slots"][:, 384:512], in_=KP[:, KP_WF:KP_WF + 128]))
        c.sync_all()

    v = c.cop("v", lambda e: e.memset(FA[:, FA_EQ:FA_EQ + 512], 0.0))
    c.W("sync", "v", v)
    dm = c.dma("d", lambda e: e.dma_start(
        out=tn["d_map_in"].ap().rearrange("(p a) o -> p (a o)", p=128),
        in_=FA[:, FA_EQ:FA_EQ + 512]))
    c.W("gpsimd", "d", dm)
    c.W("gpsimd", "v")
    breg = c.eng.to_reg(NEXP * CAP - 1) if c.eng_name == "gpsimd" else None
    for r in range(TOPK):
        for tt in range(16):
            c.dma("g", lambda e, r=r, tt=tt: e.indirect_dma_start(
                out=tn["d_map_in"].ap(),
                out_offset=IndirectOffsetOnAxis(
                    ap=KPU[:, KPU_SCAT + r * 16 + tt:KPU_SCAT + r * 16 + tt + 1], axis=0),
                in_=KP[:, KP_TOK + tt:KP_TOK + tt + 1], in_offset=None,
                bounds_check=breg, oob_is_err=False))
    gm = c.mark("g")
    c.W("gpsimd", "g", gm)
    cm = c.coll(lambda e: e.collective_compute(
        "AllReduce", mybir.AluOpType.add, replica_groups=tn["allg"],
        ins=[tn["d_map_in"].ap()], outs=[tn["d_map_out"].ap()]))
    if "map" in dbg:
        c.W("sync", "c", cm)
        c.dma("d", lambda e: e.dma_start(out=tn["DBG"]["map"], in_=tn["d_map_out"].ap()))
        c.sync_all()

    for tt in range(16):
        c.W("tensor", "g")
        for ec in range(ECH):
            tmm = c.cop("t", lambda e, tt=tt, ec=ec: e.transpose(
                psT[:, (ec % 4) * 256:(ec % 4) * 256 + 128],
                BA[:, ec * S + tt * 128: ec * S + (tt + 1) * 128],
                KPB[:, KPB_ID:KPB_ID + 128]))
            c.W("vector", "t", tmm)
            v = c.cop("v", lambda e, tt=tt, ec=ec: e.tensor_copy(
                BBsel[:, BG_TOK + ec * 128:BG_TOK + (ec + 1) * 128],
                psT[:, (ec % 4) * 256:(ec % 4) * 256 + 128]))
            c.W("tensor", "v", v)
        vm = c.mark("v")
        c.W("gpsimd", "v", vm)
        c.dma("g", lambda e, tt=tt: e.dma_start(
            out=tn["d_tok_in"].ap()[tt * 128:(tt + 1) * 128, :],
            in_=BBsel[:, BG_TOK:BG_TOK + 1024]))
    gm = c.mark("g")
    c.W("gpsimd", "g", gm)
    cm_tok = c.coll(lambda e: e.collective_compute(
        "AllGather", mybir.AluOpType.bypass, replica_groups=tn["allg"],
        ins=[tn["d_tok_in"].ap()], outs=[tn["d_tok_out"].ap()]))

    c.W("gpsimd", "c", cm_pair)
    for ec in range(ECH):
        c.dma("g", lambda e, ec=ec: e.indirect_dma_start(
            out=BA[:, ec * S + T: ec * S + 2 * T], out_offset=None,
            in_=tn["d_ag_out"].ap(),
            in_offset=IndirectOffsetOnAxis(ap=KPU[:, KPU_PG + ec:KPU_PG + ec + 1], axis=0)))
    if "xn1" in dbg:
        for ec in range(ECH):
            c.dma("g", lambda e, ec=ec: e.dma_start(
                out=tn["DBG"]["xn1"][ec * 128:(ec + 1) * 128, :], in_=BA[:, ec * S:(ec + 1) * S]))
        c.sync_all()

    xn_ch = [BA[:, ec * S:(ec + 1) * S] for ec in range(ECH)]
    tn["exp_base"] = B1_EXP
    src = I["Wk1"].rearrange("(a p) o -> p a o", p=128)
    c.dma("g", lambda e, src=src: e.dma_start(
        out=BB[:, B1_WKV:B1_WKV + 2048].rearrange("p (a o) -> p a o", a=8), in_=src))
    src = I["Wv1"].rearrange("(a p) o -> p a o", p=128)
    c.dma("g", lambda e, src=src: e.dma_start(
        out=BB[:, B1_WKV + 2048:B1_WKV + 4096].rearrange("p (a o) -> p a o", a=8), in_=src))
    gm = c.mark("g")
    c.W("tensor", "g", gm)
    c.W("tensor", "v")
    c.W("tensor", "s")
    evk = [0] * 16
    for ch in range(2):
        for kc in range(8):
            ii = ch * 8 + kc
            psk = psA[:, (ii % 2) * 512:(ii % 2 + 1) * 512]
            if ii >= 2:
                c.W("tensor", "v", evk[ii - 2])
            for ei in range(ECH):
                c.cop("t", lambda e, ch=ch, kc=kc, ei=ei, psk=psk: e.matmul(
                    psk,
                    BB[:, B1_WKV + ei * 256 + ch * 128:B1_WKV + ei * 256 + (ch + 1) * 128],
                    xn_ch[ei][:, kc * 512:(kc + 1) * 512],
                    start=(ei == 0), stop=(ei == ECH - 1)), inc=(ei == ECH - 1))
            tmk = c.mark("t")
            c.W("vector", "t", tmk)
            evk[ii] = c.cop("v", lambda e, ch=ch, kc=kc, psk=psk: e.tensor_copy(
                BB[:, B1_K + ch * 4096 + kc * 512:B1_K + ch * 4096 + (kc + 1) * 512], psk))
    c.W("tensor", "s")
    c.W("tensor", "v")
    c.W("gpsimd", "g")
    qm = c.cop("q", lambda e: e.memset(BB[:, B1_V:B1_V + 32 * 260], 1.0))
    c.W("vector", "q", qm)
    c.W("tensor", "q", qm)
    evv = [0] * NKT
    for kt in range(NKT):
        psk = psC[:, (kt % 2) * 512:(kt % 2) * 512 + 256]
        if kt >= 2:
            c.W("tensor", "v", evv[kt - 2])
        for ei in range(ECH):
            c.cop("t", lambda e, kt=kt, ei=ei, psk=psk: e.matmul(
                psk, xn_ch[ei][:, kt * 128:(kt + 1) * 128],
                BB[:, B1_WKV + 2048 + ei * 256:B1_WKV + 2048 + (ei + 1) * 256],
                start=(ei == 0), stop=(ei == ECH - 1)), inc=(ei == ECH - 1))
        tmk = c.mark("t")
        c.W("vector", "t", tmk)
        evv[kt] = c.cop("v", lambda e, kt=kt, psk=psk: e.tensor_copy(
            BB[:, B1_V + kt * 260:B1_V + kt * 260 + 260].rearrange(
                "p (g o) -> p g o", g=4)[:, :, 0:64],
            psk.rearrange("p (g o) -> p g o", g=4)))
    c.W("tensor", "v")
    for hp in range(8):
        gpair = hp // 4
        hi = hp % 4
        ha = gpair * 8 + hi
        hb = gpair * 8 + 4 + hi
        ga, gb = gpair * 2, gpair * 2 + 1
        for sub, h in ((0, ha), (1, hb)):
            wq = BB[:, B1_WQ + sub * 1024:B1_WQ + sub * 1024 + 1024]
            c.W("gpsimd", "t")
            src = I["Wq1d"].rearrange("(a p) o -> p a o", p=128)[:, :, h * 128:(h + 1) * 128]
            gm = c.dma("g", lambda e, src=src, wq=wq: e.dma_start(
                out=wq.rearrange("p (a o) -> p a o", a=8), in_=src))
            c.W("tensor", "g", gm)
            c.W("tensor", "v")
            c.W("tensor", "s")
            qbase = B1_Q if sub == 0 else B1_Q2
            evq = [0] * NQC
            for qcc in range(NQC):
                psk = psB[:, (qcc % 2) * 512:(qcc % 2 + 1) * 512]
                if qcc >= 2:
                    c.W("tensor", "v", evq[qcc - 2])
                for ei in range(ECH):
                    c.cop("t", lambda e, qcc=qcc, ei=ei, wq=wq, psk=psk: e.matmul(
                        psk, wq[:, ei * 128:(ei + 1) * 128],
                        xn_ch[ei][:, qcc * 512:(qcc + 1) * 512],
                        start=(ei == 0), stop=(ei == ECH - 1)), inc=(ei == ECH - 1))
                tmk = c.mark("t")
                c.W("vector", "t", tmk)
                evq[qcc] = c.cop("v", lambda e, qcc=qcc, psk=psk, qbase=qbase: e.tensor_copy(
                    BB[:, qbase + qcc * 512:(qbase + (qcc + 1) * 512)], psk))
            c.W("tensor", "v")

        for qcc in range(NQC):
            a_dst_a = tn["d_a"].ap()[ha * 64:(ha + 1) * 64, qcc * 512:(qcc + 1) * 512]
            a_dst_b = tn["d_a"].ap()[hb * 64:(hb + 1) * 64, qcc * 512:(qcc + 1) * 512]
            ph_attn_core_packed(c, tn, qcc, ga, gb, a_dst_a, a_dst_b)
    c.sync_all()
    if "a1" in dbg:
        c.dma("g", lambda e: e.dma_start(out=tn["DBG"]["a1"], in_=tn["d_a"].ap()))
        c.sync_all()

    src = I["Wp1"].rearrange("(a p) o -> p a o", p=128)
    c.dma("g", lambda e, src=src: e.dma_start(
        out=BB[:, BW_WP:BW_WP + 8192].rearrange("p (a o) -> p a o", a=8), in_=src))
    gm = c.mark("g")
    c.W("tensor", "g", gm)
    a_view = tn["d_a"].ap().rearrange("(a p) t -> p a t", p=128)
    xnf_view = tn["d_xnf"].ap().rearrange("(a p) t -> p a t", p=128)
    s_view = tn["d_s"].ap().rearrange("(a p) t -> p a t", p=128)
    for qc in range(NQC):
        c.W("sync", "t")
        dam = c.dma("d", lambda e, qc=qc: e.dma_start(
            out=BB[:, BW_A:BW_A + 4096].rearrange("p (a t) -> p a t", a=8),
            in_=a_view[:, :, qc * QCW:(qc + 1) * QCW]))
        dxm = c.dma("d", lambda e, qc=qc: e.dma_start(
            out=FA[:, FA_X:FA_X + 4096].rearrange("p (a t) -> p a t", a=8),
            in_=xnf_view[:, :, qc * QCW:(qc + 1) * QCW]))
        c.W("tensor", "d", dxm)
        c.W("tensor", "v")
        for oc in range(ECH):
            psX = psA if (oc % 2 == 0) else psB
            for ac in range(ECH):
                c.cop("t", lambda e, oc=oc, ac=ac, psX=psX: e.matmul(
                    psX[:, 0:512],
                    BB[:, BW_WP + ac * 1024 + oc * 128:BW_WP + ac * 1024 + (oc + 1) * 128],
                    BB[:, BW_A + ac * 512:BW_A + (ac + 1) * 512],
                    start=(ac == 0), stop=(ac == ECH - 1)), inc=(ac == ECH - 1))
            tm = c.mark("t")
            c.W("vector", "t", tm)
            c.W("vector", "d", dxm)
            v = c.cop("v", lambda e, oc=oc, psX=psX: e.tensor_tensor(
                FA[:, FA_SS + oc * 512:FA_SS + (oc + 1) * 512],
                FA[:, FA_X + oc * 512:FA_X + (oc + 1) * 512],
                psX[:, 0:512], op=OP.add))
            c.W("tensor", "v", v)
        vm = c.mark("v")
        c.W("sync", "v", vm)
        c.dma("d", lambda e, qc=qc: e.dma_start(
            out=s_view[:, :, qc * QCW:(qc + 1) * QCW],
            in_=FA[:, FA_SS:FA_SS + 4096].rearrange("p (a t) -> p a t", a=8)))
        c.W("sync", "d")
    c.sync_all()
    if "s1" in dbg:
        c.dma("d", lambda e: e.dma_start(out=tn["DBG"]["s1"], in_=tn["d_s"].ap()))
        c.sync_all()

    c.W("gpsimd", "c")
    gm = c.dma("g", lambda e: e.indirect_dma_start(
        out=KP[0:64, KP_MAPF:KP_MAPF + 128], out_offset=None,
        in_=tn["d_map_out"].ap().rearrange("(b w) o -> b (w o)", w=128),
        in_offset=IndirectOffsetOnAxis(ap=KPU[0:64, KPU_MAPIO:KPU_MAPIO + 1], axis=0)))
    c.W("tensor", "g", gm)
    tm = c.cop("t", lambda e: e.transpose(
        psC[:, 0:64], KP[0:64, KP_MAPF:KP_MAPF + 128], KP[0:64, KP_IDF:KP_IDF + 64]))
    c.W("vector", "t", tm)
    v = c.cop("v", lambda e: e.tensor_copy(KPU[:, KPU_MAPU:KPU_MAPU + 64], psC[:, 0:64]))
    c.W("gpsimd", "v", v)

    for j in range(NEXP_MY):
        c.W("gpsimd", "t")
        c.W("gpsimd", "s")
        src = I["wg_my"][j * 8:(j + 1) * 8, :].rearrange("a (p o) -> p a o", p=128)
        c.dma("g", lambda e, src=src: e.dma_start(
            out=BB[:, BM_WG:BM_WG + 4096].rearrange("p (a o) -> p a o", a=8), in_=src))
        src = I["wu_my"][j * 8:(j + 1) * 8, :].rearrange("a (p o) -> p a o", p=128)
        c.dma("g", lambda e, src=src: e.dma_start(
            out=BB[:, BM_WU:BM_WU + 4096].rearrange("p (a o) -> p a o", a=8), in_=src))
        src = I["wd_my"][j * 4:(j + 1) * 4, :].rearrange("a (p o) -> p a o", p=128)
        c.dma("g", lambda e, src=src: e.dma_start(
            out=BB[:, BM_WD:BM_WD + 4096].rearrange("p (a o) -> p a o", a=4), in_=src))
        gpm = [0] * 8
        tpm = [0] * 8

        def emit_gather_pb(pb, j=j):
            gslot = BB[:, BM_GT + (pb % 2) * 1024:BM_GT + (pb % 2 + 1) * 1024]
            gq = "g" if pb % 2 == 0 else "g2"
            if pb >= 2:
                c.W("gpsimd", "t", tpm[pb - 2])
            c.dma(gq, lambda e, j=j, pb=pb, gslot=gslot: e.indirect_dma_start(
                out=gslot, out_offset=None,
                in_=tn["d_tok_out"].ap(),
                in_offset=IndirectOffsetOnAxis(
                    ap=KPU[:, KPU_MAPU + j * 8 + pb:KPU_MAPU + j * 8 + pb + 1], axis=0)))
            gpm[pb] = c.mark(gq)

        emit_gather_pb(0)
        for pb in range(8):
            gslot = BB[:, BM_GT + (pb % 2) * 1024:BM_GT + (pb % 2 + 1) * 1024]
            if pb + 1 < 8:
                emit_gather_pb(pb + 1)
            c.W("tensor", "g" if pb % 2 == 0 else "g2", gpm[pb])
            for ec in range(ECH):
                tmm = c.cop("t", lambda e, ec=ec, gslot=gslot: e.transpose(
                    psT[:, (ec % 4) * 256:(ec % 4) * 256 + 128],
                    gslot[:, ec * 128:(ec + 1) * 128], KPB[:, KPB_ID:KPB_ID + 128]))
                c.W("vector", "t", tmm)
                v = c.cop("v", lambda e, ec=ec, pb=pb: e.tensor_copy(
                    BB[:, BM_BUFT + ec * 1024 + pb * 128:BM_BUFT + ec * 1024 + (pb + 1) * 128],
                    psT[:, (ec % 4) * 256:(ec % 4) * 256 + 128]))
                c.W("tensor", "v", v)
            tpm[pb] = c.mark("t")
        gmw = c.mark("g")
        c.W("tensor", "g", gmw)
        c.W("tensor", "v")
        gum = [0] * 8
        silm = [0] * 8
        mvm = [0] * 8

        def emit_gu(rt):
            pa = psA[:, (rt % 2) * 512:(rt % 2 + 1) * 512]
            pb = psB[:, (rt % 2) * 512:(rt % 2 + 1) * 512]
            if rt >= 2:
                c.W("tensor", "s", silm[rt - 2])
                c.W("tensor", "v", mvm[rt - 2])
            for hc in range(ECH):
                c.cop("t", lambda e, rt=rt, hc=hc, pa=pa: e.matmul(
                    pa,
                    BB[:, BM_BUFT + hc * 1024 + rt * 128:BM_BUFT + hc * 1024 + (rt + 1) * 128],
                    BB[:, BM_WG + hc * 512:BM_WG + (hc + 1) * 512],
                    start=(hc == 0), stop=(hc == ECH - 1)), inc=(hc == ECH - 1))
            for hc in range(ECH):
                c.cop("t", lambda e, rt=rt, hc=hc, pb=pb: e.matmul(
                    pb,
                    BB[:, BM_BUFT + hc * 1024 + rt * 128:BM_BUFT + hc * 1024 + (rt + 1) * 128],
                    BB[:, BM_WU + hc * 512:BM_WU + (hc + 1) * 512],
                    start=(hc == 0), stop=(hc == ECH - 1)), inc=(hc == ECH - 1))
            gum[rt] = c.mark("t")

        emit_gu(0)
        for rt in range(8):
            if rt + 1 < 8:
                emit_gu(rt + 1)
            pa = psA[:, (rt % 2) * 512:(rt % 2 + 1) * 512]
            pb = psB[:, (rt % 2) * 512:(rt % 2 + 1) * 512]
            c.W("scalar", "t", gum[rt])
            c.W("scalar", "v", mvm[rt - 2] if rt >= 2 else 0)
            silm[rt] = c.cop("s", lambda e, pa=pa, rt=rt: e.activation(
                BB[:, BM_SIL + (rt % 2) * 512:BM_SIL + (rt % 2 + 1) * 512], pa, AF.Silu))
            c.W("vector", "s", silm[rt])
            mslot = BB[:, BM_M + (rt % 2) * 512:BM_M + (rt % 2 + 1) * 512]
            v = c.cop("v", lambda e, mslot=mslot, pb=pb, rt=rt: e.tensor_tensor(
                mslot, BB[:, BM_SIL + (rt % 2) * 512:BM_SIL + (rt % 2 + 1) * 512], pb, op=OP.mult))
            c.W("tensor", "v", v)
            for ic in range(4):
                tmm = c.cop("t", lambda e, ic=ic, mslot=mslot: e.transpose(
                    psT[:, (ic % 4) * 256:(ic % 4) * 256 + 128],
                    mslot[:, ic * 128:(ic + 1) * 128], KPB[:, KPB_ID:KPB_ID + 128]))
                c.W("vector", "t", tmm)
                v = c.cop("v", lambda e, ic=ic, rt=rt: e.tensor_copy(
                    BB[:, BM_MT + ic * 1024 + rt * 128:BM_MT + ic * 1024 + (rt + 1) * 128],
                    psT[:, (ic % 4) * 256:(ic % 4) * 256 + 128]))
                c.W("tensor", "v", v)
            mvm[rt] = c.mark("v")
            for half in range(2):
                psY = psD if half == 0 else psE
                for ic in range(4):
                    c.cop("t", lambda e, ic=ic, rt=rt, half=half, psY=psY: e.matmul(
                        psY[:, 0:512],
                        BB[:, BM_MT + ic * 1024 + rt * 128:BM_MT + ic * 1024 + (rt + 1) * 128],
                        BB[:, BM_WD + ic * 1024 + half * 512:BM_WD + ic * 1024 + (half + 1) * 512],
                        start=(ic == 0), stop=(ic == 3)), inc=(ic == 3))
            tm = c.mark("t")
            c.W("vector", "t", tm)
            c.W("vector", "d")
            v = c.cop("v", lambda e: e.tensor_copy(BB[:, BM_Y:BM_Y + 512], psD[:, 0:512]))
            v = c.cop("v", lambda e: e.tensor_copy(BB[:, BM_Y + 512:BM_Y + 1024], psE[:, 0:512]))
            c.W("sync", "v", v)
            c.W("tensor", "v", v)
            dm = c.dma("d", lambda e, j=j, rt=rt: e.dma_start(
                out=tn["d_y_in"].ap()[j * CAP + rt * 128:j * CAP + (rt + 1) * 128, :],
                in_=BB[:, BM_Y:BM_Y + 1024]))
        if j == 3:
            c.W("gpsimd", "d")
            c.coll(lambda e: e.collective_compute(
                "AllGather", mybir.AluOpType.bypass, replica_groups=tn["allg"],
                ins=[tn["d_y_in"].ap()[0:4 * CAP, :]],
                outs=[tn["d_y_out"].ap()[0:NCORE * 4 * CAP, :]]))
    c.W("gpsimd", "d")
    cm = c.coll(lambda e: e.collective_compute(
        "AllGather", mybir.AluOpType.bypass, replica_groups=tn["allg"],
        ins=[tn["d_y_in"].ap()[4 * CAP:8 * CAP, :]],
        outs=[tn["d_y_out"].ap()[NCORE * 4 * CAP:NCORE * 8 * CAP, :]]))
    c.sync_all()
    if "y" in dbg:
        c.dma("g", lambda e: e.dma_start(out=tn["DBG"]["y"], in_=tn["d_y_out"].ap()))
        c.sync_all()

    x1v = tn["d_x1"].ap()
    sv_ = tn["d_s"].ap()
    gmarks = [0] * 19
    d2marks = [0] * 19
    vmarks = [0] * 19

    def emit_gather(tt):
        if tt >= 4:
            c.W("gpsimd", "v", vmarks[tt - 4])
        base = BY_Y + (tt % 4) * 8192
        for r in range(TOPK):
            q = "g" if r % 2 == 0 else "g2"
            c.dma(q, lambda e, r=r, tt=tt, base=base: e.indirect_dma_start(
                out=BB[:, base + r * 1024:base + (r + 1) * 1024], out_offset=None,
                in_=tn["d_y_out"].ap(),
                in_offset=IndirectOffsetOnAxis(
                    ap=KPU[:, KPU_YIDX + r * 16 + tt:KPU_YIDX + r * 16 + tt + 1], axis=0)))
        gmarks[tt] = c.mark("g")
        d2marks[tt] = c.mark("g2")

    emit_gather(0)
    emit_gather(1)
    emit_gather(2)
    for tt in range(16):
        if tt + 3 < 16:
            emit_gather(tt + 3)
        c.W("vector", "g", gmarks[tt])
        c.W("vector", "g2", d2marks[tt])
        for r in range(TOPK):
            wf = KP[:, KP_WF + r * 16 + tt:KP_WF + r * 16 + tt + 1]
            yr = BB[:, BY_Y + (tt % 4) * 8192 + r * 1024:BY_Y + (tt % 4) * 8192 + (r + 1) * 1024]
            if r == 0:
                v = c.cop("v", lambda e, yr=yr, wf=wf: e.tensor_scalar(
                    FA[:, FA_ACC:FA_ACC + 1024], yr, wf, None, OP.mult))
            else:
                v = c.cop("v", lambda e, yr=yr, wf=wf: e.scalar_tensor_tensor(
                    FA[:, FA_ACC:FA_ACC + 1024], yr, wf, FA[:, FA_ACC:FA_ACC + 1024],
                    OP.mult, OP.add))
            c.W("vector", "v", v)
        vmarks[tt] = c.mark("v")
        c.W("scalar", "v")
        sm = c.cop("s", lambda e: e.activation(
            FA[:, FA_GT:FA_GT + 1024], FA[:, FA_ACC:FA_ACC + 1024], AF.Silu))
        dxm = c.dma("d", lambda e, tt=tt: e.dma_start(
            out=FA[:, FA_XC:FA_XC + 1024].rearrange("p (a t) -> p a t", a=8),
            in_=x1v.rearrange("(a p) t -> p a t", p=128)[:, :, tt * 128:(tt + 1) * 128]))
        dsm = c.dma("d", lambda e, tt=tt: e.dma_start(
            out=FA[:, FA_SC2:FA_SC2 + 1024].rearrange("p (a t) -> p a t", a=8),
            in_=sv_.rearrange("(a p) t -> p a t", p=128)[:, :, tt * 128:(tt + 1) * 128]))
        c.W("tensor", "s", sm)
        c.W("tensor", "v")
        for ec in range(ECH):
            tmm = c.cop("t", lambda e, ec=ec: e.transpose(
                psC[:, 0:128] if ec % 2 == 0 else psD[:, 0:128],
                FA[:, FA_GT + ec * 128:FA_GT + (ec + 1) * 128], KP[:, KP_IDF:KP_IDF + 128]))
            c.W("vector", "t", tmm)
            c.W("vector", "d", dsm)
            psg = psC if ec % 2 == 0 else psD
            v = c.cop("v", lambda e, ec=ec, psg=psg: e.tensor_tensor(
                FA[:, FA_XO + ec * 128:FA_XO + (ec + 1) * 128],
                FA[:, FA_SC2 + ec * 128:FA_SC2 + (ec + 1) * 128],
                psg[:, 0:128], op=OP.mult))
            c.W("vector", "v", v)
            v = c.cop("v", lambda e, ec=ec: e.tensor_tensor(
                FA[:, FA_XO + ec * 128:FA_XO + (ec + 1) * 128],
                FA[:, FA_XO + ec * 128:FA_XO + (ec + 1) * 128],
                FA[:, FA_XC + ec * 128:FA_XC + (ec + 1) * 128], op=OP.add))
            c.W("tensor", "v", v)
        vm = c.mark("v")
        c.W("sync", "v", vm)
        c.dma("d", lambda e, tt=tt: e.dma_start(
            out=tn["out_T"].rearrange("(a p) t -> p a t", p=128)[:, :, tt * 128:(tt + 1) * 128],
            in_=FA[:, FA_XO:FA_XO + 1024].rearrange("p (a t) -> p a t", a=8)))
        c.W("sync", "d")
    c.sync_all()


def emit_program(c, tn, dbg):
    I = tn["I"]; KP = tn["KP"]; KPB = tn["KPB"]; KPU = tn["KPU"]; FA = tn["FA"]
    small = [("gammaT", KP_GAM, 8), ("betaT", KP_BET, 8),
             ("b1T", KP_B1, 32), ("b2T", KP_B2, 8),
             ("iota64B", KP_I64B, 1024), ("i8B", KP_I8B, 128),
             ("bias_bc", KP_BIAS, 1024),
             ("onesT_f", KP_ONEST, 128), ("UT_f", KP_UT, 128),
             ("ident_f", KP_IDF, 128), ("ones_f", KP_ONE, 1),
             ("tokio", KP_TOK, 16), ("sgn_c", KP_SGN, 1),
             ("gate_wT_r", KP_GWT, 512)]
    for nm, off, w in small:
        c.dma("d", lambda e, nm=nm, off=off, w=w: e.dma_start(
            out=KP[:, off:off + w], in_=I[nm]))
    c.dma("d", lambda e: e.dma_start(out=KP[0:1, KP_ONE1:KP_ONE1 + 128], in_=I["ones1_f"]))
    c.dma("d", lambda e: e.dma_start(out=KP[32:33, KP_ONE1:KP_ONE1 + 128], in_=I["ones1_f"]))
    c.dma("d", lambda e: e.dma_start(out=KP[0:8, KP_RANK:KP_RANK + 1], in_=I["rankm"]))
    c.dma("d", lambda e: e.dma_start(out=KPB[:, KPB_SIGNS:KPB_SIGNS + 2048], in_=I["signs_b"]))
    c.dma("d", lambda e: e.dma_start(out=KPB[:, KPB_ID:KPB_ID + 128], in_=I["ident_b"]))
    c.dma("d", lambda e: e.dma_start(out=KPU[:, KPU_PG:KPU_PG + 8], in_=I["pg_idx"]))
    c.dma("d", lambda e: e.dma_start(out=KPU[0:64, KPU_MAPIO:KPU_MAPIO + 1], in_=I["mapio"]))
    c.dma("d", lambda e: e.dma_start(out=FA[:, FA_IQ:FA_IQ + T], in_=I["iq_c"]))
    c.dma("d", lambda e: e.dma_start(out=FA[:, FA_IK2:FA_IK2 + S], in_=I["ik2_c"]))
    c.cop("q", lambda e: e.memset(KP[:, KP_EPS:KP_EPS + 1], EPS))
    c.sync_all()
    if not globals().get("_SKIP_L0", False):
        ph_l0(c, tn, dbg)
    if not globals().get("_SKIP_L1", False):
        ph_l1(c, tn, dbg)
    c.sync_all()


def build_nc(dbg=()):
    nc = bass.Bass("TRN2", target_bir_lowering=False)

    def din(name, shape, dtype=F32):
        return nc.dram_tensor(name, list(shape), dtype, kind="ExternalInput").ap()

    I = {}
    I["x_T"] = din("x_T", (E, T))
    for nm, shp in [("WqTd", (E, H * 128)), ("WkTd", (E, H * 128)), ("WvT0", (E, E)),
                    ("U1", (E, 256)), ("V1", (256, 4096)), ("U2", (4096, 256)),
                    ("V2", (256, E)),
                    ("Wq1d", (E, 2048)), ("Wk1", (E, 256)), ("Wv1", (E, 256)),
                    ("Wp1", (E, E)),
                    ("wg_my", (NEXP_MY * ECH, 128 * 512)),
                    ("wu_my", (NEXP_MY * ECH, 128 * 512)),
                    ("wd_my", (NEXP_MY * 4, 128 * 1024))]:
        I[nm] = din(nm, shp)
    for nm, shp in [("gammaT", (128, 8)), ("betaT", (128, 8)),
                    ("b1T", (128, 32)), ("b2T", (128, 8)),
                    ("iq_c", (128, T)), ("ik2_c", (128, S)), ("sgn_c", (128, 1)),
                    ("ident_f", (128, 128)), ("ones_f", (128, 1)),
                    ("ones1_f", (1, 128)), ("onesT_f", (128, 128)), ("UT_f", (128, 128)),
                    ("iota64B", (128, 1024)), ("i8B", (128, 128)),
                    ("bias_bc", (128, 1024)), ("rowb_bc", (128, 1024)),
                    ("tokio", (128, 16)), ("rankm", (8, 1)),
                    ("gate_wT_r", (128, 512))]:
        I[nm] = din(nm, shp)
    I["signs_b"] = din("signs_b", (128, 2048), BF16)
    I["ident_b"] = din("ident_b", (128, 128), BF16)
    I["pg_idx"] = din("pg_idx", (128, 8), U32)
    I["mapio"] = din("mapio", (64, 1), U32)
    out_T = dout = nc.dram_tensor("out_T", [E, T], F32, kind="ExternalOutput").ap()

    pairs = [[0, 1], [2, 3], [4, 5], [6, 7]]
    allg = [list(range(8))]
    d_x1 = nc.dram_tensor("d_x1", [E, T], F32)
    d_s = nc.dram_tensor("d_s", [E, T], F32)
    d_xnf = nc.dram_tensor("d_xnf", [E, T], F32)
    d_a = nc.dram_tensor("d_a", [E, T], BF16)
    d_ag_in = nc.dram_tensor("d_ag_in", [1024, T], BF16)
    d_ag_out = nc.dram_tensor(
        "d_ag_out", [2048, T], BF16,
        addr_space=maybe_share_collective_output_space("AllGather", pairs))
    d_tok_in = nc.dram_tensor("d_tok_in", [T, E], BF16)
    d_tok_out = nc.dram_tensor(
        "d_tok_out", [NCORE * T, E], BF16,
        addr_space=maybe_share_collective_output_space("AllGather", allg))
    d_cnt_in = nc.dram_tensor("d_cnt_in", [1, 64], F32)
    d_cnt_out = nc.dram_tensor(
        "d_cnt_out", [8, 64], F32,
        addr_space=maybe_share_collective_output_space("AllGather", allg))
    d_map_in = nc.dram_tensor("d_map_in", [NEXP * CAP, 1], F32)
    d_map_out = nc.dram_tensor(
        "d_map_out", [NEXP * CAP, 1], F32,
        addr_space=maybe_share_collective_output_space("AllReduce", allg))
    d_y_in = nc.dram_tensor("d_y_in", [NEXP_MY * CAP, E], BF16)
    d_y_out = nc.dram_tensor(
        "d_y_out", [NEXP * CAP, E], BF16,
        addr_space=maybe_share_collective_output_space("AllGather", allg))

    DBG_OUTS = {}
    for nm in dbg:
        shp, _ = DBG_SPECS[nm]
        DBG_OUTS[nm] = nc.dram_tensor("dbg_" + nm, list(shp), F32, kind="ExternalOutput").ap()

    import contextlib
    with contextlib.ExitStack() as _st:
        def _al(cm):
            return _st.enter_context(cm)
        BA = _al(nc.sbuf_tensor("BA", [128, BA_W], BF16))
        BB = _al(nc.sbuf_tensor("BB", [128, BB_W], BF16))
        FA = _al(nc.sbuf_tensor("FA", [128, FA_W], F32))
        KP = _al(nc.sbuf_tensor("KP", [128, KP_W], F32))
        KPU = _al(nc.sbuf_tensor("KPU", [128, KPU_W], U32))
        KPB = _al(nc.sbuf_tensor("KPB", [128, KPB_W], BF16))
        psA = _al(nc.psum_tensor("psA", [128, 1024], F32))
        psB = _al(nc.psum_tensor("psB", [128, 1024], F32))
        psC = _al(nc.psum_tensor("psC", [128, 1024], F32))
        psD = _al(nc.psum_tensor("psD", [128, 512], F32))
        psE = _al(nc.psum_tensor("psE", [128, 512], F32))
        sem_b = _al(nc.semaphore("sem_b"))
        sem_d = _al(nc.semaphore("sem_d"))
        sem_g = _al(nc.semaphore("sem_g"))
        sem_d2 = _al(nc.semaphore("sem_d2"))
        sem_v = _al(nc.semaphore("sem_v"))
        sem_s = _al(nc.semaphore("sem_s"))
        sem_t = _al(nc.semaphore("sem_t"))
        sem_q = _al(nc.semaphore("sem_q"))
        sem_c = _al(nc.semaphore("sem_c"))
        sem_g2 = _al(nc.semaphore("sem_g2"))
        block = _al(nc.Block())
        sems = {"b": sem_b, "d": sem_d, "g": sem_g, "g2": sem_g2, "d2": sem_d2,
                "v": sem_v, "s": sem_s, "t": sem_t, "q": sem_q, "c": sem_c}
        tn = dict(I=I, out_T=out_T, d_x1=d_x1, d_s=d_s, d_xnf=d_xnf, d_a=d_a,
                  d_ag_in=d_ag_in, d_ag_out=d_ag_out, d_tok_in=d_tok_in,
                  d_tok_out=d_tok_out, d_cnt_in=d_cnt_in, d_cnt_out=d_cnt_out,
                  d_map_in=d_map_in, d_map_out=d_map_out, d_y_in=d_y_in,
                  d_y_out=d_y_out, BA=BA, BB=BB, FA=FA, KP=KP, KPU=KPU, KPB=KPB,
                  psA=psA, psB=psB, psC=psC, psD=psD, psE=psE,
                  pairs=pairs, allg=allg, DBG=DBG_OUTS)

        def emit_all(eng_name, eng):
            c = Ctx(nc, sems, eng_name, eng)
            emit_program(c, tn, dbg)

        @block.sync
        def _(sync):
            emit_all("sync", sync)

        @block.vector
        def _(vector):
            emit_all("vector", vector)

        @block.scalar
        def _(scalar):
            emit_all("scalar", scalar)

        @block.tensor
        def _(tensor):
            emit_all("tensor", tensor)

        @block.gpsimd
        def _(gpsimd):
            emit_all("gpsimd", gpsimd)

    return nc



_LAST_DBG = None


def _prep_inputs(inputs):
    x = np.ascontiguousarray(np.asarray(inputs["L0_attn_x"], np.float32))
    Wq = np.asarray(inputs["L0_attn_Wq"], np.float32)
    Wk = np.asarray(inputs["L0_attn_Wk"], np.float32)
    Wv = np.asarray(inputs["L0_attn_Wv"], np.float32)
    gam = np.asarray(inputs["L0_attn_ln_gamma"], np.float32)
    bet = np.asarray(inputs["L0_attn_ln_beta"], np.float32)
    U1 = np.asarray(inputs["L0_ffn_U1"], np.float32)
    V1 = np.asarray(inputs["L0_ffn_V1"], np.float32)
    b1 = np.asarray(inputs["L0_ffn_b1"], np.float32)
    U2 = np.asarray(inputs["L0_ffn_U2"], np.float32)
    V2 = np.asarray(inputs["L0_ffn_V2"], np.float32)
    b2 = np.asarray(inputs["L0_ffn_b2"], np.float32)
    Wq1 = np.asarray(inputs["L1_attn_Wq"], np.float32)
    Wk1 = np.asarray(inputs["L1_attn_Wk"], np.float32)
    Wv1 = np.asarray(inputs["L1_attn_Wv"], np.float32)
    Wp1 = np.asarray(inputs["L1_attn_Wpost"], np.float32)
    gw = np.asarray(inputs["L1_ffn_gate_w"], np.float32)
    gb = np.asarray(inputs["L1_ffn_gate_bias"], np.float32)
    wgt = np.asarray(inputs["L1_ffn_w_gate"], np.float32)
    wup = np.asarray(inputs["L1_ffn_w_up"], np.float32)
    wdn = np.asarray(inputs["L1_ffn_w_down"], np.float32)

    WqT = Wq.T.copy()
    WkT = Wk.T.copy()
    WqTd = np.concatenate([np.concatenate([WqT[:, h * 64:(h + 1) * 64]] * 2, 1) for h in range(H)], 1)
    WkTd = np.concatenate([np.concatenate([WkT[:, h * 64:(h + 1) * 64]] * 2, 1) for h in range(H)], 1)

    shared = {
        "WqTd": np.ascontiguousarray(WqTd), "WkTd": np.ascontiguousarray(WkTd),
        "WvT0": np.ascontiguousarray(Wv.T),
        "U1": U1, "V1": V1, "U2": U2, "V2": V2,
        "Wq1d": np.ascontiguousarray(np.concatenate(
            [np.concatenate([Wq1[:, h * 64:(h + 1) * 64]] * 2, 1) for h in range(H)], 1)),
        "Wk1": Wk1, "Wv1": Wv1, "Wp1": Wp1,
        "gammaT": np.ascontiguousarray(gam.reshape(8, 128).T),
        "betaT": np.ascontiguousarray(bet.reshape(8, 128).T),
        "b1T": np.ascontiguousarray(b1.reshape(32, 128).T),
        "b2T": np.ascontiguousarray(b2.reshape(8, 128).T),
        "ident_f": np.eye(128, dtype=np.float32),
        "ident_b": np.eye(128, dtype=np.float32).astype(BF),
        "ones_f": np.ones((128, 1), np.float32),
        "ones1_f": np.ones((1, 128), np.float32),
        "onesT_f": np.ones((128, 128), np.float32),
        "UT_f": np.triu(np.ones((128, 128), np.float32), 1),
        "gate_wT_r": np.ascontiguousarray(gw.T.reshape(8, 128, 64).transpose(1, 0, 2).reshape(128, 512)),
        "bias_bc": np.ascontiguousarray(np.broadcast_to(np.tile(gb, 16)[None, :], (128, 1024))),
    }
    i64 = np.tile(np.arange(64, dtype=np.float32) - BIGC, 16)
    shared["iota64B"] = np.ascontiguousarray(np.broadcast_to(i64[None, :], (128, 1024)))
    ee = np.arange(64)
    rowb = (np.where(ee % 8 >= 4, 32768.0, 0.0) + (ee // 8) * 4096.0
            + ((ee % 8) % 4) * 1024.0).astype(np.float32)
    shared["rowb_bc"] = np.ascontiguousarray(
        np.broadcast_to(np.tile(rowb, 16)[None, :], (128, 1024)))
    i8 = np.tile(np.arange(8, dtype=np.float32) - BIGC, 16)
    shared["i8B"] = np.ascontiguousarray(np.broadcast_to(i8[None, :], (128, 128)))
    signs = np.zeros((128, 2048), np.float32)
    for sub in range(2):
        for i in range(2):
            off = (sub * 2 + i) * 128
            qq = np.arange(512, dtype=np.float32)[None, :]
            pp = np.arange(128, dtype=np.float32)[:, None]
            signs[:, sub * 1024 + i * 512: sub * 1024 + (i + 1) * 512] = np.sign(qq - off - pp)
    shared["signs_b"] = signs.astype(BF)

    in_maps = []
    for cc in range(NCORE):
        b = cc // 2
        r0 = (cc % 2) * T
        sg = 1.0 if (cc % 2) == 1 else -1.0
        m = dict(shared)
        m["x_T"] = np.ascontiguousarray(x[b, r0:r0 + T, :].T)
        m["iq_c"] = np.ascontiguousarray(np.broadcast_to(
            ((np.arange(T, dtype=np.float64) + r0) * CL0).astype(np.float32)[None, :], (128, T)))
        kown = (np.arange(T, dtype=np.float64) + r0) * CL0
        kpart = (np.arange(T, dtype=np.float64) + (T - r0)) * CL0 * sg
        m["ik2_c"] = np.ascontiguousarray(np.broadcast_to(
            np.concatenate([kown, kpart]).astype(np.float32)[None, :], (128, S)))
        m["sgn_c"] = np.full((128, 1), sg, np.float32)
        m["tokio"] = np.ascontiguousarray(
            (cc * T + np.arange(16, dtype=np.float32)[None, :] * 128
             + np.arange(128, dtype=np.float32)[:, None]))
        m["rankm"] = (np.arange(8) < cc).astype(np.float32).reshape(8, 1)
        m["pg_idx"] = np.ascontiguousarray(
            ((1 - (cc % 2)) * 1024 + np.arange(8, dtype=np.uint32)[None, :] * 128
             + np.arange(128, dtype=np.uint32)[:, None]).astype(np.uint32))
        m["mapio"] = (cc * 64 + np.arange(64, dtype=np.uint32)).reshape(64, 1)
        eb = cc * NEXP_MY
        m["wg_my"] = np.ascontiguousarray(
            wgt[eb:eb + 8].reshape(8, 8, 128, 512).transpose(0, 1, 2, 3).reshape(64, 128 * 512))
        m["wu_my"] = np.ascontiguousarray(
            wup[eb:eb + 8].reshape(8, 8, 128, 512).reshape(64, 128 * 512))
        m["wd_my"] = np.ascontiguousarray(
            wdn[eb:eb + 8].reshape(8, 4, 128, 1024).reshape(32, 128 * 1024))
        in_maps.append(m)
    return in_maps


def kernel(**inputs):
    global _LAST_DBG
    dbg = tuple(globals().get("_BUILD_DBG", ()))
    in_maps = _prep_inputs(inputs)
    nc = build_nc(dbg)
    res = run_bass_kernel_spmd(nc, in_maps, core_ids=list(range(NCORE)))
    out = np.empty((4, S, E), np.float32)
    for cc in range(NCORE):
        b = cc // 2
        r0 = (cc % 2) * T
        out[b, r0:r0 + T, :] = res.results[cc]["out_T"].T
    if dbg:
        _LAST_DBG = [{k: res.results[cc].get("dbg_" + k) for k in dbg} for cc in range(NCORE)]
    return out
